# version 18
# baseline (speedup 1.0000x reference)
import numpy as np
import ml_dtypes
import concourse.bass as bass
import concourse.mybir as mybir
from concourse.bass_utils import run_bass_kernel_spmd

F32 = mybir.dt.float32
BF16 = mybir.dt.bfloat16
I32 = mybir.dt.int32
OP = mybir.AluOpType
ACT = mybir.ActivationFunctionType
AX = mybir.AxisListType.X
BF = ml_dtypes.bfloat16

E, S, NB = 1024, 4096, 32
H, KVH, D, QL, KVL = 16, 4, 64, 64, 32
NEXP, DEXP = 6, 512
EPS = 1e-5
GRP = 256
NGRP = S // GRP
NQG = 8
OWN = 2048
SCALE = 0.125
QTEMP_INV = 1.0 / 0.3
STEP_LIMIT = [10**9]
SCHED_DUMP = []
PRED = [0.0]


def own_blocks(half):
    keep = (0, 3) if half == 0 else (1, 2)
    return [g for g in range(NB) if g % 4 in keep]



def _swap64(Wcols):
    r = Wcols.reshape(Wcols.shape[0], -1, 32, 2)
    return np.ascontiguousarray(r[..., ::-1].reshape(Wcols.shape))


def _rope_rows():
    inv = 1.0 / (10000.0 ** (np.arange(0, D, 2, dtype=np.float64) / D))
    ang = np.outer(inv, np.arange(S, dtype=np.float64))
    cos = np.repeat(np.cos(ang), 2, axis=0)
    sin = np.repeat(np.sin(ang), 2, axis=0).copy()
    sin[0::2] *= -1.0
    C = np.concatenate([cos, cos], 0).astype(np.float32)
    Sg = np.concatenate([sin, sin], 0).astype(np.float32)
    return C, Sg


def host_prepare(inputs):
    x = np.asarray(inputs["L0_attn_x"], dtype=np.float32)
    Wqa = np.asarray(inputs["L0_attn_Wqa"], np.float32)
    Wqb = np.asarray(inputs["L0_attn_Wqb"], np.float32)
    Wkva = np.asarray(inputs["L0_attn_Wkva"], np.float32)
    Wkvb = np.asarray(inputs["L0_attn_Wkvb"], np.float32)
    Wout = np.asarray(inputs["L0_attn_Wout"], np.float32)
    bout = np.asarray(inputs["L0_attn_bout"], np.float32)
    projw = np.asarray(inputs["L0_ffn_proj_w"], np.float32)
    anch = np.asarray(inputs["L0_ffn_anchors"], np.float32)
    temp = np.asarray(inputs["L0_ffn_routing_temp"], np.float32).reshape(1, 1)
    w1 = np.asarray(inputs["L0_ffn_w1"], np.float32)
    w2 = np.asarray(inputs["L0_ffn_w2"], np.float32)

    sh = {}
    sh["w1r"] = np.ascontiguousarray(w1.reshape(NEXP, 8, 128, DEXP))
    sh["w2r"] = np.ascontiguousarray(w2.reshape(NEXP, 4, 128, E))
    sh["wqa"] = np.ascontiguousarray(Wqa.reshape(8, 128, QL))
    WqbP = np.empty_like(Wqb)
    for c in range(8):
        WqbP[:, c*128:c*128+64] = Wqb[:, c*64:(c+1)*64]
        WqbP[:, c*128+64:(c+1)*128] = Wqb[:, (c+8)*64:(c+9)*64]
    sh["wqb"] = np.ascontiguousarray(np.concatenate([WqbP, _swap64(WqbP)], 1))
    sh["wkva"] = np.ascontiguousarray(Wkva.reshape(8, 128, KVL))
    kvr = Wkvb.reshape(KVL, KVH, D, 2)
    WK = np.ascontiguousarray(kvr[..., 0].reshape(KVL, KVH * D))
    WV = np.ascontiguousarray(kvr[..., 1].reshape(KVL, KVH * D))
    WKs = _swap64(WK)
    sh["wkvbk"] = np.ascontiguousarray(np.concatenate(
        [WK[:, 0:64], WK[:, 128:192],
         WKs[:, 0:64], WKs[:, 128:192],
         WK[:, 64:128], WK[:, 192:256],
         WKs[:, 64:128], WKs[:, 192:256]], 1))
    sh["wkvbv"] = WV
    wop = np.empty_like(Wout)
    for j, oc in enumerate([0, 4, 1, 5, 2, 6, 3, 7]):
        wop[j*128:(j+1)*128] = Wout[oc*128:(oc+1)*128]
    sh["wout"] = np.ascontiguousarray(wop.reshape(8, 128, E))
    sh["bout"] = np.ascontiguousarray(bout.reshape(8, 128).T)
    sh["projw"] = np.ascontiguousarray(
        projw.T.reshape(8, 128, NEXP).transpose(1, 0, 2).reshape(128, 8 * NEXP))
    sh["anchT"] = np.ascontiguousarray(anch.T)
    sh["rtemp"] = temp
    sh["ident"] = np.eye(128, dtype=np.float32)

    C, Sg = _rope_rows()
    csk = np.zeros((128, 2 * S), dtype=np.float32)
    for g in range(NGRP):
        csk[:, g*512:g*512+256] = C[:, g*256:(g+1)*256]
        csk[:, g*512+256:(g+1)*512] = Sg[:, g*256:(g+1)*256]
    sh["csk"] = csk.astype(BF)

    in_maps, metas = [], []
    for core in range(8):
        b, half = core // 2, core % 2
        own = own_blocks(half)
        m = dict(sh)
        m["x"] = np.ascontiguousarray(x[b].T.reshape(8, 128, S))
        m["coreinfof"] = np.array([[half]], dtype=np.float32)
        csq = np.zeros((128, 2 * OWN), dtype=np.float32)
        for i in range(NQG):
            cols = list(range(own[2*i]*128, own[2*i]*128+128)) + \
                   list(range(own[2*i+1]*128, own[2*i+1]*128+128))
            csq[:, i*512:i*512+256] = C[:, cols]
            csq[:, i*512+256:(i+1)*512] = Sg[:, cols]
        m["csq"] = csq.astype(BF)
        mt = np.zeros((NQG * 4, 128, 256), dtype=np.float32)
        kk = np.arange(128)[:, None]
        qq = np.arange(128)[None, :]
        for i in range(NQG):
            g0, g1 = own[2*i], own[2*i+1]
            for j in range(4):
                kt = 4 * i + j
                for bi, g in ((0, g0), (1, g1)):
                    mt[4*i+j, :, bi*128:(bi+1)*128] = (kt*128 + kk <= g*128 + qq)
        m["maskt"] = mt.astype(BF)
        in_maps.append(m)
        metas.append((b, own))
    return in_maps, metas



class _Counts:
    def __init__(self):
        self.c = {}

    def inc(self, instr, semh, amt=1):
        instr.then_inc(semh, amt)
        self.c[semh.name] = self.c.get(semh.name, 0) + amt

    def val(self, semh):
        return self.c.get(semh.name, 0)


def _shift_ap(ap, delta):
    return bass.AP(ap.tensor, ap.offset + delta, ap.ap)


def build_program(taps=(), ngrp_a=NGRP, do_b=True, do_c=True, nqg=NQG, nc_only=False):
    nc = bass.Bass("TRN2", target_bir_lowering=False, dynamic_dma_scratch_size=16384)
    nt = _Counts()

    def dram(n, s, d, k="ExternalInput"):
        return nc.dram_tensor(n, s, d, kind=k).ap()

    d_x = dram("x", [8, 128, S], F32)
    d_w1 = dram("w1r", [NEXP, 8, 128, DEXP], F32)
    d_w2 = dram("w2r", [NEXP, 4, 128, E], F32)
    d_wqa = dram("wqa", [8, 128, QL], F32)
    d_wqb = dram("wqb", [QL, 2048], F32)
    d_wkva = dram("wkva", [8, 128, KVL], F32)
    d_wkvbk = dram("wkvbk", [KVL, 512], F32)
    d_wkvbv = dram("wkvbv", [KVL, 256], F32)
    d_wout = dram("wout", [8, 128, E], F32)
    d_bout = dram("bout", [128, 8], F32)
    d_projw = dram("projw", [128, 8 * NEXP], F32)
    d_anchT = dram("anchT", [NEXP, NEXP], F32)
    d_rtemp = dram("rtemp", [1, 1], F32)
    d_ident = dram("ident", [128, 128], F32)
    d_csk = dram("csk", [128, 2 * S], BF16)
    d_csq = dram("csq", [128, 2 * OWN], BF16)
    d_maskt = dram("maskt", [NQG * 4, 128, 256], BF16)
    d_cinfo = dram("coreinfof", [1, 1], F32)
    d_out = dram("out", [8, 128, OWN], F32, "ExternalOutput")

    spk = "ExternalOutput" if "spill" in taps else "Internal"
    d_x1own = dram("x1own", [8, 128, OWN], F32, spk)
    d_yown = dram("yown", [8, 128, OWN], BF16, spk)
    x2k = "ExternalOutput" if "x2" in taps else "Internal"
    d_x2sp = dram("x2sp", [8, 128, OWN], F32, x2k)
    if "kv" in taps:
        d_kadbg = dram("kadbg", [128, S], BF16, "ExternalOutput")
        d_kbdbg = dram("kbdbg", [128, S], BF16, "ExternalOutput")
        d_vdbg = dram("vdbg", [128, NB * 260], BF16, "ExternalOutput")
    if "ew" in taps:
        d_ewdbg = dram("ewdbg", [1, NGRP * 1536], F32, "ExternalOutput")
        d_zdbg = dram("zdbg", [NEXP, S], F32, "ExternalOutput")
        d_r6dbg = dram("r6dbg", [128, 16 * NGRP], F32, "ExternalOutput")
        d_ewtdbg = dram("ewtdbg", [128, 16 * NGRP], F32, "ExternalOutput")
        d_q6dbg = dram("q6dbg", [NEXP, S], F32, "ExternalOutput")
        d_rscdbg = dram("rscdbg", [128, 64 * NGRP], F32, "ExternalOutput")

    from contextlib import ExitStack
    es = ExitStack()

    def sbt(n, s, d):
        return es.enter_context(nc.sbuf_tensor(n, s, d))

    def mksem(n):
        return es.enter_context(nc.semaphore(n))

    t_w1 = [sbt(f"w1_{e_}", [128, 8 * DEXP], BF16) for e_ in range(NEXP)]
    t_w2 = [sbt(f"w2_{e_}", [128, 4 * E], BF16) for e_ in range(NEXP)]
    t_wqa = sbt("wqas", [128, 8 * QL], BF16)
    t_wqb = sbt("wqbs", [QL, 2048], BF16)
    t_wkva = sbt("wkvas", [128, 8 * KVL], BF16)
    t_wkvbk = sbt("wkvbks", [KVL, 512], BF16)
    t_wkvbv = sbt("wkvbvs", [KVL, 256], BF16)
    t_bout = sbt("bouts", [128, 8], F32)
    t_projw = sbt("projws", [128, 8 * NEXP], F32)
    t_anchT = sbt("anchTs", [NEXP, NEXP], F32)
    t_ident = sbt("idents", [128, 128], F32)
    t_ka = sbt("ka", [128, S], BF16)
    t_kb = sbt("kb", [128, S], BF16)
    t_v = sbt("v", [128, NB * 260], BF16)
    t_onesr = sbt("onesr", [128, 128], F32)
    t_onesm = sbt("onesm", [1, 128], F32)
    t_onescol = sbt("onescol", [128, 1], F32)
    t_eps = sbt("epsc", [128, 1], F32)
    t_onescb = sbt("onescb", [128, 1], BF16)
    t_pw = sbt("pwsum", [1, NEXP], F32)
    t_alpha = sbt("alpha", [128, 1], F32)
    t_alpham = sbt("alpham", [128, 1], F32)
    t_rt = sbt("rtemps", [1, 1], F32)
    t_ci = sbt("cis", [1, 1], F32)
    t_hf = sbt("hflag", [128, 1], F32)

    def wout_chunk(ec):
        t = t_w1[ec] if ec < NEXP else t_w2[ec - NEXP]
        return t[:, 0:E]

    t_xga = sbt("xga", [128, 8 * GRP], F32)
    t_xgb = sbt("xgb", [128, 8 * GRP], F32)
    t_xsqp = sbt("xsqp", [128, 512], BF16)
    t_lnbc = sbt("lnbc", [128, 512], BF16)
    t_xh = sbt("xh", [128, 8 * GRP], BF16)
    t_stat = sbt("stat", [1, 1536], F32)
    t_z = sbt("z", [NEXP, GRP], F32)
    t_q6 = sbt("q6", [NEXP, GRP], F32)
    t_r6 = sbt("r6", [128, 16], F32)
    t_rsc = sbt("rsc", [128, 64], F32)
    t_rs2 = sbt("rs2", [128, 16], F32)
    t_ew = sbt("ew", [128, 16], F32)
    t_ewf6 = sbt("ewf6", [1, NEXP * GRP], F32)
    t_ewb = sbt("ewb", [128, NEXP * GRP], BF16)
    t_h = sbt("h", [128, 4 * GRP], BF16)
    t_ht = sbt("ht", [128, NEXP * 4 * GRP], BF16)
    t_x1g = sbt("x1g", [128, 8 * GRP], F32)
    t_yh = sbt("yhg", [128, 8 * GRP], BF16)
    t_kva = sbt("kvag", [KVL, GRP], BF16)
    t_cs = sbt("cs", [128, 512], BF16)
    t_rtmp = sbt("rtmp", [128, 512], BF16)
    t_qa = sbt("qag", [QL, 256], BF16)

    t_xsq = t_ht[:, 0:4096].bitcast(F32)
    t_rc = t_stat[0:1, 0:1024]
    t_osp = t_ht[:, 0:2048].bitcast(F32)
    v_q = t_ht[:, 0:2048]
    v_ps = (t_ht[:, 2048:3072], t_ht[:, 3072:4096])
    v_a = t_ht[:, 4096:6144]
    v_rb = t_rtmp
    v_masks = t_ewb[:, 0:1024]
    v_x1b = t_xga
    v_yb = t_yh

    psA = es.enter_context(nc.psum_tensor("arena", [128, 4096], F32))

    def bk(i, w=512, p0=0, p1=128, off=0):
        return psA[p0:p1, i * 512 + off: i * 512 + off + w]

    s_dma = mksem("s_dma")
    s_dmag = mksem("s_dmag")
    s_sp = mksem("s_sp")
    s_pe = mksem("s_pe")
    s_act = mksem("s_act")
    s_dve = mksem("s_dve")
    s_fen = mksem("s_fen")
    fenc = [0]

    blk = es.enter_context(nc.Block())
    SEM_OF = {"tensor": s_pe, "scalar": s_act, "vector": s_dve}
    COMPUTE = [s_pe, s_act, s_dve]
    ALLS = [s_dma, s_dmag, s_sp, s_pe, s_act, s_dve]

    stepno = [0]
    writers = {}
    readers = {}
    SEMS = {s.name: s for s in ALLS}
    all_steps = []
    epoch = [0]

    class _Step:
        __slots__ = ("sid", "eng", "fn", "dma_sem", "ndma", "deps", "est",
                     "epoch", "sem", "amt", "cnt", "start", "finish")

        def __init__(s, sid, eng, fn, dma_sem, ndma, deps, est, ep):
            s.sid, s.eng, s.fn, s.dma_sem, s.ndma = sid, eng, fn, dma_sem, ndma
            s.deps, s.est, s.epoch = deps, est, ep
            s.sem = dma_sem if dma_sem is not None else SEM_OF[eng]
            s.amt = 16 * ndma if dma_sem is not None else 1

    EST = {"tensor": 1500, "scalar": 1200, "vector": 800, "sync": 4000,
           "gpsimd": 4000}
    import os
    SCHED_MODE = [os.environ.get("KSCHED", "cp")]
    last_on_sem = {}
    bar_deps = [{}]

    def step(eng, deps_unused, fn, dma_sem=None, ndma=1, r=(), w=(), est=None):
        stepno[0] += 1
        if stepno[0] > STEP_LIMIT[0]:
            return
        annotated = bool(r) or bool(w)
        deps = set(bar_deps[0].values())
        if annotated:
            for res in r:
                if res in writers:
                    deps.add(writers[res])
            for res in w:
                if res in writers:
                    deps.add(writers[res])
                deps.update(readers.get(res, ()))
        else:
            deps.update(last_on_sem.values())
        st = _Step(stepno[0], eng, fn, dma_sem, ndma, deps,
                   est if est is not None else EST[eng], epoch[0])
        all_steps.append(st)
        last_on_sem[st.sem.name] = st
        if annotated:
            for res in r:
                readers.setdefault(res, []).append(st)
            for res in w:
                writers[res] = st
                readers[res] = []
        else:
            for res in set(list(writers) + list(readers)):
                writers[res] = st
                readers[res] = []
        return st

    def barrier():
        epoch[0] += 1
        bar_deps[0] = dict(last_on_sem)

    def flush_plan():
        import heapq
        order = {e: [] for e in plan_engines}
        eng_free = {e: 0.0 for e in plan_engines}
        maxep = max((s.epoch for s in all_steps), default=0)
        ep_end = 0.0
        mode = SCHED_MODE[0]
        for ep in range(maxep + 1):
            steps_ = [s for s in all_steps if s.epoch == ep]
            if not steps_:
                continue
            indeg = {}
            for s in steps_:
                indeg[s] = sum(1 for d in s.deps if d.epoch == ep)
            succs = {s: [] for s in steps_}
            for s in steps_:
                for d in s.deps:
                    if d.epoch == ep:
                        succs[d].append(s)
            lp = {}
            for s in sorted(steps_, key=lambda t: -t.sid):
                lp[s] = s.est + max((lp[t] for t in succs[s]), default=0.0)

            def key(s, rt):
                if mode == 'fifo':
                    return s.sid
                if mode == 'cp':
                    return (rt, -lp[s])
                if mode == 'cpb':
                    return (int(rt // 2000), -lp[s])
                return rt
            ready = []
            for s in steps_:
                if indeg[s] == 0:
                    rt = max([ep_end] + [d.finish for d in s.deps if d.epoch < ep])
                    heapq.heappush(ready, (key(s, rt), s.sid, s))
            scheduled = 0
            while ready:
                _, _, s = heapq.heappop(ready)
                rt = max([ep_end] + [d.finish for d in s.deps if d.epoch == ep] +
                         [d.finish for d in s.deps if d.epoch < ep])
                start = max(rt, eng_free[s.eng])
                s.start = start
                s.finish = start + s.est
                eng_free[s.eng] = s.finish
                order[s.eng].append(s)
                scheduled += 1
                for t in succs[s]:
                    indeg[t] -= 1
                    if indeg[t] == 0:
                        rt2 = max([ep_end] + [d.finish for d in t.deps])
                        heapq.heappush(ready, (key(t, rt2), t.sid, t))
            assert scheduled == len(steps_), (scheduled, len(steps_))
            ep_end = max(eng_free.values())
        PRED[:] = [ep_end]
        sem_eng = {}
        for e in plan_engines:
            c = {}
            for s in order[e]:
                assert sem_eng.setdefault(s.sem.name, e) == e, s.sem.name
                c[s.sem.name] = c.get(s.sem.name, 0) + s.amt
                s.cnt = c[s.sem.name]
        SCHED_DUMP.clear()
        for e in plan_engines:
            for pos, s in enumerate(order[e]):
                SCHED_DUMP.append((e, pos, s.sid, s.sem.name, s.cnt,
                                   sorted((d.eng, d.sid, d.sem.name, d.cnt)
                                          for d in s.deps)))
        for e in plan_engines:
            if not order[e]:
                continue

            def body(eng_h, steps_=order[e]):
                hw = {}
                for s in steps_:
                    dd = {}
                    for d in s.deps:
                        sn = d.sem.name
                        if dd.get(sn, 0) < d.cnt:
                            dd[sn] = d.cnt
                    for sn, c in sorted(dd.items()):
                        if hw.get(sn, 0) < c:
                            eng_h.wait_ge(SEMS[sn], c)
                            hw[sn] = c
                    rr = s.fn(eng_h)
                    if s.dma_sem is not None:
                        lst = rr if isinstance(rr, (list, tuple)) else [rr]
                        assert len(lst) == s.ndma, (len(lst), s.ndma)
                        for ins in lst:
                            ins.then_inc(s.dma_sem, 16)
                    else:
                        rr.then_inc(s.sem, 1)
            getattr(blk, e)(body)

    plan_engines = ("sync", "gpsimd", "tensor", "scalar", "vector")


    step("sync", [], lambda e: [
        e.dma_start(out=t_ident[:], in_=d_ident),
        e.dma_start(out=t_anchT[:], in_=d_anchT),
        e.dma_start(out=t_bout[:], in_=d_bout),
        e.dma_start(out=t_projw[:], in_=d_projw),
        e.dma_start(out=t_rt[:], in_=d_rtemp),
        e.dma_start(out=t_ci[:], in_=d_cinfo),
    ], dma_sem=s_dma, ndma=6, w=("ident", "anchT", "bout", "projw", "smallw"))

    def f_init_gp(e):
        r = []
        for x_ in range(NEXP):
            r.append(e.dma_start(out=t_w1[x_][:].rearrange("p (c j) -> p c j", c=8), in_=d_w1[x_].rearrange("c p j -> p c j")))
            r.append(e.dma_start(out=t_w2[x_][:].rearrange("p (c j) -> p c j", c=4), in_=d_w2[x_].rearrange("c p j -> p c j")))
        r.append(e.dma_start(out=t_wqa[:].rearrange("p (c j) -> p c j", c=8), in_=d_wqa.rearrange("c p j -> p c j")))
        r.append(e.dma_start(out=t_wqb[:], in_=d_wqb))
        r.append(e.dma_start(out=t_wkva[:].rearrange("p (c j) -> p c j", c=8), in_=d_wkva.rearrange("c p j -> p c j")))
        r.append(e.dma_start(out=t_wkvbk[:], in_=d_wkvbk))
        r.append(e.dma_start(out=t_wkvbv[:], in_=d_wkvbv))
        return r
    step("gpsimd", [], f_init_gp, dma_sem=s_dmag, ndma=17, w=("wsp",))

    def f_init_dve(e):
        e.memset(t_onesr[:], 1.0)
        e.memset(t_onesm[:], -1.0)
        e.memset(t_onescol[:], 1.0)
        e.memset(t_eps[:], EPS)
        e.memset(t_onescb[:], 1.0)
        vones = t_v[:].rearrange("p (t k j) -> p t k j", t=NB, k=4)[:, :, :, 64:65]
        return e.memset(vones, 1.0)
    step("vector", [], f_init_dve, w=("consts", "v"))

    def f_init_pe(e):
        for c in range(8):
            e.matmul(bk(0, NEXP, 0, 1), t_onescol[:, 0:1],
                     t_projw[:, c*NEXP:(c+1)*NEXP], start=(c == 0), stop=(c == 7))
        e.matmul(bk(1, 1), t_onesr[0:1, 0:128], t_rt[0:1, 0:1],
                 start=True, stop=True)
        return e.matmul(bk(1, 1, off=1), t_onesr[0:1, 0:128], t_ci[0:1, 0:1],
                        start=True, stop=True)
    step("tensor", [s_dma, s_dmag], f_init_pe, r=("projw", "consts", "smallw"), w=("pb0", "pb1"))

    def f_init_dve2(e):
        e.tensor_copy(t_pw[0:1, 0:NEXP], bk(0, NEXP, 0, 1))
        e.tensor_copy(t_hf[:, 0:1], bk(1, 1, off=1))
        e.tensor_scalar_max(t_rsc[:, 0:1], bk(1, 1), 0.1)
        e.drain()
        e.reciprocal(t_rsc[:, 1:2], t_rsc[:, 0:1])
        e.drain()
        e.tensor_scalar_mul(t_alpha[:, 0:1], t_rsc[:, 1:2], 0.5)
        e.drain()
        return e.tensor_scalar_mul(t_alpham[:, 0:1], t_alpha[:, 0:1], -1.0)
    step("vector", [s_pe], f_init_dve2, r=("pb0", "pb1"), w=("pw", "hf", "alpha", "rsc"))

    def ln_block(src_c, dst, router_extras, srcres, dstres, src_t=None):
        n = GRP

        def f_sums(e):
            r = None
            for c in range(8):
                r = e.matmul(bk(0, n, 0, 1), t_onescol[:, 0:1], src_c(c),
                             start=(c == 0), stop=(c == 7))
            return r
        step("tensor", [s_dma, s_dve], f_sums, r=(srcres, "consts"), w=("pb0",))
        for c in range(8):
            pp = c % 2
            step("scalar", [], lambda e, c=c, pp=pp: e.activation(
                t_xsqp[:, pp*n:(pp+1)*n], src_c(c), ACT.Square),
                 r=(srcres,), w=("xsqp%d" % pp,))
            step("tensor", [], lambda e, c=c, pp=pp: e.matmul(
                bk(0, n, 0, 1, off=n), t_onescb[:, 0:1], t_xsqp[:, pp*n:(pp+1)*n],
                start=(c == 0), stop=(c == 7)),
                 r=("xsqp%d" % pp, "consts"), w=("pb0",))

        def f_mA(e):
            e.activation(t_stat[0:1, 0:n], bk(0, n, 0, 1), ACT.Copy, scale=1.0 / E)
            e.drain()
            return e.activation(t_stat[0:1, 256:256+n], t_stat[0:1, 0:n],
                                ACT.Square)
        step("scalar", [s_pe], f_mA, r=("pb0",), w=("stat",))

        def f_var(e):
            return e.scalar_tensor_tensor(
                t_stat[0:1, 512:512+n], bk(0, n, 0, 1, off=n), 1.0 / E,
                t_stat[0:1, 256:256+n], OP.mult, OP.subtract)
        step("vector", [s_act], f_var, r=("pb0", "stat"), w=("stat",))

        def f_rstd(e):
            e.activation(t_stat[0:1, 256:256+n], t_stat[0:1, 512:512+n], ACT.Ln,
                         bias=t_eps[0:1, 0:1])
            e.drain()
            r = e.activation(t_stat[0:1, 768:768+n], t_stat[0:1, 256:256+n],
                             ACT.Exp, scale=-0.5)
            if router_extras:
                r = e.activation(t_stat[0:1, 1280:1280+n], t_stat[0:1, 0:n],
                                 ACT.Copy, scale=-1.0)
            return r
        step("scalar", [s_dve], f_rstd, r=("stat", "consts"), w=("stat",))

        def f_chain3(e):
            return e.tensor_mul(t_stat[0:1, 1024:1024+n], t_stat[0:1, 0:n],
                                t_stat[0:1, 768:768+n])
        step("vector", [s_act], f_chain3, r=("stat",), w=("stat",))

        def f_bc(e):
            e.matmul(bk(1, n), t_onesr[0:1, 0:128], t_stat[0:1, 768:768+n],
                     start=True, stop=True)
            return e.matmul(bk(1, n, off=256), t_onesm[0:1, 0:128],
                            t_stat[0:1, 1024:1024+n], start=True, stop=True)
        step("tensor", [s_dve], f_bc, r=("stat", "consts"), w=("pb1",))

        def f_apply(e):
            e.tensor_copy(t_lnbc[:, 0:512], bk(1, 512))
            e.drain()
            if src_t is not None:
                s3 = src_t[:, 0:8*n].rearrange("p (c j) -> p c j", c=8)
                d3 = dst[:, 0:8*n].rearrange("p (c j) -> p c j", c=8)
                rb3 = t_lnbc[:, 0:n].unsqueeze(1).to_broadcast([128, 8, n])
                mb3 = t_lnbc[:, 256:256+n].unsqueeze(1).to_broadcast([128, 8, n])
                e.tensor_tensor(d3, s3, rb3, OP.mult)
                e.drain()
                return e.tensor_tensor(d3, d3, mb3, OP.add)
            def mul(c):
                e.tensor_tensor(t_rtmp[:, (c % 2)*n:(c % 2)*n+n], src_c(c),
                                t_lnbc[:, 0:n], OP.mult)
            def add(c):
                return e.tensor_tensor(dst[:, c*n:(c+1)*n],
                                       t_rtmp[:, (c % 2)*n:(c % 2)*n+n],
                                       t_lnbc[:, 256:256+n], OP.add)
            mul(0)
            for c in range(7):
                mul(c + 1)
                add(c)
            return add(7)
        step("vector", [s_pe], f_apply, r=(srcres, "pb1"), w=("lnbc", dstres, "rtmp"))

    def router_block(src_c, srcres):
        n = GRP

        def f_z(e):
            for c in range(8):
                e.matmul(bk(2, n, 0, NEXP), t_projw[:, c*NEXP:(c+1)*NEXP], src_c(c),
                         start=(c == 0), stop=False)
            return e.matmul(bk(2, n, 0, NEXP), t_pw[0:1, 0:NEXP],
                            t_stat[0:1, 1280:1280+n], start=False, stop=True)
        step("tensor", [s_dve], f_z, r=(srcres, "projw", "pw", "stat"), w=("pb2",))

        def f_zev(e):
            e.tensor_copy(t_q6[0:NEXP, 0:n], bk(2, n, 0, NEXP))
            e.drain()
            return e.tensor_tensor(t_z[0:NEXP, 0:n], t_q6[0:NEXP, 0:n],
                                   bk(1, n, 0, NEXP), OP.mult)
        step("vector", [s_pe], f_zev, r=("pb2", "pb1"), w=("q6", "z"))

        step("scalar", [s_dve], lambda e: e.activation(
            t_q6[0:NEXP, 0:n], t_z[0:NEXP, 0:n], ACT.Exp, scale=2.0 * QTEMP_INV),
             r=("z",), w=("q6",))

        def f_tanh2(e):
            e.tensor_scalar_add(t_z[0:NEXP, 0:n], t_q6[0:NEXP, 0:n], 1.0)
            e.drain()
            e.reciprocal(t_q6[0:NEXP, 0:n], t_z[0:NEXP, 0:n])
            e.drain()
            return e.tensor_scalar(t_q6[0:NEXP, 0:n], t_q6[0:NEXP, 0:n], -2.0, 1.0,
                                   OP.mult, OP.add)
        step("vector", [s_act], f_tanh2, r=("q6", "z"), w=("q6", "z"))

        def f_dot(e):
            r = None
            for b2 in range(2):
                r = e.matmul(bk(2, NEXP, 0, 128, off=256 + b2*NEXP),
                             t_q6[0:NEXP, b2*128:(b2+1)*128], t_anchT[0:NEXP, 0:NEXP],
                             start=True, stop=True)
            return r
        step("tensor", [s_act], f_dot, r=("q6", "anchT"), w=("pb2",))

        def f_tok(e):
            R = lambda b2: t_r6[:, b2*8:b2*8+NEXP]
            for b2 in range(2):
                e.tensor_copy(R(b2), bk(2, NEXP, 0, 128, off=256 + b2*NEXP))
            for b2 in range(2):
                e.tensor_reduce(t_rs2[:, b2*8+0:b2*8+1], R(b2), AX, OP.max)
            for b2 in range(2):
                e.tensor_scalar(t_rsc[:, b2*32+8:b2*32+8+NEXP], R(b2),
                                t_rs2[:, b2*8+0:b2*8+1], 0.0, OP.subtract, OP.is_equal)
            for b2 in range(2):
                e.scalar_tensor_tensor(t_rsc[:, b2*32+16:b2*32+16+NEXP],
                                       t_rsc[:, b2*32+8:b2*32+8+NEXP], -1e30, R(b2),
                                       OP.mult, OP.add)
            for b2 in range(2):
                e.tensor_reduce(t_rs2[:, b2*8+1:b2*8+2],
                                t_rsc[:, b2*32+16:b2*32+16+NEXP], AX, OP.max)
            for b2 in range(2):
                rr = e.tensor_tensor(t_rs2[:, b2*8+2:b2*8+3], t_rs2[:, b2*8+0:b2*8+1],
                                     t_rs2[:, b2*8+1:b2*8+2], OP.subtract)
            return rr
        step("vector", [s_pe], f_tok, r=("pb2",), w=("r6", "rsc", "rs2"))

        def f_sig(e):
            for b2 in range(2):
                r = e.activation(t_rs2[:, b2*8+5:b2*8+6], t_rs2[:, b2*8+2:b2*8+3],
                                 ACT.Exp, scale=t_alpham[:, 0:1])
            return r
        step("scalar", [s_dve], f_sig, r=("rs2", "alpha"), w=("rs2",))

        def f_ew(e):
            for b2 in range(2):
                e.tensor_scalar_add(t_rsc[:, b2*32:b2*32+1], t_rs2[:, b2*8+5:b2*8+6], 1.0)
            for b2 in range(2):
                e.reciprocal(t_rs2[:, b2*8+3:b2*8+4], t_rsc[:, b2*32:b2*32+1])
            for b2 in range(2):
                e.tensor_scalar(t_rs2[:, b2*8+4:b2*8+5], t_rs2[:, b2*8+3:b2*8+4],
                                -1.0, 1.0, OP.mult, OP.add)
            for b2 in range(2):
                e.tensor_scalar(t_rsc[:, b2*32+24:b2*32+24+NEXP],
                                t_rsc[:, b2*32+16:b2*32+16+NEXP],
                                t_rs2[:, b2*8+1:b2*8+2], 0.0, OP.subtract, OP.is_equal)
            for b2 in range(2):
                e.tensor_scalar(t_rsc[:, b2*32+16:b2*32+16+NEXP],
                                t_rsc[:, b2*32+24:b2*32+24+NEXP],
                                t_rs2[:, b2*8+4:b2*8+5], None, OP.mult)
            for b2 in range(2):
                r = e.scalar_tensor_tensor(t_ew[:, b2*8:b2*8+NEXP],
                                           t_rsc[:, b2*32+8:b2*32+8+NEXP],
                                           t_rs2[:, b2*8+3:b2*8+4],
                                           t_rsc[:, b2*32+16:b2*32+16+NEXP],
                                           OP.mult, OP.add)
            return r
        step("vector", [s_act], f_ew, r=("rsc", "rs2"), w=("rsc", "rs2", "ew"))

        def f_ewT(e):
            r = None
            for x_ in range(NEXP):
                for b2 in range(2):
                    r = e.transpose(psA[0:1, 1024 + x_*256 + b2*128:
                                        1024 + x_*256 + b2*128 + 128],
                                    t_ew[:, b2*8 + x_: b2*8 + x_ + 1], t_ident[:])
            return r
        step("tensor", [s_dve], f_ewT, r=("ew", "ident"), w=("pb2", "pb3", "pb4"))

        def f_ewf(e):
            r = None
            for x_ in range(NEXP):
                for b2 in range(2):
                    o = x_*256 + b2*128
                    r = e.tensor_copy(t_ewf6[0:1, o:o+128],
                                      psA[0:1, 1024 + o: 1024 + o + 128])
            return r
        step("vector", [s_pe], f_ewf, r=("pb2", "pb3", "pb4"), w=("ewf",))

    def ffn_block(evac, xgres):
        n = GRP
        for x0 in (0, 2, 4):
            eb = 2 + (x0 // 2) % 2
            def f_ewbc(e, x0=x0, eb=eb):
                e.matmul(bk(eb, n), t_onesr[0:1, 0:128], t_ewf6[0:1, x0*n:(x0+1)*n],
                         start=True, stop=True)
                return e.matmul(bk(eb, n, off=256), t_onesr[0:1, 0:128],
                                t_ewf6[0:1, (x0+1)*n:(x0+2)*n], start=True, stop=True)
            step("tensor", [s_dve], f_ewbc, r=("ewf", "consts"), w=("pb%d" % eb,))

            def f_ewbev(e, x0=x0, eb=eb):
                e.tensor_copy(t_ewb[:, x0*n:(x0+1)*n], bk(eb, n))
                return e.tensor_copy(t_ewb[:, (x0+1)*n:(x0+2)*n], bk(eb, n, off=256))
            step("vector", [s_pe], f_ewbev, r=("pb%d" % eb,), w=("ewb",))

        for x_ in range(NEXP):
            hb = 4 if x_ % 2 == 0 else 6

            def f_w1(e, x_=x_, hb=hb):
                r = None
                for m_ in range(4):
                    for c in range(8):
                        r = e.matmul(psA[0:128, hb*512 + m_*n: hb*512 + (m_+1)*n],
                                     t_w1[x_][:, c*512 + m_*128: c*512 + (m_+1)*128],
                                     t_xh[:, c*n:(c+1)*n], start=(c == 0), stop=(c == 7))
                return r
            step("tensor", [s_dve, s_act], f_w1, r=("xh", "wsp"),
                 w=(("pb4", "pb5") if hb == 4 else ("pb6", "pb7")))

            def f_gelu(e, x_=x_, hb=hb):
                return e.activation(t_h[:, 0:4*n], psA[0:128, hb*512: hb*512 + 4*n],
                                    ACT.Gelu_apprx_tanh)
            step("scalar", [s_pe], f_gelu,
                 r=(("pb4", "pb5") if hb == 4 else ("pb6", "pb7")), w=("h",))

            def f_ht(e, x_=x_):
                base = t_ewb[:, x_*n:(x_+1)*n]
                bc = base.unsqueeze(1).to_broadcast([128, 4, n])
                hh = t_h[:, 0:4*n].rearrange("p (m j) -> p m j", m=4)
                dd = t_ht[:, x_*4*n:(x_+1)*4*n].rearrange("p (m j) -> p m j", m=4)
                return e.tensor_tensor(dd, hh, bc, OP.mult)
            step("vector", [s_act], f_ht, r=("h", "ewb"), w=("ht",))

        for oc in range(8):
            ob = oc % 2
            def f_w2(e, oc=oc, ob=ob):
                first = True
                r = None
                for x_ in range(NEXP):
                    for k_ in range(4):
                        r = e.matmul(bk(ob, n), t_w2[x_][:, k_*E + oc*128: k_*E + (oc+1)*128],
                                     t_ht[:, (x_*4 + k_)*n:(x_*4 + k_ + 1)*n],
                                     start=first, stop=(x_ == NEXP-1 and k_ == 3))
                        first = False
                return r
            step("tensor", [s_dve], f_w2, r=("ht", "wsp"), w=("pb%d" % ob,))
            step("vector", [s_pe], lambda e, oc=oc, ob=ob: evac(e, oc, ob),
                 r=("pb%d" % ob, xgres), w=("x1g",))

    _sidA = nc.enter_named_scope("phaseA", False)[0]
    src_x1g = lambda c: t_x1g[:, c*GRP:(c+1)*GRP]

    def mk_src(t):
        return lambda c: t[:, c*GRP:(c+1)*GRP]

    def mk_evac(t):
        def ev(e, oc, ob):
            return e.scalar_tensor_tensor(t_x1g[:, oc*GRP:(oc+1)*GRP], bk(ob, GRP),
                                          0.5, t[:, oc*GRP:(oc+1)*GRP],
                                          OP.mult, OP.add)
        return ev

    for g in range(ngrp_a):
        t_xg = t_xga if g % 2 == 0 else t_xgb
        xgres = "xga" if g % 2 == 0 else "xgb"
        src_xg = mk_src(t_xg)

        def f_load(e, g=g, t_xg=t_xg):
            return [
                e.dma_start(out=t_xg[:].rearrange("p (c j) -> p c j", c=8),
                            in_=d_x[:, :, g*GRP:(g+1)*GRP].rearrange("c p j -> p c j")),
                e.dma_start(out=t_cs[:], in_=d_csk[:, g*512:(g+1)*512]),
            ]
        step("sync", COMPUTE, f_load, dma_sem=s_dma, ndma=2, w=(xgres, "cs"))

        ln_block(src_xg, t_xh, True, xgres, "xh", src_t=t_xg)
        router_block(src_xg, xgres)
        ffn_block(mk_evac(t_xg), xgres)

        s0 = (g % 2) * 128
        s1 = 128 - s0
        def f_blendx(e, g=g, s0=s0, s1=s1):
            def sub(c):
                e.tensor_tensor(t_osp[:, c*128:(c+1)*128], t_x1g[:, c*GRP+s1:c*GRP+s1+128],
                                t_x1g[:, c*GRP+s0:c*GRP+s0+128], OP.subtract)
            def stt(c):
                return e.scalar_tensor_tensor(t_osp[:, c*128:(c+1)*128],
                                              t_osp[:, c*128:(c+1)*128], t_hf[:, 0:1],
                                              t_x1g[:, c*GRP+s0:c*GRP+s0+128],
                                              OP.mult, OP.add)
            sub(0)
            for c in range(7):
                sub(c + 1)
                stt(c)
            return stt(7)
        step("vector", [s_dve], f_blendx, r=("x1g", "hf"), w=("ht",))

        def f_spill_x1(e, g=g):
            return e.dma_start(out=d_x1own[:, :, g*128:(g+1)*128].rearrange("c p j -> p c j"),
                               in_=t_osp[:].rearrange("p (c j) -> p c j", c=8))
        step("sync", [s_dve], f_spill_x1, dma_sem=s_sp, r=("ht",), w=("d_x1own",))

        if "ew" in taps:
            def f_ewtap(e, g=g):
                return [
                    e.dma_start(out=d_ewdbg[0:1, g*1536:(g+1)*1536],
                                in_=t_ewf6[0:1, 0:1536]),
                    e.dma_start(out=d_zdbg[:, g*GRP:(g+1)*GRP], in_=t_z[0:NEXP, 0:GRP]),
                    e.dma_start(out=d_q6dbg[:, g*GRP:(g+1)*GRP], in_=t_q6[0:NEXP, 0:GRP]),
                    e.dma_start(out=d_r6dbg[:, g*16:(g+1)*16], in_=t_r6[:, 0:16]),
                    e.dma_start(out=d_ewtdbg[:, g*16:(g+1)*16], in_=t_ew[:, 0:16]),
                    e.dma_start(out=d_rscdbg[:, g*64:g*64+16], in_=t_rs2[:, 0:16]),
                    e.dma_start(out=d_rscdbg[:, g*64+16:(g+1)*64], in_=t_rsc[:, 0:48]),
                ]
            step("sync", [s_dve], f_ewtap, dma_sem=s_sp, ndma=7)

        ln_block(src_x1g, t_yh, False, "x1g", "yh", src_t=t_x1g)

        def f_blendy(e, g=g, s0=s0, s1=s1):
            yb = t_osp[:, 0:512].bitcast(BF16)
            def sub(c):
                e.tensor_tensor(yb[:, c*128:(c+1)*128], t_yh[:, c*GRP+s1:c*GRP+s1+128],
                                t_yh[:, c*GRP+s0:c*GRP+s0+128], OP.subtract)
            def stt(c):
                return e.scalar_tensor_tensor(yb[:, c*128:(c+1)*128],
                                              yb[:, c*128:(c+1)*128], t_hf[:, 0:1],
                                              t_yh[:, c*GRP+s0:c*GRP+s0+128],
                                              OP.mult, OP.add)
            sub(0)
            for c in range(7):
                sub(c + 1)
                stt(c)
            return stt(7)
        step("vector", [s_dve], f_blendy, r=("yh", "hf"), w=("ht",))

        def f_spill_y(e, g=g):
            yb = t_osp[:, 0:512].bitcast(BF16)
            return e.dma_start(out=d_yown[:, :, g*128:(g+1)*128].rearrange("c p j -> p c j"),
                               in_=yb.rearrange("p (c j) -> p c j", c=8))
        step("sync", [s_dve], f_spill_y, dma_sem=s_sp, r=("ht",), w=("d_yown",))

        def f_kva(e):
            r = None
            for c in range(8):
                r = e.matmul(bk(2, GRP, 0, KVL), t_wkva[:, c*KVL:(c+1)*KVL],
                             t_yh[:, c*GRP:(c+1)*GRP], start=(c == 0), stop=(c == 7))
            return r
        step("tensor", [s_dve], f_kva, r=("yh", "wsp"), w=("pb2",))
        step("vector", [s_pe], lambda e: e.tensor_copy(
            t_kva[0:KVL, 0:GRP], bk(2, GRP, 0, KVL)), r=("pb2",), w=("kva",))

        for base, dst, kres in ((0, t_ka, "ka"), (256, t_kb, "kb")):
            def f_k(e, base=base):
                e.matmul(bk(3, 256), t_wkvbk[:, base:base+128], t_kva[0:KVL, 0:GRP],
                         start=True, stop=True)
                return e.matmul(bk(3, 256, off=256), t_wkvbk[:, base+128:base+256],
                                t_kva[0:KVL, 0:GRP], start=True, stop=True)
            step("tensor", [s_dve], f_k, r=("wsp", "kva"), w=("pb3",))

            def f_krope(e, dst=dst, g=g):
                e.tensor_tensor(t_rtmp[:, 0:512], bk(3, 512), t_cs[:, 0:512], OP.mult)
                e.drain()
                return e.tensor_tensor(dst[:, g*GRP:(g+1)*GRP], t_rtmp[:, 0:256],
                                       t_rtmp[:, 256:512], OP.add)
            step("vector", [s_pe], f_krope, r=("pb3", "cs"), w=("rtmp", kres))

        for tl in range(2):
            def f_v(e, tl=tl):
                return e.matmul(bk(2, 256, off=256), t_kva[0:KVL, tl*128:(tl+1)*128],
                                t_wkvbv[0:KVL, 0:256], start=True, stop=True)
            step("tensor", [s_dve], f_v, r=("kva", "wsp"), w=("pb2",))

            def f_vev(e, tl=tl, g=g):
                dstv = t_v[:].rearrange("p (t k j) -> p t k j", t=NB, k=4,
                                        j=65)[:, 2*g+tl, :, 0:64]
                return e.tensor_copy(dstv, bk(2, 256, off=256).rearrange(
                    "p (k j) -> p k j", k=4))
            step("vector", [s_pe], f_vev, r=("pb2",), w=("v",))

    if "kv" in taps:
        step("sync", [s_dve], lambda e: [
            e.dma_start(out=d_kadbg, in_=t_ka[:]),
            e.dma_start(out=d_kbdbg, in_=t_kb[:]),
            e.dma_start(out=d_vdbg, in_=t_v[:]),
        ], dma_sem=s_sp, ndma=3)

    nc.leave_named_scope("phaseA", _sidA, False)
    _sidB = nc.enter_named_scope("phaseB", False)[0]
    barrier()
    for nm_ in ("q", "p0", "p1", "a"):
        if "ht" in writers:
            writers[nm_] = writers["ht"]
        readers[nm_] = list(readers.get("ht", ()))
    def f_woutload(e):
        return [e.dma_start(out=wout_chunk(ec), in_=d_wout[ec])
                for ec in range(8)]
    step("gpsimd", [], f_woutload, dma_sem=s_dmag, ndma=8, w=("wsp",))

    if not do_b:
        nqg = 0
    step("sync", COMPUTE + [s_dma], lambda e: [
        e.dma_start(out=t_cs[:], in_=d_csq[:, 0:512])], dma_sem=s_dma, w=("cs",))

    for i in range(nqg):
        p_ = i % 2
        vx1 = t_xga if p_ == 0 else t_xgb
        xres = "xga" if p_ == 0 else "xgb"
        vmk = t_ewb[:, 0:1024] if p_ == 0 else t_xh[:, 0:1024]
        mres = "ewb" if p_ == 0 else "ewb2"
        vcs = t_cs if p_ == 0 else t_xh[:, 1024:1536]
        cres = "cs" if p_ == 0 else "cs2"

        def f_bload(e, i=i, vx1=vx1, vmk=vmk, vcs=vcs):
            r = [
                e.dma_start(
                    out=vx1[:].rearrange("p (c j) -> p c j", c=8),
                    in_=d_x1own[:, :, i*256:(i+1)*256].rearrange("c p j -> p c j")),
                e.dma_start(
                    out=v_yb[:].rearrange("p (c j) -> p c j", c=8),
                    in_=d_yown[:, :, i*256:(i+1)*256].rearrange("c p j -> p c j")),
                e.dma_start(
                    out=vmk.rearrange("p (t j) -> p t j", t=4),
                    in_=d_maskt[4*i:4*i+4].rearrange("t p j -> p t j")),
            ]
            if i > 0:
                r.append(e.dma_start(out=vcs[:, 0:512], in_=d_csq[:, i*512:(i+1)*512]))
            return r
        step("sync", COMPUTE, f_bload, dma_sem=s_dma, est=8000,
             ndma=4 if i > 0 else 3, r=("d_x1own", "d_yown"),
             w=(xres, "yh", mres) + ((cres,) if i > 0 else ()))

        def f_qa(e):
            r = None
            for c in range(8):
                r = e.matmul(bk(0, 256, 0, QL), t_wqa[:, c*QL:(c+1)*QL],
                             v_yb[:, c*256:(c+1)*256], start=(c == 0), stop=(c == 7))
            return r
        step("tensor", [s_dma], f_qa, r=("yh", "wsp"), w=("pb0",))
        step("vector", [s_pe], lambda e: e.tensor_copy(t_qa[0:QL, 0:256],
                                                       bk(0, 256, 0, QL)),
             r=("pb0",), w=("qa",))

        for c in range(8):
            def f_qc(e, c=c):
                e.matmul(bk(1, 256), t_wqb[0:QL, c*128:(c+1)*128], t_qa[0:QL, 0:256],
                         start=True, stop=True)
                return e.matmul(bk(1, 256, off=256),
                                t_wqb[0:QL, 1024 + c*128: 1024 + (c+1)*128],
                                t_qa[0:QL, 0:256], start=True, stop=True)
            step("tensor", [s_dve], f_qc, r=("wsp", "qa"), w=("pb1",))

            def f_qrope(e, c=c, vcs=vcs):
                e.tensor_tensor(t_rtmp[:, 0:512], bk(1, 512), vcs[:, 0:512], OP.mult)
                e.drain()
                return e.tensor_tensor(v_q[:, c*256:(c+1)*256], t_rtmp[:, 0:256],
                                       t_rtmp[:, 256:512], OP.add)
            step("vector", [s_pe], f_qrope, r=("pb1", cres), w=("rtmp", "q"))

        ktmax = 4 * i + 4
        for cp in range(4):
            Ksrc = t_ka if cp < 2 else t_kb
            kvlo = cp // 2
            kvhi = kvlo + 2

            def qk(e, kt, cp=cp, Ksrc=Ksrc):
                sb = (kt % 2) * 2
                e.matmul(bk(sb, 256), Ksrc[0:64, kt*128:(kt+1)*128],
                         v_q[0:64, (2*cp)*256:(2*cp+1)*256], start=True, stop=True,
                         tile_position=(0, 0))
                e.matmul(bk(sb, 256, off=256), Ksrc[0:64, kt*128:(kt+1)*128],
                         v_q[0:64, (2*cp+1)*256:(2*cp+2)*256], start=True, stop=True,
                         tile_position=(0, 0))
                e.matmul(bk(sb + 1, 256), Ksrc[64:128, kt*128:(kt+1)*128],
                         v_q[64:128, (2*cp)*256:(2*cp+1)*256], start=True, stop=True,
                         tile_position=(64, 0))
                return e.matmul(bk(sb + 1, 256, off=256), Ksrc[64:128, kt*128:(kt+1)*128],
                                v_q[64:128, (2*cp+1)*256:(2*cp+2)*256], start=True,
                                stop=True, tile_position=(64, 0))

            def sres(kt):
                return ("pb0", "pb1") if kt % 2 == 0 else ("pb2", "pb3")

            def f_tzero(e):
                e.memset(psA[0:128, 4*512:5*512], 0.0)
                return e.memset(psA[0:128, 5*512:6*512], 0.0)
            step("vector", [], f_tzero, w=("pb4", "pb5"))

            step("tensor", [], lambda e, qk=qk: qk(e, 0),
                 r=("q", "ka", "kb"), w=sres(0))
            if ktmax > 1:
                step("tensor", [], lambda e, qk=qk: qk(e, 1),
                     r=("q", "ka", "kb"), w=sres(1))

            for kt in range(ktmax):
                pb = kt % 2
                step("scalar", [], lambda e, pb=pb: e.activation(
                    v_ps[pb][:, 0:1024], psA[0:128, pb*1024:(pb+1)*1024], ACT.Exp,
                    scale=SCALE),
                     r=sres(kt), w=("p%d" % pb,))

                if kt >= 4 * i:
                    j = kt - 4 * i

                    def f_mask(e, j=j, pb=pb, vmk=vmk):
                        mb = vmk[:, j*256:(j+1)*256]
                        bc4 = mb.unsqueeze(1).to_broadcast([128, 4, 256])
                        pv = v_ps[pb][:, 0:1024].rearrange("p (s j) -> p s j", s=4)
                        return e.tensor_tensor(pv, pv, bc4, OP.mult)
                    step("vector", [], f_mask, r=("p%d" % pb, mres), w=("p%d" % pb,))

                def f_av(e, kt=kt, cp=cp, kvlo=kvlo, kvhi=kvhi, qk=qk, pb=pb,
                         last=(kt == ktmax - 1), pre=(kt + 2 < ktmax)):
                    st = (kt == 0)
                    sp_ = last
                    vp = v_ps[pb]
                    e.matmul(psA[0:64, 4*512: 4*512+256],
                             t_v[:, kt*260 + kvlo*65: kt*260 + kvlo*65 + 64],
                             vp[:, 0:256], start=False, stop=sp_, tile_position=(0, 0),
                             skip_group_check=True)
                    e.matmul(psA[64:128, 4*512: 4*512+256],
                             t_v[:, kt*260 + kvlo*65: kt*260 + kvlo*65 + 64],
                             vp[:, 256:512], start=False, stop=sp_, tile_position=(0, 64),
                             skip_group_check=True)
                    e.matmul(psA[0:64, 5*512: 5*512+256],
                             t_v[:, kt*260 + kvhi*65: kt*260 + kvhi*65 + 64],
                             vp[:, 512:768], start=False, stop=sp_, tile_position=(0, 0),
                             skip_group_check=True)
                    e.matmul(psA[64:128, 5*512: 5*512+256],
                             t_v[:, kt*260 + kvhi*65: kt*260 + kvhi*65 + 64],
                             vp[:, 768:1024], start=False, stop=sp_, tile_position=(0, 64),
                             skip_group_check=True)
                    e.matmul(bk(6, 512, 0, 1), t_onescb[:, 0:1], vp[:, 0:512],
                             start=st, stop=sp_)
                    r = e.matmul(bk(7, 512, 0, 1), t_onescb[:, 0:1], vp[:, 512:1024],
                                 start=st, stop=sp_)
                    if pre:
                        r = qk(e, kt + 2)
                    return r
                avw = ("pb4", "pb5", "pb6", "pb7") + (sres(kt) if kt + 2 < ktmax else ())
                step("tensor", [], f_av, r=("p%d" % pb, "v", "consts", "q", "ka", "kb"),
                     w=avw)

            def f_rc(e):
                return e.reciprocal(t_rc[0:1, 0:1024], psA[0:1, 6*512:8*512])
            step("vector", [], f_rc, r=("pb6", "pb7"), w=("stat",))

            def f_rb(e):
                e.matmul(bk(6, 256, 0, 64), t_onesr[0:1, 0:64], t_rc[0:1, 0:256],
                         start=True, stop=True, tile_position=(0, 0))
                e.matmul(bk(6, 256, 64, 128), t_onesr[0:1, 0:64], t_rc[0:1, 256:512],
                         start=True, stop=True, tile_position=(0, 64))
                e.matmul(bk(6, 256, 0, 64, off=256), t_onesr[0:1, 0:64],
                         t_rc[0:1, 512:768], start=True, stop=True, tile_position=(0, 0))
                return e.matmul(bk(6, 256, 64, 128, off=256), t_onesr[0:1, 0:64],
                                t_rc[0:1, 768:1024], start=True, stop=True,
                                tile_position=(0, 64))
            step("tensor", [], f_rb, r=("stat", "consts"), w=("pb6",))

            def f_norm(e, cp=cp):
                e.tensor_copy(v_rb[:, 0:512], bk(6, 512))
                e.drain()
                e.tensor_tensor(v_a[0:64, (2*cp)*256:(2*cp+1)*256],
                                psA[0:64, 4*512: 4*512+256], v_rb[0:64, 0:256], OP.mult)
                e.tensor_tensor(v_a[64:128, (2*cp)*256:(2*cp+1)*256],
                                psA[64:128, 4*512: 4*512+256], v_rb[64:128, 0:256],
                                OP.mult)
                e.tensor_tensor(v_a[0:64, (2*cp+1)*256:(2*cp+2)*256],
                                psA[0:64, 5*512: 5*512+256], v_rb[0:64, 256:512],
                                OP.mult)
                return e.tensor_tensor(v_a[64:128, (2*cp+1)*256:(2*cp+2)*256],
                                       psA[64:128, 5*512: 5*512+256],
                                       v_rb[64:128, 256:512], OP.mult)
            step("vector", [], f_norm,
                 r=("pb6", "pb4", "pb5"), w=("rtmp", "a"))

        for oc in range(8):
            wbk = 1 if oc % 2 == 0 else 0
            wres = "pb1" if oc % 2 == 0 else "pb0"

            def f_wo(e, oc=oc, wbk=wbk):
                r = None
                for ec in range(8):
                    r = e.matmul(bk(wbk, 256),
                                 wout_chunk(ec)[:, oc*128:(oc+1)*128],
                                 v_a[:, ec*256:(ec+1)*256], start=(ec == 0), stop=(ec == 7),
                                 skip_group_check=True)
                return r
            step("tensor", [s_dve], f_wo, r=("a", "wsp"), w=(wres,))

            def f_x2(e, oc=oc, vx1=vx1, wbk=wbk):
                return e.scalar_tensor_tensor(
                    t_x1g[:, oc*256:(oc+1)*256], bk(wbk, 256),
                    t_bout[:, oc:oc+1],
                    vx1[:, oc*256:(oc+1)*256], OP.add, OP.add)
            step("vector", [s_pe], f_x2, r=(wres, xres, "bout"), w=("x1g",))

        def f_x2sp(e, i=i):
            return e.dma_start(out=d_x2sp[:, :, i*256:(i+1)*256].rearrange("c p j -> p c j"),
                               in_=t_x1g[:].rearrange("p (c j) -> p c j", c=8))
        step("sync", [s_dve], f_x2sp, dma_sem=s_sp, r=("x1g",), w=("x2sp%d" % i,))

    nc.leave_named_scope("phaseB", _sidB, False)
    _sidC = nc.enter_named_scope("phaseC", False)[0]
    for dst_, srcs_ in (("ht", ("q", "p0", "p1", "a")),
                        ("xh", ("ewb2", "cs2"))):
        for s__ in srcs_:
            if s__ in writers:
                readers.setdefault(dst_, []).append(writers[s__])
            readers.setdefault(dst_, []).extend(readers.get(s__, ()))
    def f_wreload(e):
        r = []
        for x_ in range(NEXP):
            r.append(e.dma_start(
                out=t_w1[x_][:, 0:1024].rearrange("p (c j) -> p c j", c=2),
                in_=d_w1[x_][0:2].rearrange("c p j -> p c j")))
        for x_ in range(2):
            r.append(e.dma_start(
                out=t_w2[x_][:, 0:1024].rearrange("p (c j) -> p c j", c=1),
                in_=d_w2[x_][0:1].rearrange("c p j -> p c j")))
        return r
    step("gpsimd", [], f_wreload, dma_sem=s_dmag, ndma=8, w=("wsp",))

    for j in range(NQG if do_c else 0):
        t_xg = t_xga if j % 2 == 0 else t_xgb
        xgres = "xga" if j % 2 == 0 else "xgb"
        src_xg = mk_src(t_xg)

        def f_cload(e, j=j, t_xg=t_xg):
            return [e.dma_start(out=t_xg[:].rearrange("p (c j) -> p c j", c=8),
                                in_=d_x2sp[:, :, j*GRP:(j+1)*GRP].rearrange("c p j -> p c j"))]
        step("sync", COMPUTE + [s_sp], f_cload, dma_sem=s_dma, r=("x2sp%d" % j,), w=(xgres,))

        ln_block(src_xg, t_xh, True, xgres, "xh", src_t=t_xg)
        router_block(src_xg, xgres)
        ffn_block(mk_evac(t_xg), xgres)

        def f_store(e, j=j):
            return e.dma_start(out=d_out[:, :, j*GRP:(j+1)*GRP].rearrange("c p j -> p c j"),
                               in_=t_x1g[:].rearrange("p (c j) -> p c j", c=8))
        step("sync", [s_dve], f_store, dma_sem=s_sp, r=("x1g",), w=("d_out",))

    nc.leave_named_scope("phaseC", _sidC, False)
    flush_plan()
    _sp_total = sum(s.amt for s in all_steps if s.sem is s_sp)

    def f_final(e):
        e.wait_ge(s_sp, _sp_total)
    blk.sync(f_final)

    es.close()
    return nc



def kernel(**inputs):
    in_maps, metas = host_prepare(inputs)
    nc = build_program()
    res = run_bass_kernel_spmd(nc, in_maps, core_ids=list(range(8)))
    out = np.zeros((4, S, E), dtype=np.float32)
    for core in range(8):
        b, own = metas[core]
        o = res.results[core]["out"].reshape(E, OWN)
        for j, g in enumerate(own):
            out[b, g*128:(g+1)*128, :] = o[:, j*128:(j+1)*128].T
    return out


# version 19
# speedup vs baseline: 1.0052x; 1.0052x over previous
import numpy as np
import ml_dtypes
import concourse.bass as bass
import concourse.mybir as mybir
from concourse.bass_utils import run_bass_kernel_spmd

F32 = mybir.dt.float32
BF16 = mybir.dt.bfloat16
I32 = mybir.dt.int32
OP = mybir.AluOpType
ACT = mybir.ActivationFunctionType
AX = mybir.AxisListType.X
BF = ml_dtypes.bfloat16

E, S, NB = 1024, 4096, 32
H, KVH, D, QL, KVL = 16, 4, 64, 64, 32
NEXP, DEXP = 6, 512
EPS = 1e-5
GRP = 256
NGRP = S // GRP
NQG = 8
OWN = 2048
SCALE = 0.125
QTEMP_INV = 1.0 / 0.3
STEP_LIMIT = [10**9]
SCHED_DUMP = []
PRED = [0.0]


def own_blocks(half):
    keep = (0, 3) if half == 0 else (1, 2)
    return [g for g in range(NB) if g % 4 in keep]



def _swap64(Wcols):
    r = Wcols.reshape(Wcols.shape[0], -1, 32, 2)
    return np.ascontiguousarray(r[..., ::-1].reshape(Wcols.shape))


def _rope_rows():
    inv = 1.0 / (10000.0 ** (np.arange(0, D, 2, dtype=np.float64) / D))
    ang = np.outer(inv, np.arange(S, dtype=np.float64))
    cos = np.repeat(np.cos(ang), 2, axis=0)
    sin = np.repeat(np.sin(ang), 2, axis=0).copy()
    sin[0::2] *= -1.0
    C = np.concatenate([cos, cos], 0).astype(np.float32)
    Sg = np.concatenate([sin, sin], 0).astype(np.float32)
    return C, Sg


def host_prepare(inputs):
    x = np.asarray(inputs["L0_attn_x"], dtype=np.float32)
    Wqa = np.asarray(inputs["L0_attn_Wqa"], np.float32)
    Wqb = np.asarray(inputs["L0_attn_Wqb"], np.float32)
    Wkva = np.asarray(inputs["L0_attn_Wkva"], np.float32)
    Wkvb = np.asarray(inputs["L0_attn_Wkvb"], np.float32)
    Wout = np.asarray(inputs["L0_attn_Wout"], np.float32)
    bout = np.asarray(inputs["L0_attn_bout"], np.float32)
    projw = np.asarray(inputs["L0_ffn_proj_w"], np.float32)
    anch = np.asarray(inputs["L0_ffn_anchors"], np.float32)
    temp = np.asarray(inputs["L0_ffn_routing_temp"], np.float32).reshape(1, 1)
    w1 = np.asarray(inputs["L0_ffn_w1"], np.float32)
    w2 = np.asarray(inputs["L0_ffn_w2"], np.float32)

    sh = {}
    sh["w1r"] = np.ascontiguousarray(w1.reshape(NEXP, 8, 128, DEXP))
    sh["w2r"] = np.ascontiguousarray(w2.reshape(NEXP, 4, 128, E))
    sh["wqa"] = np.ascontiguousarray(Wqa.reshape(8, 128, QL))
    WqbP = np.empty_like(Wqb)
    for c in range(8):
        WqbP[:, c*128:c*128+64] = Wqb[:, c*64:(c+1)*64]
        WqbP[:, c*128+64:(c+1)*128] = Wqb[:, (c+8)*64:(c+9)*64]
    sh["wqb"] = np.ascontiguousarray(np.concatenate([WqbP, _swap64(WqbP)], 1))
    sh["wkva"] = np.ascontiguousarray(Wkva.reshape(8, 128, KVL))
    kvr = Wkvb.reshape(KVL, KVH, D, 2)
    WK = np.ascontiguousarray(kvr[..., 0].reshape(KVL, KVH * D))
    WV = np.ascontiguousarray(kvr[..., 1].reshape(KVL, KVH * D))
    WKs = _swap64(WK)
    sh["wkvbk"] = np.ascontiguousarray(np.concatenate(
        [WK[:, 0:64], WK[:, 128:192],
         WKs[:, 0:64], WKs[:, 128:192],
         WK[:, 64:128], WK[:, 192:256],
         WKs[:, 64:128], WKs[:, 192:256]], 1))
    sh["wkvbv"] = WV
    wop = np.empty_like(Wout)
    for j, oc in enumerate([0, 4, 1, 5, 2, 6, 3, 7]):
        wop[j*128:(j+1)*128] = Wout[oc*128:(oc+1)*128]
    sh["wout"] = np.ascontiguousarray(wop.reshape(8, 128, E))
    sh["bout"] = np.ascontiguousarray(bout.reshape(8, 128).T)
    sh["projw"] = np.ascontiguousarray(
        projw.T.reshape(8, 128, NEXP).transpose(1, 0, 2).reshape(128, 8 * NEXP))
    sh["anchT"] = np.ascontiguousarray(anch.T)
    sh["rtemp"] = temp
    sh["ident"] = np.eye(128, dtype=np.float32)

    C, Sg = _rope_rows()
    csk = np.zeros((128, 2 * S), dtype=np.float32)
    for g in range(NGRP):
        csk[:, g*512:g*512+256] = C[:, g*256:(g+1)*256]
        csk[:, g*512+256:(g+1)*512] = Sg[:, g*256:(g+1)*256]
    sh["csk"] = csk.astype(BF)

    in_maps, metas = [], []
    for core in range(8):
        b, half = core // 2, core % 2
        own = own_blocks(half)
        m = dict(sh)
        m["x"] = np.ascontiguousarray(x[b].T.reshape(8, 128, S))
        m["coreinfof"] = np.array([[half]], dtype=np.float32)
        csq = np.zeros((128, 2 * OWN), dtype=np.float32)
        for i in range(NQG):
            cols = list(range(own[2*i]*128, own[2*i]*128+128)) + \
                   list(range(own[2*i+1]*128, own[2*i+1]*128+128))
            csq[:, i*512:i*512+256] = C[:, cols]
            csq[:, i*512+256:(i+1)*512] = Sg[:, cols]
        m["csq"] = csq.astype(BF)
        mt = np.zeros((NQG * 4, 128, 256), dtype=np.float32)
        kk = np.arange(128)[:, None]
        qq = np.arange(128)[None, :]
        for i in range(NQG):
            g0, g1 = own[2*i], own[2*i+1]
            for j in range(4):
                kt = 4 * i + j
                for bi, g in ((0, g0), (1, g1)):
                    mt[4*i+j, :, bi*128:(bi+1)*128] = (kt*128 + kk <= g*128 + qq)
        m["maskt"] = mt.astype(BF)
        in_maps.append(m)
        metas.append((b, own))
    return in_maps, metas



class _Counts:
    def __init__(self):
        self.c = {}

    def inc(self, instr, semh, amt=1):
        instr.then_inc(semh, amt)
        self.c[semh.name] = self.c.get(semh.name, 0) + amt

    def val(self, semh):
        return self.c.get(semh.name, 0)


def _shift_ap(ap, delta):
    return bass.AP(ap.tensor, ap.offset + delta, ap.ap)


def build_program(taps=(), ngrp_a=NGRP, do_b=True, do_c=True, nqg=NQG, nc_only=False):
    nc = bass.Bass("TRN2", target_bir_lowering=False, dynamic_dma_scratch_size=16384)
    nt = _Counts()

    def dram(n, s, d, k="ExternalInput"):
        return nc.dram_tensor(n, s, d, kind=k).ap()

    d_x = dram("x", [8, 128, S], F32)
    d_w1 = dram("w1r", [NEXP, 8, 128, DEXP], F32)
    d_w2 = dram("w2r", [NEXP, 4, 128, E], F32)
    d_wqa = dram("wqa", [8, 128, QL], F32)
    d_wqb = dram("wqb", [QL, 2048], F32)
    d_wkva = dram("wkva", [8, 128, KVL], F32)
    d_wkvbk = dram("wkvbk", [KVL, 512], F32)
    d_wkvbv = dram("wkvbv", [KVL, 256], F32)
    d_wout = dram("wout", [8, 128, E], F32)
    d_bout = dram("bout", [128, 8], F32)
    d_projw = dram("projw", [128, 8 * NEXP], F32)
    d_anchT = dram("anchT", [NEXP, NEXP], F32)
    d_rtemp = dram("rtemp", [1, 1], F32)
    d_ident = dram("ident", [128, 128], F32)
    d_csk = dram("csk", [128, 2 * S], BF16)
    d_csq = dram("csq", [128, 2 * OWN], BF16)
    d_maskt = dram("maskt", [NQG * 4, 128, 256], BF16)
    d_cinfo = dram("coreinfof", [1, 1], F32)
    d_out = dram("out", [8, 128, OWN], F32, "ExternalOutput")

    spk = "ExternalOutput" if "spill" in taps else "Internal"
    d_x1own = dram("x1own", [8, 128, OWN], F32, spk)
    d_yown = dram("yown", [8, 128, OWN], BF16, spk)
    x2k = "ExternalOutput" if "x2" in taps else "Internal"
    d_x2sp = dram("x2sp", [8, 128, OWN], F32, x2k)
    if "kv" in taps:
        d_kadbg = dram("kadbg", [128, S], BF16, "ExternalOutput")
        d_kbdbg = dram("kbdbg", [128, S], BF16, "ExternalOutput")
        d_vdbg = dram("vdbg", [128, NB * 260], BF16, "ExternalOutput")
    if "ew" in taps:
        d_ewdbg = dram("ewdbg", [1, NGRP * 1536], F32, "ExternalOutput")
        d_zdbg = dram("zdbg", [NEXP, S], F32, "ExternalOutput")
        d_r6dbg = dram("r6dbg", [128, 16 * NGRP], F32, "ExternalOutput")
        d_ewtdbg = dram("ewtdbg", [128, 16 * NGRP], F32, "ExternalOutput")
        d_q6dbg = dram("q6dbg", [NEXP, S], F32, "ExternalOutput")
        d_rscdbg = dram("rscdbg", [128, 64 * NGRP], F32, "ExternalOutput")

    from contextlib import ExitStack
    es = ExitStack()

    def sbt(n, s, d):
        return es.enter_context(nc.sbuf_tensor(n, s, d))

    def mksem(n):
        return es.enter_context(nc.semaphore(n))

    t_w1 = [sbt(f"w1_{e_}", [128, 8 * DEXP], BF16) for e_ in range(NEXP)]
    t_w2 = [sbt(f"w2_{e_}", [128, 4 * E], BF16) for e_ in range(NEXP)]
    t_wqa = sbt("wqas", [128, 8 * QL], BF16)
    t_wqb = sbt("wqbs", [QL, 2048], BF16)
    t_wkva = sbt("wkvas", [128, 8 * KVL], BF16)
    t_wkvbk = sbt("wkvbks", [KVL, 512], BF16)
    t_wkvbv = sbt("wkvbvs", [KVL, 256], BF16)
    t_bout = sbt("bouts", [128, 8], F32)
    t_projw = sbt("projws", [128, 8 * NEXP], F32)
    t_anchT = sbt("anchTs", [NEXP, NEXP], F32)
    t_ident = sbt("idents", [128, 128], F32)
    t_ka = sbt("ka", [128, S], BF16)
    t_kb = sbt("kb", [128, S], BF16)
    t_v = sbt("v", [128, NB * 260], BF16)
    t_onesr = sbt("onesr", [128, 128], F32)
    t_onesm = sbt("onesm", [1, 128], F32)
    t_onescol = sbt("onescol", [128, 1], F32)
    t_eps = sbt("epsc", [128, 1], F32)
    t_onescb = sbt("onescb", [128, 1], BF16)
    t_pw = sbt("pwsum", [1, NEXP], F32)
    t_alpha = sbt("alpha", [128, 1], F32)
    t_alpham = sbt("alpham", [128, 1], F32)
    t_rt = sbt("rtemps", [1, 1], F32)
    t_ci = sbt("cis", [1, 1], F32)
    t_hf = sbt("hflag", [128, 1], F32)

    def wout_chunk(ec):
        t = t_w1[ec] if ec < NEXP else t_w2[ec - NEXP]
        return t[:, 0:E]

    t_xga = sbt("xga", [128, 8 * GRP], F32)
    t_xgb = sbt("xgb", [128, 8 * GRP], F32)
    t_xsqp = sbt("xsqp", [128, 512], BF16)
    t_lnbc = sbt("lnbc", [128, 512], BF16)
    t_xh = sbt("xh", [128, 8 * GRP], BF16)
    t_stat = sbt("stat", [1, 1536], F32)
    t_z = sbt("z", [NEXP, GRP], F32)
    t_q6 = sbt("q6", [NEXP, GRP], F32)
    t_r6 = sbt("r6", [128, 16], F32)
    t_rsc = sbt("rsc", [128, 64], F32)
    t_rs2 = sbt("rs2", [128, 16], F32)
    t_ew = sbt("ew", [128, 16], F32)
    t_ewf6 = sbt("ewf6", [1, NEXP * GRP], F32)
    t_ewb = sbt("ewb", [128, NEXP * GRP], BF16)
    t_h = sbt("h", [128, 4 * GRP], BF16)
    t_ht = sbt("ht", [128, NEXP * 4 * GRP], BF16)
    t_x1g = sbt("x1g", [128, 8 * GRP], F32)
    t_yh = sbt("yhg", [128, 8 * GRP], BF16)
    t_kva = sbt("kvag", [KVL, GRP], BF16)
    t_cs = sbt("cs", [128, 512], BF16)
    t_rtmp = sbt("rtmp", [128, 512], BF16)
    t_qa = sbt("qag", [QL, 256], BF16)

    t_xsq = t_ht[:, 0:4096].bitcast(F32)
    t_rc = t_stat[0:1, 0:1024]
    t_osp = t_ht[:, 0:2048].bitcast(F32)
    v_q = t_ht[:, 0:2048]
    v_ps = (t_ht[:, 2048:3072], t_ht[:, 3072:4096])
    v_a = t_ht[:, 4096:6144]
    v_rb = t_rtmp
    v_masks = t_ewb[:, 0:1024]
    v_x1b = t_xga
    v_yb = t_yh

    psA = es.enter_context(nc.psum_tensor("arena", [128, 4096], F32))

    def bk(i, w=512, p0=0, p1=128, off=0):
        return psA[p0:p1, i * 512 + off: i * 512 + off + w]

    s_dma = mksem("s_dma")
    s_dmag = mksem("s_dmag")
    s_sp = mksem("s_sp")
    s_pe = mksem("s_pe")
    s_act = mksem("s_act")
    s_dve = mksem("s_dve")
    s_fen = mksem("s_fen")
    fenc = [0]

    blk = es.enter_context(nc.Block())
    SEM_OF = {"tensor": s_pe, "scalar": s_act, "vector": s_dve}
    COMPUTE = [s_pe, s_act, s_dve]
    ALLS = [s_dma, s_dmag, s_sp, s_pe, s_act, s_dve]

    stepno = [0]
    writers = {}
    readers = {}
    SEMS = {s.name: s for s in ALLS}
    all_steps = []
    epoch = [0]

    class _Step:
        __slots__ = ("sid", "eng", "fn", "dma_sem", "ndma", "deps", "est",
                     "epoch", "sem", "amt", "cnt", "start", "finish")

        def __init__(s, sid, eng, fn, dma_sem, ndma, deps, est, ep):
            s.sid, s.eng, s.fn, s.dma_sem, s.ndma = sid, eng, fn, dma_sem, ndma
            s.deps, s.est, s.epoch = deps, est, ep
            s.sem = dma_sem if dma_sem is not None else SEM_OF[eng]
            s.amt = 16 * ndma if dma_sem is not None else 1

    EST = {"tensor": 1500, "scalar": 1200, "vector": 800, "sync": 4000,
           "gpsimd": 4000}
    import os
    SCHED_MODE = [os.environ.get("KSCHED", "list")]
    last_on_sem = {}
    bar_deps = [{}]

    def step(eng, deps_unused, fn, dma_sem=None, ndma=1, r=(), w=(), est=None):
        stepno[0] += 1
        if stepno[0] > STEP_LIMIT[0]:
            return
        annotated = bool(r) or bool(w)
        deps = set(bar_deps[0].values())
        if annotated:
            for res in r:
                if res in writers:
                    deps.add(writers[res])
            for res in w:
                if res in writers:
                    deps.add(writers[res])
                deps.update(readers.get(res, ()))
        else:
            deps.update(last_on_sem.values())
        st = _Step(stepno[0], eng, fn, dma_sem, ndma, deps,
                   est if est is not None else EST[eng], epoch[0])
        all_steps.append(st)
        last_on_sem[st.sem.name] = st
        if annotated:
            for res in r:
                readers.setdefault(res, []).append(st)
            for res in w:
                writers[res] = st
                readers[res] = []
        else:
            for res in set(list(writers) + list(readers)):
                writers[res] = st
                readers[res] = []
        return st

    def barrier():
        epoch[0] += 1
        bar_deps[0] = dict(last_on_sem)

    def flush_plan():
        import heapq
        order = {e: [] for e in plan_engines}
        eng_free = {e: 0.0 for e in plan_engines}
        maxep = max((s.epoch for s in all_steps), default=0)
        ep_end = 0.0
        mode = SCHED_MODE[0]
        for ep in range(maxep + 1):
            steps_ = [s for s in all_steps if s.epoch == ep]
            if not steps_:
                continue
            indeg = {}
            for s in steps_:
                indeg[s] = sum(1 for d in s.deps if d.epoch == ep)
            succs = {s: [] for s in steps_}
            for s in steps_:
                for d in s.deps:
                    if d.epoch == ep:
                        succs[d].append(s)
            lp = {}
            for s in sorted(steps_, key=lambda t: -t.sid):
                lp[s] = s.est + max((lp[t] for t in succs[s]), default=0.0)

            def key(s, rt):
                if mode == 'fifo':
                    return s.sid
                if mode == 'cp':
                    return (rt, -lp[s])
                if mode == 'cpb':
                    return (int(rt // 2000), -lp[s])
                return rt
            ready = []
            for s in steps_:
                if indeg[s] == 0:
                    rt = max([ep_end] + [d.finish for d in s.deps if d.epoch < ep])
                    heapq.heappush(ready, (key(s, rt), s.sid, s))
            scheduled = 0
            while ready:
                _, _, s = heapq.heappop(ready)
                rt = max([ep_end] + [d.finish for d in s.deps if d.epoch == ep] +
                         [d.finish for d in s.deps if d.epoch < ep])
                start = max(rt, eng_free[s.eng])
                s.start = start
                s.finish = start + s.est
                eng_free[s.eng] = s.finish
                order[s.eng].append(s)
                scheduled += 1
                for t in succs[s]:
                    indeg[t] -= 1
                    if indeg[t] == 0:
                        rt2 = max([ep_end] + [d.finish for d in t.deps])
                        heapq.heappush(ready, (key(t, rt2), t.sid, t))
            assert scheduled == len(steps_), (scheduled, len(steps_))
            ep_end = max(eng_free.values())
        PRED[:] = [ep_end]
        sem_eng = {}
        for e in plan_engines:
            c = {}
            for s in order[e]:
                assert sem_eng.setdefault(s.sem.name, e) == e, s.sem.name
                c[s.sem.name] = c.get(s.sem.name, 0) + s.amt
                s.cnt = c[s.sem.name]
        SCHED_DUMP.clear()
        for e in plan_engines:
            for pos, s in enumerate(order[e]):
                SCHED_DUMP.append((e, pos, s.sid, s.sem.name, s.cnt,
                                   sorted((d.eng, d.sid, d.sem.name, d.cnt)
                                          for d in s.deps)))
        for e in plan_engines:
            if not order[e]:
                continue

            def body(eng_h, steps_=order[e]):
                hw = {}
                for s in steps_:
                    dd = {}
                    for d in s.deps:
                        sn = d.sem.name
                        if dd.get(sn, 0) < d.cnt:
                            dd[sn] = d.cnt
                    for sn, c in sorted(dd.items()):
                        if hw.get(sn, 0) < c:
                            eng_h.wait_ge(SEMS[sn], c)
                            hw[sn] = c
                    rr = s.fn(eng_h)
                    if s.dma_sem is not None:
                        lst = rr if isinstance(rr, (list, tuple)) else [rr]
                        assert len(lst) == s.ndma, (len(lst), s.ndma)
                        for ins in lst:
                            ins.then_inc(s.dma_sem, 16)
                    else:
                        rr.then_inc(s.sem, 1)
            getattr(blk, e)(body)

    plan_engines = ("sync", "gpsimd", "tensor", "scalar", "vector")


    step("sync", [], lambda e: [
        e.dma_start(out=t_ident[:], in_=d_ident),
        e.dma_start(out=t_anchT[:], in_=d_anchT),
        e.dma_start(out=t_bout[:], in_=d_bout),
        e.dma_start(out=t_projw[:], in_=d_projw),
        e.dma_start(out=t_rt[:], in_=d_rtemp),
        e.dma_start(out=t_ci[:], in_=d_cinfo),
    ], dma_sem=s_dma, ndma=6, w=("ident", "anchT", "bout", "projw", "smallw"))

    def f_init_gp(e):
        r = []
        for x_ in range(NEXP):
            r.append(e.dma_start(out=t_w1[x_][:].rearrange("p (c j) -> p c j", c=8), in_=d_w1[x_].rearrange("c p j -> p c j")))
            r.append(e.dma_start(out=t_w2[x_][:].rearrange("p (c j) -> p c j", c=4), in_=d_w2[x_].rearrange("c p j -> p c j")))
        r.append(e.dma_start(out=t_wqa[:].rearrange("p (c j) -> p c j", c=8), in_=d_wqa.rearrange("c p j -> p c j")))
        r.append(e.dma_start(out=t_wqb[:], in_=d_wqb))
        r.append(e.dma_start(out=t_wkva[:].rearrange("p (c j) -> p c j", c=8), in_=d_wkva.rearrange("c p j -> p c j")))
        r.append(e.dma_start(out=t_wkvbk[:], in_=d_wkvbk))
        r.append(e.dma_start(out=t_wkvbv[:], in_=d_wkvbv))
        return r
    step("gpsimd", [], f_init_gp, dma_sem=s_dmag, ndma=17, w=("wsp",))

    def f_init_dve(e):
        e.memset(t_onesr[:], 1.0)
        e.memset(t_onesm[:], -1.0)
        e.memset(t_onescol[:], 1.0)
        e.memset(t_eps[:], EPS)
        e.memset(t_onescb[:], 1.0)
        vones = t_v[:].rearrange("p (t k j) -> p t k j", t=NB, k=4)[:, :, :, 64:65]
        return e.memset(vones, 1.0)
    step("vector", [], f_init_dve, w=("consts", "v"))

    def f_init_pe(e):
        for c in range(8):
            e.matmul(bk(0, NEXP, 0, 1), t_onescol[:, 0:1],
                     t_projw[:, c*NEXP:(c+1)*NEXP], start=(c == 0), stop=(c == 7))
        e.matmul(bk(1, 1), t_onesr[0:1, 0:128], t_rt[0:1, 0:1],
                 start=True, stop=True)
        return e.matmul(bk(1, 1, off=1), t_onesr[0:1, 0:128], t_ci[0:1, 0:1],
                        start=True, stop=True)
    step("tensor", [s_dma, s_dmag], f_init_pe, r=("projw", "consts", "smallw"), w=("pb0", "pb1"))

    def f_init_dve2(e):
        e.tensor_copy(t_pw[0:1, 0:NEXP], bk(0, NEXP, 0, 1))
        e.tensor_copy(t_hf[:, 0:1], bk(1, 1, off=1))
        e.tensor_scalar_max(t_rsc[:, 0:1], bk(1, 1), 0.1)
        e.drain()
        e.reciprocal(t_rsc[:, 1:2], t_rsc[:, 0:1])
        e.drain()
        e.tensor_scalar_mul(t_alpha[:, 0:1], t_rsc[:, 1:2], 0.5)
        e.drain()
        return e.tensor_scalar_mul(t_alpham[:, 0:1], t_alpha[:, 0:1], -1.0)
    step("vector", [s_pe], f_init_dve2, r=("pb0", "pb1"), w=("pw", "hf", "alpha", "rsc"))

    def ln_block(src_c, dst, router_extras, srcres, dstres, src_t=None):
        n = GRP

        def f_sums(e):
            r = None
            for c in range(8):
                r = e.matmul(bk(0, n, 0, 1), t_onescol[:, 0:1], src_c(c),
                             start=(c == 0), stop=(c == 7))
            return r
        step("tensor", [s_dma, s_dve], f_sums, r=(srcres, "consts"), w=("pb0",))
        for c in range(8):
            pp = c % 2
            step("scalar", [], lambda e, c=c, pp=pp: e.activation(
                t_xsqp[:, pp*n:(pp+1)*n], src_c(c), ACT.Square),
                 r=(srcres,), w=("xsqp%d" % pp,))
            step("tensor", [], lambda e, c=c, pp=pp: e.matmul(
                bk(0, n, 0, 1, off=n), t_onescb[:, 0:1], t_xsqp[:, pp*n:(pp+1)*n],
                start=(c == 0), stop=(c == 7)),
                 r=("xsqp%d" % pp, "consts"), w=("pb0",))

        def f_mv(e):
            e.tensor_scalar_mul(t_stat[0:1, 0:n], bk(0, n, 0, 1), 1.0 / E)
            e.drain()
            e.tensor_mul(t_stat[0:1, 256:256+n], t_stat[0:1, 0:n], t_stat[0:1, 0:n])
            e.drain()
            return e.scalar_tensor_tensor(
                t_stat[0:1, 512:512+n], bk(0, n, 0, 1, off=n), 1.0 / E,
                t_stat[0:1, 256:256+n], OP.mult, OP.subtract)
        step("vector", [s_pe], f_mv, r=("pb0",), w=("stat",))

        def f_rstd(e):
            e.activation(t_stat[0:1, 256:256+n], t_stat[0:1, 512:512+n], ACT.Ln,
                         bias=t_eps[0:1, 0:1])
            e.drain()
            r = e.activation(t_stat[0:1, 768:768+n], t_stat[0:1, 256:256+n],
                             ACT.Exp, scale=-0.5)
            if router_extras:
                r = e.activation(t_stat[0:1, 1280:1280+n], t_stat[0:1, 0:n],
                                 ACT.Copy, scale=-1.0)
            return r
        step("scalar", [s_dve], f_rstd, r=("stat", "consts"), w=("stat",))

        def f_chain3(e):
            return e.tensor_mul(t_stat[0:1, 1024:1024+n], t_stat[0:1, 0:n],
                                t_stat[0:1, 768:768+n])
        step("vector", [s_act], f_chain3, r=("stat",), w=("stat",))

        def f_bc(e):
            e.matmul(bk(1, n), t_onesr[0:1, 0:128], t_stat[0:1, 768:768+n],
                     start=True, stop=True)
            return e.matmul(bk(1, n, off=256), t_onesm[0:1, 0:128],
                            t_stat[0:1, 1024:1024+n], start=True, stop=True)
        step("tensor", [s_dve], f_bc, r=("stat", "consts"), w=("pb1",))

        def f_apply(e):
            e.tensor_copy(t_lnbc[:, 0:512], bk(1, 512))
            e.drain()
            if src_t is not None:
                s3 = src_t[:, 0:8*n].rearrange("p (c j) -> p c j", c=8)
                d3 = dst[:, 0:8*n].rearrange("p (c j) -> p c j", c=8)
                rb3 = t_lnbc[:, 0:n].unsqueeze(1).to_broadcast([128, 8, n])
                mb3 = t_lnbc[:, 256:256+n].unsqueeze(1).to_broadcast([128, 8, n])
                e.tensor_tensor(d3, s3, rb3, OP.mult)
                e.drain()
                return e.tensor_tensor(d3, d3, mb3, OP.add)
            def mul(c):
                e.tensor_tensor(t_rtmp[:, (c % 2)*n:(c % 2)*n+n], src_c(c),
                                t_lnbc[:, 0:n], OP.mult)
            def add(c):
                return e.tensor_tensor(dst[:, c*n:(c+1)*n],
                                       t_rtmp[:, (c % 2)*n:(c % 2)*n+n],
                                       t_lnbc[:, 256:256+n], OP.add)
            mul(0)
            for c in range(7):
                mul(c + 1)
                add(c)
            return add(7)
        step("vector", [s_pe], f_apply, r=(srcres, "pb1"), w=("lnbc", dstres, "rtmp"))

    def router_block(src_c, srcres):
        n = GRP

        def f_z(e):
            for c in range(8):
                e.matmul(bk(2, n, 0, NEXP), t_projw[:, c*NEXP:(c+1)*NEXP], src_c(c),
                         start=(c == 0), stop=False)
            return e.matmul(bk(2, n, 0, NEXP), t_pw[0:1, 0:NEXP],
                            t_stat[0:1, 1280:1280+n], start=False, stop=True)
        step("tensor", [s_dve], f_z, r=(srcres, "projw", "pw", "stat"), w=("pb2",))

        def f_zev(e):
            e.tensor_copy(t_q6[0:NEXP, 0:n], bk(2, n, 0, NEXP))
            e.drain()
            return e.tensor_tensor(t_z[0:NEXP, 0:n], t_q6[0:NEXP, 0:n],
                                   bk(1, n, 0, NEXP), OP.mult)
        step("vector", [s_pe], f_zev, r=("pb2", "pb1"), w=("q6", "z"))

        step("scalar", [s_dve], lambda e: e.activation(
            t_q6[0:NEXP, 0:n], t_z[0:NEXP, 0:n], ACT.Exp, scale=2.0 * QTEMP_INV),
             r=("z",), w=("q6",))

        def f_tanh2(e):
            e.tensor_scalar_add(t_z[0:NEXP, 0:n], t_q6[0:NEXP, 0:n], 1.0)
            e.drain()
            e.reciprocal(t_q6[0:NEXP, 0:n], t_z[0:NEXP, 0:n])
            e.drain()
            return e.tensor_scalar(t_q6[0:NEXP, 0:n], t_q6[0:NEXP, 0:n], -2.0, 1.0,
                                   OP.mult, OP.add)
        step("vector", [s_act], f_tanh2, r=("q6", "z"), w=("q6", "z"))

        def f_dot(e):
            r = None
            for b2 in range(2):
                r = e.matmul(bk(2, NEXP, 0, 128, off=256 + b2*NEXP),
                             t_q6[0:NEXP, b2*128:(b2+1)*128], t_anchT[0:NEXP, 0:NEXP],
                             start=True, stop=True)
            return r
        step("tensor", [s_act], f_dot, r=("q6", "anchT"), w=("pb2",))

        def f_tok(e):
            R = lambda b2: t_r6[:, b2*8:b2*8+NEXP]
            for b2 in range(2):
                e.tensor_copy(R(b2), bk(2, NEXP, 0, 128, off=256 + b2*NEXP))
            for b2 in range(2):
                e.tensor_reduce(t_rs2[:, b2*8+0:b2*8+1], R(b2), AX, OP.max)
            for b2 in range(2):
                e.tensor_scalar(t_rsc[:, b2*32+8:b2*32+8+NEXP], R(b2),
                                t_rs2[:, b2*8+0:b2*8+1], 0.0, OP.subtract, OP.is_equal)
            for b2 in range(2):
                e.scalar_tensor_tensor(t_rsc[:, b2*32+16:b2*32+16+NEXP],
                                       t_rsc[:, b2*32+8:b2*32+8+NEXP], -1e30, R(b2),
                                       OP.mult, OP.add)
            for b2 in range(2):
                e.tensor_reduce(t_rs2[:, b2*8+1:b2*8+2],
                                t_rsc[:, b2*32+16:b2*32+16+NEXP], AX, OP.max)
            for b2 in range(2):
                rr = e.tensor_tensor(t_rs2[:, b2*8+2:b2*8+3], t_rs2[:, b2*8+0:b2*8+1],
                                     t_rs2[:, b2*8+1:b2*8+2], OP.subtract)
            return rr
        step("vector", [s_pe], f_tok, r=("pb2",), w=("r6", "rsc", "rs2"))

        def f_sig(e):
            for b2 in range(2):
                r = e.activation(t_rs2[:, b2*8+5:b2*8+6], t_rs2[:, b2*8+2:b2*8+3],
                                 ACT.Exp, scale=t_alpham[:, 0:1])
            return r
        step("scalar", [s_dve], f_sig, r=("rs2", "alpha"), w=("rs2",))

        def f_ew(e):
            for b2 in range(2):
                e.tensor_scalar_add(t_rsc[:, b2*32:b2*32+1], t_rs2[:, b2*8+5:b2*8+6], 1.0)
            for b2 in range(2):
                e.reciprocal(t_rs2[:, b2*8+3:b2*8+4], t_rsc[:, b2*32:b2*32+1])
            for b2 in range(2):
                e.tensor_scalar(t_rs2[:, b2*8+4:b2*8+5], t_rs2[:, b2*8+3:b2*8+4],
                                -1.0, 1.0, OP.mult, OP.add)
            for b2 in range(2):
                e.tensor_scalar(t_rsc[:, b2*32+24:b2*32+24+NEXP],
                                t_rsc[:, b2*32+16:b2*32+16+NEXP],
                                t_rs2[:, b2*8+1:b2*8+2], 0.0, OP.subtract, OP.is_equal)
            for b2 in range(2):
                e.tensor_scalar(t_rsc[:, b2*32+16:b2*32+16+NEXP],
                                t_rsc[:, b2*32+24:b2*32+24+NEXP],
                                t_rs2[:, b2*8+4:b2*8+5], None, OP.mult)
            for b2 in range(2):
                r = e.scalar_tensor_tensor(t_ew[:, b2*8:b2*8+NEXP],
                                           t_rsc[:, b2*32+8:b2*32+8+NEXP],
                                           t_rs2[:, b2*8+3:b2*8+4],
                                           t_rsc[:, b2*32+16:b2*32+16+NEXP],
                                           OP.mult, OP.add)
            return r
        step("vector", [s_act], f_ew, r=("rsc", "rs2"), w=("rsc", "rs2", "ew"))

        def f_ewT(e):
            r = None
            for x_ in range(NEXP):
                for b2 in range(2):
                    r = e.transpose(psA[0:1, 1024 + x_*256 + b2*128:
                                        1024 + x_*256 + b2*128 + 128],
                                    t_ew[:, b2*8 + x_: b2*8 + x_ + 1], t_ident[:])
            return r
        step("tensor", [s_dve], f_ewT, r=("ew", "ident"), w=("pb2", "pb3", "pb4"))

        def f_ewf(e):
            r = None
            for x_ in range(NEXP):
                for b2 in range(2):
                    o = x_*256 + b2*128
                    r = e.tensor_copy(t_ewf6[0:1, o:o+128],
                                      psA[0:1, 1024 + o: 1024 + o + 128])
            return r
        step("vector", [s_pe], f_ewf, r=("pb2", "pb3", "pb4"), w=("ewf",))

    def ffn_block(evac, xgres):
        n = GRP
        for x0 in (0, 2, 4):
            eb = 2 + (x0 // 2) % 2
            def f_ewbc(e, x0=x0, eb=eb):
                e.matmul(bk(eb, n), t_onesr[0:1, 0:128], t_ewf6[0:1, x0*n:(x0+1)*n],
                         start=True, stop=True)
                return e.matmul(bk(eb, n, off=256), t_onesr[0:1, 0:128],
                                t_ewf6[0:1, (x0+1)*n:(x0+2)*n], start=True, stop=True)
            step("tensor", [s_dve], f_ewbc, r=("ewf", "consts"), w=("pb%d" % eb,))

            def f_ewbev(e, x0=x0, eb=eb):
                e.tensor_copy(t_ewb[:, x0*n:(x0+1)*n], bk(eb, n))
                return e.tensor_copy(t_ewb[:, (x0+1)*n:(x0+2)*n], bk(eb, n, off=256))
            step("vector", [s_pe], f_ewbev, r=("pb%d" % eb,), w=("ewb",))

        for x_ in range(NEXP):
            hb = 4 if x_ % 2 == 0 else 6

            def f_w1(e, x_=x_, hb=hb):
                r = None
                for m_ in range(4):
                    for c in range(8):
                        r = e.matmul(psA[0:128, hb*512 + m_*n: hb*512 + (m_+1)*n],
                                     t_w1[x_][:, c*512 + m_*128: c*512 + (m_+1)*128],
                                     t_xh[:, c*n:(c+1)*n], start=(c == 0), stop=(c == 7))
                return r
            step("tensor", [s_dve, s_act], f_w1, r=("xh", "wsp"),
                 w=(("pb4", "pb5") if hb == 4 else ("pb6", "pb7")))

            def f_gelu(e, x_=x_, hb=hb):
                return e.activation(t_h[:, 0:4*n], psA[0:128, hb*512: hb*512 + 4*n],
                                    ACT.Gelu_apprx_tanh)
            step("scalar", [s_pe], f_gelu,
                 r=(("pb4", "pb5") if hb == 4 else ("pb6", "pb7")), w=("h",))

            def f_ht(e, x_=x_):
                base = t_ewb[:, x_*n:(x_+1)*n]
                bc = base.unsqueeze(1).to_broadcast([128, 4, n])
                hh = t_h[:, 0:4*n].rearrange("p (m j) -> p m j", m=4)
                dd = t_ht[:, x_*4*n:(x_+1)*4*n].rearrange("p (m j) -> p m j", m=4)
                return e.tensor_tensor(dd, hh, bc, OP.mult)
            step("vector", [s_act], f_ht, r=("h", "ewb"), w=("ht",))

        for oc in range(8):
            ob = oc % 2
            def f_w2(e, oc=oc, ob=ob):
                first = True
                r = None
                for x_ in range(NEXP):
                    for k_ in range(4):
                        r = e.matmul(bk(ob, n), t_w2[x_][:, k_*E + oc*128: k_*E + (oc+1)*128],
                                     t_ht[:, (x_*4 + k_)*n:(x_*4 + k_ + 1)*n],
                                     start=first, stop=(x_ == NEXP-1 and k_ == 3))
                        first = False
                return r
            step("tensor", [s_dve], f_w2, r=("ht", "wsp"), w=("pb%d" % ob,))
            step("vector", [s_pe], lambda e, oc=oc, ob=ob: evac(e, oc, ob),
                 r=("pb%d" % ob, xgres), w=("x1g",))

    _sidA = nc.enter_named_scope("phaseA", False)[0]
    src_x1g = lambda c: t_x1g[:, c*GRP:(c+1)*GRP]

    def mk_src(t):
        return lambda c: t[:, c*GRP:(c+1)*GRP]

    def mk_evac(t):
        def ev(e, oc, ob):
            return e.scalar_tensor_tensor(t_x1g[:, oc*GRP:(oc+1)*GRP], bk(ob, GRP),
                                          0.5, t[:, oc*GRP:(oc+1)*GRP],
                                          OP.mult, OP.add)
        return ev

    for g in range(ngrp_a):
        t_xg = t_xga if g % 2 == 0 else t_xgb
        xgres = "xga" if g % 2 == 0 else "xgb"
        src_xg = mk_src(t_xg)

        def f_load(e, g=g, t_xg=t_xg):
            return [
                e.dma_start(out=t_xg[:].rearrange("p (c j) -> p c j", c=8),
                            in_=d_x[:, :, g*GRP:(g+1)*GRP].rearrange("c p j -> p c j")),
                e.dma_start(out=t_cs[:], in_=d_csk[:, g*512:(g+1)*512]),
            ]
        step("sync", COMPUTE, f_load, dma_sem=s_dma, ndma=2, w=(xgres, "cs"))

        ln_block(src_xg, t_xh, True, xgres, "xh", src_t=t_xg)
        router_block(src_xg, xgres)
        ffn_block(mk_evac(t_xg), xgres)

        s0 = (g % 2) * 128
        s1 = 128 - s0
        def f_blendx(e, g=g, s0=s0, s1=s1):
            def sub(c):
                e.tensor_tensor(t_osp[:, c*128:(c+1)*128], t_x1g[:, c*GRP+s1:c*GRP+s1+128],
                                t_x1g[:, c*GRP+s0:c*GRP+s0+128], OP.subtract)
            def stt(c):
                return e.scalar_tensor_tensor(t_osp[:, c*128:(c+1)*128],
                                              t_osp[:, c*128:(c+1)*128], t_hf[:, 0:1],
                                              t_x1g[:, c*GRP+s0:c*GRP+s0+128],
                                              OP.mult, OP.add)
            sub(0)
            for c in range(7):
                sub(c + 1)
                stt(c)
            return stt(7)
        step("vector", [s_dve], f_blendx, r=("x1g", "hf"), w=("ht",))

        def f_spill_x1(e, g=g):
            return e.dma_start(out=d_x1own[:, :, g*128:(g+1)*128].rearrange("c p j -> p c j"),
                               in_=t_osp[:].rearrange("p (c j) -> p c j", c=8))
        step("sync", [s_dve], f_spill_x1, dma_sem=s_sp, r=("ht",), w=("d_x1own",))

        if "ew" in taps:
            def f_ewtap(e, g=g):
                return [
                    e.dma_start(out=d_ewdbg[0:1, g*1536:(g+1)*1536],
                                in_=t_ewf6[0:1, 0:1536]),
                    e.dma_start(out=d_zdbg[:, g*GRP:(g+1)*GRP], in_=t_z[0:NEXP, 0:GRP]),
                    e.dma_start(out=d_q6dbg[:, g*GRP:(g+1)*GRP], in_=t_q6[0:NEXP, 0:GRP]),
                    e.dma_start(out=d_r6dbg[:, g*16:(g+1)*16], in_=t_r6[:, 0:16]),
                    e.dma_start(out=d_ewtdbg[:, g*16:(g+1)*16], in_=t_ew[:, 0:16]),
                    e.dma_start(out=d_rscdbg[:, g*64:g*64+16], in_=t_rs2[:, 0:16]),
                    e.dma_start(out=d_rscdbg[:, g*64+16:(g+1)*64], in_=t_rsc[:, 0:48]),
                ]
            step("sync", [s_dve], f_ewtap, dma_sem=s_sp, ndma=7)

        ln_block(src_x1g, t_yh, False, "x1g", "yh", src_t=t_x1g)

        def f_blendy(e, g=g, s0=s0, s1=s1):
            yb = t_osp[:, 0:512].bitcast(BF16)
            def sub(c):
                e.tensor_tensor(yb[:, c*128:(c+1)*128], t_yh[:, c*GRP+s1:c*GRP+s1+128],
                                t_yh[:, c*GRP+s0:c*GRP+s0+128], OP.subtract)
            def stt(c):
                return e.scalar_tensor_tensor(yb[:, c*128:(c+1)*128],
                                              yb[:, c*128:(c+1)*128], t_hf[:, 0:1],
                                              t_yh[:, c*GRP+s0:c*GRP+s0+128],
                                              OP.mult, OP.add)
            sub(0)
            for c in range(7):
                sub(c + 1)
                stt(c)
            return stt(7)
        step("vector", [s_dve], f_blendy, r=("yh", "hf"), w=("ht",))

        def f_spill_y(e, g=g):
            yb = t_osp[:, 0:512].bitcast(BF16)
            return e.dma_start(out=d_yown[:, :, g*128:(g+1)*128].rearrange("c p j -> p c j"),
                               in_=yb.rearrange("p (c j) -> p c j", c=8))
        step("sync", [s_dve], f_spill_y, dma_sem=s_sp, r=("ht",), w=("d_yown",))

        def f_kva(e):
            r = None
            for c in range(8):
                r = e.matmul(bk(2, GRP, 0, KVL), t_wkva[:, c*KVL:(c+1)*KVL],
                             t_yh[:, c*GRP:(c+1)*GRP], start=(c == 0), stop=(c == 7))
            return r
        step("tensor", [s_dve], f_kva, r=("yh", "wsp"), w=("pb2",))
        step("vector", [s_pe], lambda e: e.tensor_copy(
            t_kva[0:KVL, 0:GRP], bk(2, GRP, 0, KVL)), r=("pb2",), w=("kva",))

        for base, dst, kres in ((0, t_ka, "ka"), (256, t_kb, "kb")):
            def f_k(e, base=base):
                e.matmul(bk(3, 256), t_wkvbk[:, base:base+128], t_kva[0:KVL, 0:GRP],
                         start=True, stop=True)
                return e.matmul(bk(3, 256, off=256), t_wkvbk[:, base+128:base+256],
                                t_kva[0:KVL, 0:GRP], start=True, stop=True)
            step("tensor", [s_dve], f_k, r=("wsp", "kva"), w=("pb3",))

            def f_krope(e, dst=dst, g=g):
                e.tensor_tensor(t_rtmp[:, 0:512], bk(3, 512), t_cs[:, 0:512], OP.mult)
                e.drain()
                return e.tensor_tensor(dst[:, g*GRP:(g+1)*GRP], t_rtmp[:, 0:256],
                                       t_rtmp[:, 256:512], OP.add)
            step("vector", [s_pe], f_krope, r=("pb3", "cs"), w=("rtmp", kres))

        for tl in range(2):
            def f_v(e, tl=tl):
                return e.matmul(bk(2, 256, off=256), t_kva[0:KVL, tl*128:(tl+1)*128],
                                t_wkvbv[0:KVL, 0:256], start=True, stop=True)
            step("tensor", [s_dve], f_v, r=("kva", "wsp"), w=("pb2",))

            def f_vev(e, tl=tl, g=g):
                dstv = t_v[:].rearrange("p (t k j) -> p t k j", t=NB, k=4,
                                        j=65)[:, 2*g+tl, :, 0:64]
                return e.tensor_copy(dstv, bk(2, 256, off=256).rearrange(
                    "p (k j) -> p k j", k=4))
            step("vector", [s_pe], f_vev, r=("pb2",), w=("v",))

    if "kv" in taps:
        step("sync", [s_dve], lambda e: [
            e.dma_start(out=d_kadbg, in_=t_ka[:]),
            e.dma_start(out=d_kbdbg, in_=t_kb[:]),
            e.dma_start(out=d_vdbg, in_=t_v[:]),
        ], dma_sem=s_sp, ndma=3)

    nc.leave_named_scope("phaseA", _sidA, False)
    _sidB = nc.enter_named_scope("phaseB", False)[0]
    barrier()
    for nm_ in ("q", "p0", "p1", "a"):
        if "ht" in writers:
            writers[nm_] = writers["ht"]
        readers[nm_] = list(readers.get("ht", ()))
    def f_woutload(e):
        return [e.dma_start(out=wout_chunk(ec), in_=d_wout[ec])
                for ec in range(8)]
    step("gpsimd", [], f_woutload, dma_sem=s_dmag, ndma=8, w=("wsp",))

    if not do_b:
        nqg = 0
    step("sync", COMPUTE + [s_dma], lambda e: [
        e.dma_start(out=t_cs[:], in_=d_csq[:, 0:512])], dma_sem=s_dma, w=("cs",))

    for i in range(nqg):
        p_ = i % 2
        vx1 = t_xga if p_ == 0 else t_xgb
        xres = "xga" if p_ == 0 else "xgb"
        vmk = t_ewb[:, 0:1024] if p_ == 0 else t_xh[:, 0:1024]
        mres = "ewb" if p_ == 0 else "ewb2"
        vcs = t_cs if p_ == 0 else t_xh[:, 1024:1536]
        cres = "cs" if p_ == 0 else "cs2"

        def f_bload(e, i=i, vx1=vx1, vmk=vmk, vcs=vcs):
            r = [
                e.dma_start(
                    out=vx1[:].rearrange("p (c j) -> p c j", c=8),
                    in_=d_x1own[:, :, i*256:(i+1)*256].rearrange("c p j -> p c j")),
                e.dma_start(
                    out=v_yb[:].rearrange("p (c j) -> p c j", c=8),
                    in_=d_yown[:, :, i*256:(i+1)*256].rearrange("c p j -> p c j")),
                e.dma_start(
                    out=vmk.rearrange("p (t j) -> p t j", t=4),
                    in_=d_maskt[4*i:4*i+4].rearrange("t p j -> p t j")),
            ]
            if i > 0:
                r.append(e.dma_start(out=vcs[:, 0:512], in_=d_csq[:, i*512:(i+1)*512]))
            return r
        step("sync", COMPUTE, f_bload, dma_sem=s_dma, est=8000,
             ndma=4 if i > 0 else 3, r=("d_x1own", "d_yown"),
             w=(xres, "yh", mres) + ((cres,) if i > 0 else ()))

        def f_qa(e):
            r = None
            for c in range(8):
                r = e.matmul(bk(0, 256, 0, QL), t_wqa[:, c*QL:(c+1)*QL],
                             v_yb[:, c*256:(c+1)*256], start=(c == 0), stop=(c == 7))
            return r
        step("tensor", [s_dma], f_qa, r=("yh", "wsp"), w=("pb0",))
        step("vector", [s_pe], lambda e: e.tensor_copy(t_qa[0:QL, 0:256],
                                                       bk(0, 256, 0, QL)),
             r=("pb0",), w=("qa",))

        for c in range(8):
            def f_qc(e, c=c):
                e.matmul(bk(1, 256), t_wqb[0:QL, c*128:(c+1)*128], t_qa[0:QL, 0:256],
                         start=True, stop=True)
                return e.matmul(bk(1, 256, off=256),
                                t_wqb[0:QL, 1024 + c*128: 1024 + (c+1)*128],
                                t_qa[0:QL, 0:256], start=True, stop=True)
            step("tensor", [s_dve], f_qc, r=("wsp", "qa"), w=("pb1",))

            def f_qrope(e, c=c, vcs=vcs):
                e.tensor_tensor(t_rtmp[:, 0:512], bk(1, 512), vcs[:, 0:512], OP.mult)
                e.drain()
                return e.tensor_tensor(v_q[:, c*256:(c+1)*256], t_rtmp[:, 0:256],
                                       t_rtmp[:, 256:512], OP.add)
            step("vector", [s_pe], f_qrope, r=("pb1", cres), w=("rtmp", "q"))

        ktmax = 4 * i + 4
        for cp in range(4):
            Ksrc = t_ka if cp < 2 else t_kb
            kvlo = cp // 2
            kvhi = kvlo + 2

            def qk(e, kt, cp=cp, Ksrc=Ksrc):
                sb = (kt % 2) * 2
                e.matmul(bk(sb, 256), Ksrc[0:64, kt*128:(kt+1)*128],
                         v_q[0:64, (2*cp)*256:(2*cp+1)*256], start=True, stop=True,
                         tile_position=(0, 0))
                e.matmul(bk(sb, 256, off=256), Ksrc[0:64, kt*128:(kt+1)*128],
                         v_q[0:64, (2*cp+1)*256:(2*cp+2)*256], start=True, stop=True,
                         tile_position=(0, 0))
                e.matmul(bk(sb + 1, 256), Ksrc[64:128, kt*128:(kt+1)*128],
                         v_q[64:128, (2*cp)*256:(2*cp+1)*256], start=True, stop=True,
                         tile_position=(64, 0))
                return e.matmul(bk(sb + 1, 256, off=256), Ksrc[64:128, kt*128:(kt+1)*128],
                                v_q[64:128, (2*cp+1)*256:(2*cp+2)*256], start=True,
                                stop=True, tile_position=(64, 0))

            def sres(kt):
                return ("pb0", "pb1") if kt % 2 == 0 else ("pb2", "pb3")

            def f_tzero(e):
                e.memset(psA[0:128, 4*512:5*512], 0.0)
                return e.memset(psA[0:128, 5*512:6*512], 0.0)
            step("vector", [], f_tzero, w=("pb4", "pb5"))

            step("tensor", [], lambda e, qk=qk: qk(e, 0),
                 r=("q", "ka", "kb"), w=sres(0))
            if ktmax > 1:
                step("tensor", [], lambda e, qk=qk: qk(e, 1),
                     r=("q", "ka", "kb"), w=sres(1))

            for kt in range(ktmax):
                pb = kt % 2
                step("scalar", [], lambda e, pb=pb: e.activation(
                    v_ps[pb][:, 0:1024], psA[0:128, pb*1024:(pb+1)*1024], ACT.Exp,
                    scale=SCALE),
                     r=sres(kt), w=("p%d" % pb,))

                if kt >= 4 * i:
                    j = kt - 4 * i

                    def f_mask(e, j=j, pb=pb, vmk=vmk):
                        mb = vmk[:, j*256:(j+1)*256]
                        bc4 = mb.unsqueeze(1).to_broadcast([128, 4, 256])
                        pv = v_ps[pb][:, 0:1024].rearrange("p (s j) -> p s j", s=4)
                        return e.tensor_tensor(pv, pv, bc4, OP.mult)
                    step("vector", [], f_mask, r=("p%d" % pb, mres), w=("p%d" % pb,))

                def f_av(e, kt=kt, cp=cp, kvlo=kvlo, kvhi=kvhi, qk=qk, pb=pb,
                         last=(kt == ktmax - 1), pre=(kt + 2 < ktmax)):
                    st = (kt == 0)
                    sp_ = last
                    vp = v_ps[pb]
                    e.matmul(psA[0:64, 4*512: 4*512+256],
                             t_v[:, kt*260 + kvlo*65: kt*260 + kvlo*65 + 64],
                             vp[:, 0:256], start=False, stop=sp_, tile_position=(0, 0),
                             skip_group_check=True)
                    e.matmul(psA[64:128, 4*512: 4*512+256],
                             t_v[:, kt*260 + kvlo*65: kt*260 + kvlo*65 + 64],
                             vp[:, 256:512], start=False, stop=sp_, tile_position=(0, 64),
                             skip_group_check=True)
                    e.matmul(psA[0:64, 5*512: 5*512+256],
                             t_v[:, kt*260 + kvhi*65: kt*260 + kvhi*65 + 64],
                             vp[:, 512:768], start=False, stop=sp_, tile_position=(0, 0),
                             skip_group_check=True)
                    e.matmul(psA[64:128, 5*512: 5*512+256],
                             t_v[:, kt*260 + kvhi*65: kt*260 + kvhi*65 + 64],
                             vp[:, 768:1024], start=False, stop=sp_, tile_position=(0, 64),
                             skip_group_check=True)
                    e.matmul(bk(6, 512, 0, 1), t_onescb[:, 0:1], vp[:, 0:512],
                             start=st, stop=sp_)
                    r = e.matmul(bk(7, 512, 0, 1), t_onescb[:, 0:1], vp[:, 512:1024],
                                 start=st, stop=sp_)
                    if pre:
                        r = qk(e, kt + 2)
                    return r
                avw = ("pb4", "pb5", "pb6", "pb7") + (sres(kt) if kt + 2 < ktmax else ())
                step("tensor", [], f_av, r=("p%d" % pb, "v", "consts", "q", "ka", "kb"),
                     w=avw)

            def f_rc(e):
                return e.reciprocal(t_rc[0:1, 0:1024], psA[0:1, 6*512:8*512])
            step("vector", [], f_rc, r=("pb6", "pb7"), w=("stat",))

            def f_rb(e):
                e.matmul(bk(6, 256, 0, 64), t_onesr[0:1, 0:64], t_rc[0:1, 0:256],
                         start=True, stop=True, tile_position=(0, 0))
                e.matmul(bk(6, 256, 64, 128), t_onesr[0:1, 0:64], t_rc[0:1, 256:512],
                         start=True, stop=True, tile_position=(0, 64))
                e.matmul(bk(6, 256, 0, 64, off=256), t_onesr[0:1, 0:64],
                         t_rc[0:1, 512:768], start=True, stop=True, tile_position=(0, 0))
                return e.matmul(bk(6, 256, 64, 128, off=256), t_onesr[0:1, 0:64],
                                t_rc[0:1, 768:1024], start=True, stop=True,
                                tile_position=(0, 64))
            step("tensor", [], f_rb, r=("stat", "consts"), w=("pb6",))

            def f_norm(e, cp=cp):
                e.tensor_copy(v_rb[:, 0:512], bk(6, 512))
                e.drain()
                e.tensor_tensor(v_a[0:64, (2*cp)*256:(2*cp+1)*256],
                                psA[0:64, 4*512: 4*512+256], v_rb[0:64, 0:256], OP.mult)
                e.tensor_tensor(v_a[64:128, (2*cp)*256:(2*cp+1)*256],
                                psA[64:128, 4*512: 4*512+256], v_rb[64:128, 0:256],
                                OP.mult)
                e.tensor_tensor(v_a[0:64, (2*cp+1)*256:(2*cp+2)*256],
                                psA[0:64, 5*512: 5*512+256], v_rb[0:64, 256:512],
                                OP.mult)
                return e.tensor_tensor(v_a[64:128, (2*cp+1)*256:(2*cp+2)*256],
                                       psA[64:128, 5*512: 5*512+256],
                                       v_rb[64:128, 256:512], OP.mult)
            step("vector", [], f_norm,
                 r=("pb6", "pb4", "pb5"), w=("rtmp", "a"))

        for oc in range(8):
            wbk = 1 if oc % 2 == 0 else 0
            wres = "pb1" if oc % 2 == 0 else "pb0"

            def f_wo(e, oc=oc, wbk=wbk):
                r = None
                for ec in range(8):
                    r = e.matmul(bk(wbk, 256),
                                 wout_chunk(ec)[:, oc*128:(oc+1)*128],
                                 v_a[:, ec*256:(ec+1)*256], start=(ec == 0), stop=(ec == 7),
                                 skip_group_check=True)
                return r
            step("tensor", [s_dve], f_wo, r=("a", "wsp"), w=(wres,))

            def f_x2(e, oc=oc, vx1=vx1, wbk=wbk):
                return e.scalar_tensor_tensor(
                    t_x1g[:, oc*256:(oc+1)*256], bk(wbk, 256),
                    t_bout[:, oc:oc+1],
                    vx1[:, oc*256:(oc+1)*256], OP.add, OP.add)
            step("vector", [s_pe], f_x2, r=(wres, xres, "bout"), w=("x1g",))

        def f_x2sp(e, i=i):
            return e.dma_start(out=d_x2sp[:, :, i*256:(i+1)*256].rearrange("c p j -> p c j"),
                               in_=t_x1g[:].rearrange("p (c j) -> p c j", c=8))
        step("sync", [s_dve], f_x2sp, dma_sem=s_sp, r=("x1g",), w=("x2sp%d" % i,))

    nc.leave_named_scope("phaseB", _sidB, False)
    _sidC = nc.enter_named_scope("phaseC", False)[0]
    for dst_, srcs_ in (("ht", ("q", "p0", "p1", "a")),
                        ("xh", ("ewb2", "cs2"))):
        for s__ in srcs_:
            if s__ in writers:
                readers.setdefault(dst_, []).append(writers[s__])
            readers.setdefault(dst_, []).extend(readers.get(s__, ()))
    def f_wreload(e):
        r = []
        for x_ in range(NEXP):
            r.append(e.dma_start(
                out=t_w1[x_][:, 0:1024].rearrange("p (c j) -> p c j", c=2),
                in_=d_w1[x_][0:2].rearrange("c p j -> p c j")))
        for x_ in range(2):
            r.append(e.dma_start(
                out=t_w2[x_][:, 0:1024].rearrange("p (c j) -> p c j", c=1),
                in_=d_w2[x_][0:1].rearrange("c p j -> p c j")))
        return r
    step("gpsimd", [], f_wreload, dma_sem=s_dmag, ndma=8, w=("wsp",))

    for j in range(NQG if do_c else 0):
        t_xg = t_xga if j % 2 == 0 else t_xgb
        xgres = "xga" if j % 2 == 0 else "xgb"
        src_xg = mk_src(t_xg)

        def f_cload(e, j=j, t_xg=t_xg):
            return [e.dma_start(out=t_xg[:].rearrange("p (c j) -> p c j", c=8),
                                in_=d_x2sp[:, :, j*GRP:(j+1)*GRP].rearrange("c p j -> p c j"))]
        step("sync", COMPUTE + [s_sp], f_cload, dma_sem=s_dma, r=("x2sp%d" % j,), w=(xgres,))

        ln_block(src_xg, t_xh, True, xgres, "xh", src_t=t_xg)
        router_block(src_xg, xgres)
        ffn_block(mk_evac(t_xg), xgres)

        def f_store(e, j=j):
            return e.dma_start(out=d_out[:, :, j*GRP:(j+1)*GRP].rearrange("c p j -> p c j"),
                               in_=t_x1g[:].rearrange("p (c j) -> p c j", c=8))
        step("sync", [s_dve], f_store, dma_sem=s_sp, r=("x1g",), w=("d_out",))

    nc.leave_named_scope("phaseC", _sidC, False)
    flush_plan()
    _sp_total = sum(s.amt for s in all_steps if s.sem is s_sp)

    def f_final(e):
        e.wait_ge(s_sp, _sp_total)
    blk.sync(f_final)

    es.close()
    return nc



def kernel(**inputs):
    in_maps, metas = host_prepare(inputs)
    nc = build_program()
    res = run_bass_kernel_spmd(nc, in_maps, core_ids=list(range(8)))
    out = np.zeros((4, S, E), dtype=np.float32)
    for core in range(8):
        b, own = metas[core]
        o = res.results[core]["out"].reshape(E, OWN)
        for j, g in enumerate(own):
            out[b, g*128:(g+1)*128, :] = o[:, j*128:(j+1)*128].T
    return out


# version 20
# speedup vs baseline: 1.0161x; 1.0108x over previous
import numpy as np
import ml_dtypes
import concourse.bass as bass
import concourse.mybir as mybir
from concourse.bass_utils import run_bass_kernel_spmd

F32 = mybir.dt.float32
BF16 = mybir.dt.bfloat16
I32 = mybir.dt.int32
OP = mybir.AluOpType
ACT = mybir.ActivationFunctionType
AX = mybir.AxisListType.X
BF = ml_dtypes.bfloat16

E, S, NB = 1024, 4096, 32
H, KVH, D, QL, KVL = 16, 4, 64, 64, 32
NEXP, DEXP = 6, 512
EPS = 1e-5
GRP = 256
NGRP = S // GRP
NQG = 8
OWN = 2048
SCALE = 0.125
QTEMP_INV = 1.0 / 0.3
STEP_LIMIT = [10**9]
SCHED_DUMP = []
PRED = [0.0]


def own_blocks(half):
    keep = (0, 3) if half == 0 else (1, 2)
    return [g for g in range(NB) if g % 4 in keep]



def _swap64(Wcols):
    r = Wcols.reshape(Wcols.shape[0], -1, 32, 2)
    return np.ascontiguousarray(r[..., ::-1].reshape(Wcols.shape))


def _rope_rows():
    inv = 1.0 / (10000.0 ** (np.arange(0, D, 2, dtype=np.float64) / D))
    ang = np.outer(inv, np.arange(S, dtype=np.float64))
    cos = np.repeat(np.cos(ang), 2, axis=0)
    sin = np.repeat(np.sin(ang), 2, axis=0).copy()
    sin[0::2] *= -1.0
    C = np.concatenate([cos, cos], 0).astype(np.float32)
    Sg = np.concatenate([sin, sin], 0).astype(np.float32)
    return C, Sg


def host_prepare(inputs):
    x = np.asarray(inputs["L0_attn_x"], dtype=np.float32)
    Wqa = np.asarray(inputs["L0_attn_Wqa"], np.float32)
    Wqb = np.asarray(inputs["L0_attn_Wqb"], np.float32)
    Wkva = np.asarray(inputs["L0_attn_Wkva"], np.float32)
    Wkvb = np.asarray(inputs["L0_attn_Wkvb"], np.float32)
    Wout = np.asarray(inputs["L0_attn_Wout"], np.float32)
    bout = np.asarray(inputs["L0_attn_bout"], np.float32)
    projw = np.asarray(inputs["L0_ffn_proj_w"], np.float32)
    anch = np.asarray(inputs["L0_ffn_anchors"], np.float32)
    temp = np.asarray(inputs["L0_ffn_routing_temp"], np.float32).reshape(1, 1)
    w1 = np.asarray(inputs["L0_ffn_w1"], np.float32)
    w2 = np.asarray(inputs["L0_ffn_w2"], np.float32)

    sh = {}
    sh["w1r"] = np.ascontiguousarray(w1.reshape(NEXP, 8, 128, DEXP))
    sh["w2r"] = np.ascontiguousarray(w2.reshape(NEXP, 4, 128, E))
    sh["wqa"] = np.ascontiguousarray(Wqa.reshape(8, 128, QL))
    WqbP = np.empty_like(Wqb)
    for c in range(8):
        WqbP[:, c*128:c*128+64] = Wqb[:, c*64:(c+1)*64]
        WqbP[:, c*128+64:(c+1)*128] = Wqb[:, (c+8)*64:(c+9)*64]
    sh["wqb"] = np.ascontiguousarray(np.concatenate([WqbP, _swap64(WqbP)], 1))
    sh["wkva"] = np.ascontiguousarray(Wkva.reshape(8, 128, KVL))
    kvr = Wkvb.reshape(KVL, KVH, D, 2)
    WK = np.ascontiguousarray(kvr[..., 0].reshape(KVL, KVH * D))
    WV = np.ascontiguousarray(kvr[..., 1].reshape(KVL, KVH * D))
    WKs = _swap64(WK)
    sh["wkvbk"] = np.ascontiguousarray(np.concatenate(
        [WK[:, 0:64], WK[:, 128:192],
         WKs[:, 0:64], WKs[:, 128:192],
         WK[:, 64:128], WK[:, 192:256],
         WKs[:, 64:128], WKs[:, 192:256]], 1))
    sh["wkvbv"] = WV
    wop = np.empty_like(Wout)
    for j, oc in enumerate([0, 4, 1, 5, 2, 6, 3, 7]):
        wop[j*128:(j+1)*128] = Wout[oc*128:(oc+1)*128]
    sh["wout"] = np.ascontiguousarray(wop.reshape(8, 128, E))
    sh["bout"] = np.ascontiguousarray(bout.reshape(8, 128).T)
    sh["projw"] = np.ascontiguousarray(
        projw.T.reshape(8, 128, NEXP).transpose(1, 0, 2).reshape(128, 8 * NEXP))
    sh["anchT"] = np.ascontiguousarray(anch.T)
    sh["rtemp"] = temp
    sh["ident"] = np.eye(128, dtype=np.float32)

    C, Sg = _rope_rows()
    csk = np.zeros((128, 2 * S), dtype=np.float32)
    for g in range(NGRP):
        csk[:, g*512:g*512+256] = C[:, g*256:(g+1)*256]
        csk[:, g*512+256:(g+1)*512] = Sg[:, g*256:(g+1)*256]
    sh["csk"] = csk.astype(BF)

    in_maps, metas = [], []
    for core in range(8):
        b, half = core // 2, core % 2
        own = own_blocks(half)
        m = dict(sh)
        m["x"] = np.ascontiguousarray(x[b].T.reshape(8, 128, S))
        m["coreinfof"] = np.array([[half]], dtype=np.float32)
        csq = np.zeros((128, 2 * OWN), dtype=np.float32)
        for i in range(NQG):
            cols = list(range(own[2*i]*128, own[2*i]*128+128)) + \
                   list(range(own[2*i+1]*128, own[2*i+1]*128+128))
            csq[:, i*512:i*512+256] = C[:, cols]
            csq[:, i*512+256:(i+1)*512] = Sg[:, cols]
        m["csq"] = csq.astype(BF)
        mt = np.zeros((NQG * 4, 128, 256), dtype=np.float32)
        kk = np.arange(128)[:, None]
        qq = np.arange(128)[None, :]
        for i in range(NQG):
            g0, g1 = own[2*i], own[2*i+1]
            for j in range(4):
                kt = 4 * i + j
                for bi, g in ((0, g0), (1, g1)):
                    mt[4*i+j, :, bi*128:(bi+1)*128] = (kt*128 + kk <= g*128 + qq)
        m["maskt"] = mt.astype(BF)
        in_maps.append(m)
        metas.append((b, own))
    return in_maps, metas



class _Counts:
    def __init__(self):
        self.c = {}

    def inc(self, instr, semh, amt=1):
        instr.then_inc(semh, amt)
        self.c[semh.name] = self.c.get(semh.name, 0) + amt

    def val(self, semh):
        return self.c.get(semh.name, 0)


def _shift_ap(ap, delta):
    return bass.AP(ap.tensor, ap.offset + delta, ap.ap)


def build_program(taps=(), ngrp_a=NGRP, do_b=True, do_c=True, nqg=NQG, nc_only=False):
    nc = bass.Bass("TRN2", target_bir_lowering=False, dynamic_dma_scratch_size=16384)
    nt = _Counts()

    def dram(n, s, d, k="ExternalInput"):
        return nc.dram_tensor(n, s, d, kind=k).ap()

    d_x = dram("x", [8, 128, S], F32)
    d_w1 = dram("w1r", [NEXP, 8, 128, DEXP], F32)
    d_w2 = dram("w2r", [NEXP, 4, 128, E], F32)
    d_wqa = dram("wqa", [8, 128, QL], F32)
    d_wqb = dram("wqb", [QL, 2048], F32)
    d_wkva = dram("wkva", [8, 128, KVL], F32)
    d_wkvbk = dram("wkvbk", [KVL, 512], F32)
    d_wkvbv = dram("wkvbv", [KVL, 256], F32)
    d_wout = dram("wout", [8, 128, E], F32)
    d_bout = dram("bout", [128, 8], F32)
    d_projw = dram("projw", [128, 8 * NEXP], F32)
    d_anchT = dram("anchT", [NEXP, NEXP], F32)
    d_rtemp = dram("rtemp", [1, 1], F32)
    d_ident = dram("ident", [128, 128], F32)
    d_csk = dram("csk", [128, 2 * S], BF16)
    d_csq = dram("csq", [128, 2 * OWN], BF16)
    d_maskt = dram("maskt", [NQG * 4, 128, 256], BF16)
    d_cinfo = dram("coreinfof", [1, 1], F32)
    d_out = dram("out", [8, 128, OWN], F32, "ExternalOutput")

    spk = "ExternalOutput" if "spill" in taps else "Internal"
    d_x1own = dram("x1own", [8, 128, OWN], F32, spk)
    d_yown = dram("yown", [8, 128, OWN], BF16, spk)
    x2k = "ExternalOutput" if "x2" in taps else "Internal"
    d_x2sp = dram("x2sp", [8, 128, OWN], F32, x2k)
    if "kv" in taps:
        d_kadbg = dram("kadbg", [128, S], BF16, "ExternalOutput")
        d_kbdbg = dram("kbdbg", [128, S], BF16, "ExternalOutput")
        d_vdbg = dram("vdbg", [128, NB * 260], BF16, "ExternalOutput")
    if "ew" in taps:
        d_ewdbg = dram("ewdbg", [1, NGRP * 1536], F32, "ExternalOutput")
        d_zdbg = dram("zdbg", [NEXP, S], F32, "ExternalOutput")
        d_r6dbg = dram("r6dbg", [128, 16 * NGRP], F32, "ExternalOutput")
        d_ewtdbg = dram("ewtdbg", [128, 16 * NGRP], F32, "ExternalOutput")
        d_q6dbg = dram("q6dbg", [NEXP, S], F32, "ExternalOutput")
        d_rscdbg = dram("rscdbg", [128, 64 * NGRP], F32, "ExternalOutput")

    from contextlib import ExitStack
    es = ExitStack()

    def sbt(n, s, d):
        return es.enter_context(nc.sbuf_tensor(n, s, d))

    def mksem(n):
        return es.enter_context(nc.semaphore(n))

    t_w1 = [sbt(f"w1_{e_}", [128, 8 * DEXP], BF16) for e_ in range(NEXP)]
    t_w2 = [sbt(f"w2_{e_}", [128, 4 * E], BF16) for e_ in range(NEXP)]
    t_wqa = sbt("wqas", [128, 8 * QL], BF16)
    t_wqb = sbt("wqbs", [QL, 2048], BF16)
    t_wkva = sbt("wkvas", [128, 8 * KVL], BF16)
    t_wkvbk = sbt("wkvbks", [KVL, 512], BF16)
    t_wkvbv = sbt("wkvbvs", [KVL, 256], BF16)
    t_bout = sbt("bouts", [128, 8], F32)
    t_projw = sbt("projws", [128, 8 * NEXP], F32)
    t_anchT = sbt("anchTs", [NEXP, NEXP], F32)
    t_ident = sbt("idents", [128, 128], F32)
    t_ka = sbt("ka", [128, S], BF16)
    t_kb = sbt("kb", [128, S], BF16)
    t_v = sbt("v", [128, NB * 260], BF16)
    t_onesr = sbt("onesr", [128, 128], F32)
    t_onesm = sbt("onesm", [1, 128], F32)
    t_onescol = sbt("onescol", [128, 1], F32)
    t_eps = sbt("epsc", [128, 1], F32)
    t_onescb = sbt("onescb", [128, 1], BF16)
    t_pw = sbt("pwsum", [1, NEXP], F32)
    t_alpha = sbt("alpha", [128, 1], F32)
    t_alpham = sbt("alpham", [128, 1], F32)
    t_rt = sbt("rtemps", [1, 1], F32)
    t_ci = sbt("cis", [1, 1], F32)
    t_hf = sbt("hflag", [128, 1], F32)

    def wout_chunk(ec):
        t = t_w1[ec] if ec < NEXP else t_w2[ec - NEXP]
        return t[:, 0:E]

    t_xga = sbt("xga", [128, 8 * GRP], F32)
    t_xgb = sbt("xgb", [128, 8 * GRP], F32)
    t_xsqp = sbt("xsqp", [128, 512], BF16)
    t_lnbc = sbt("lnbc", [128, 512], BF16)
    t_xh = sbt("xh", [128, 8 * GRP], BF16)
    t_stat = sbt("stat", [1, 1536], F32)
    t_z = sbt("z", [NEXP, GRP], F32)
    t_q6 = sbt("q6", [NEXP, GRP], F32)
    t_r6 = sbt("r6", [128, 16], F32)
    t_rsc = sbt("rsc", [128, 64], F32)
    t_rs2 = sbt("rs2", [128, 16], F32)
    t_ew = sbt("ew", [128, 16], F32)
    t_ewf6 = sbt("ewf6", [1, NEXP * GRP], F32)
    t_ewb = sbt("ewb", [128, NEXP * GRP], BF16)
    t_h = sbt("h", [128, 4 * GRP], BF16)
    t_ht = sbt("ht", [128, NEXP * 4 * GRP], BF16)
    t_x1g = sbt("x1g", [128, 8 * GRP], F32)
    t_yh = sbt("yhg", [128, 8 * GRP], BF16)
    t_kva = sbt("kvag", [KVL, GRP], BF16)
    t_cs = sbt("cs", [128, 512], BF16)
    t_rtmp = sbt("rtmp", [128, 512], BF16)
    t_qa = sbt("qag", [QL, 256], BF16)

    t_xsq = t_ht[:, 0:4096].bitcast(F32)
    t_rc = t_stat[0:1, 0:1024]
    t_osp = t_ht[:, 0:2048].bitcast(F32)
    v_q = t_ht[:, 0:2048]
    v_ps = (t_ht[:, 2048:3072], t_ht[:, 3072:4096])
    v_a = t_ht[:, 4096:6144]
    v_rb = t_rtmp
    v_masks = t_ewb[:, 0:1024]
    v_x1b = t_xga
    v_yb = t_yh

    psA = es.enter_context(nc.psum_tensor("arena", [128, 4096], F32))

    def bk(i, w=512, p0=0, p1=128, off=0):
        return psA[p0:p1, i * 512 + off: i * 512 + off + w]

    s_dma = mksem("s_dma")
    s_dmag = mksem("s_dmag")
    s_sp = mksem("s_sp")
    s_pe = mksem("s_pe")
    s_act = mksem("s_act")
    s_dve = mksem("s_dve")
    s_fen = mksem("s_fen")
    fenc = [0]

    blk = es.enter_context(nc.Block())
    SEM_OF = {"tensor": s_pe, "scalar": s_act, "vector": s_dve}
    COMPUTE = [s_pe, s_act, s_dve]
    ALLS = [s_dma, s_dmag, s_sp, s_pe, s_act, s_dve]

    stepno = [0]
    writers = {}
    readers = {}
    SEMS = {s.name: s for s in ALLS}
    all_steps = []
    epoch = [0]

    class _Step:
        __slots__ = ("sid", "eng", "fn", "dma_sem", "ndma", "deps", "est",
                     "epoch", "sem", "amt", "cnt", "start", "finish")

        def __init__(s, sid, eng, fn, dma_sem, ndma, deps, est, ep):
            s.sid, s.eng, s.fn, s.dma_sem, s.ndma = sid, eng, fn, dma_sem, ndma
            s.deps, s.est, s.epoch = deps, est, ep
            s.sem = dma_sem if dma_sem is not None else SEM_OF[eng]
            s.amt = 16 * ndma if dma_sem is not None else 1

    EST = {"tensor": 1500, "scalar": 1200, "vector": 800, "sync": 4000,
           "gpsimd": 4000}
    import os
    SCHED_MODE = [os.environ.get("KSCHED", "list")]
    last_on_sem = {}
    bar_deps = [{}]

    def step(eng, deps_unused, fn, dma_sem=None, ndma=1, r=(), w=(), est=None):
        stepno[0] += 1
        if stepno[0] > STEP_LIMIT[0]:
            return
        annotated = bool(r) or bool(w)
        deps = set(bar_deps[0].values())
        if annotated:
            for res in r:
                if res in writers:
                    deps.add(writers[res])
            for res in w:
                if res in writers:
                    deps.add(writers[res])
                deps.update(readers.get(res, ()))
        else:
            deps.update(last_on_sem.values())
        st = _Step(stepno[0], eng, fn, dma_sem, ndma, deps,
                   est if est is not None else EST[eng], epoch[0])
        all_steps.append(st)
        last_on_sem[st.sem.name] = st
        if annotated:
            for res in r:
                readers.setdefault(res, []).append(st)
            for res in w:
                writers[res] = st
                readers[res] = []
        else:
            for res in set(list(writers) + list(readers)):
                writers[res] = st
                readers[res] = []
        return st

    def barrier():
        epoch[0] += 1
        bar_deps[0] = dict(last_on_sem)

    def flush_plan():
        import heapq
        order = {e: [] for e in plan_engines}
        eng_free = {e: 0.0 for e in plan_engines}
        maxep = max((s.epoch for s in all_steps), default=0)
        ep_end = 0.0
        mode = SCHED_MODE[0]
        for ep in range(maxep + 1):
            steps_ = [s for s in all_steps if s.epoch == ep]
            if not steps_:
                continue
            indeg = {}
            for s in steps_:
                indeg[s] = sum(1 for d in s.deps if d.epoch == ep)
            succs = {s: [] for s in steps_}
            for s in steps_:
                for d in s.deps:
                    if d.epoch == ep:
                        succs[d].append(s)
            lp = {}
            for s in sorted(steps_, key=lambda t: -t.sid):
                lp[s] = s.est + max((lp[t] for t in succs[s]), default=0.0)

            def key(s, rt):
                if mode == 'fifo':
                    return s.sid
                if mode == 'cp':
                    return (rt, -lp[s])
                if mode == 'cpb':
                    return (int(rt // 2000), -lp[s])
                return rt
            ready = []
            for s in steps_:
                if indeg[s] == 0:
                    rt = max([ep_end] + [d.finish for d in s.deps if d.epoch < ep])
                    heapq.heappush(ready, (key(s, rt), s.sid, s))
            scheduled = 0
            while ready:
                _, _, s = heapq.heappop(ready)
                rt = max([ep_end] + [d.finish for d in s.deps if d.epoch == ep] +
                         [d.finish for d in s.deps if d.epoch < ep])
                start = max(rt, eng_free[s.eng])
                s.start = start
                s.finish = start + s.est
                eng_free[s.eng] = s.finish
                order[s.eng].append(s)
                scheduled += 1
                for t in succs[s]:
                    indeg[t] -= 1
                    if indeg[t] == 0:
                        rt2 = max([ep_end] + [d.finish for d in t.deps])
                        heapq.heappush(ready, (key(t, rt2), t.sid, t))
            assert scheduled == len(steps_), (scheduled, len(steps_))
            ep_end = max(eng_free.values())
        PRED[:] = [ep_end]
        sem_eng = {}
        for e in plan_engines:
            c = {}
            for s in order[e]:
                assert sem_eng.setdefault(s.sem.name, e) == e, s.sem.name
                c[s.sem.name] = c.get(s.sem.name, 0) + s.amt
                s.cnt = c[s.sem.name]
        SCHED_DUMP.clear()
        for e in plan_engines:
            for pos, s in enumerate(order[e]):
                SCHED_DUMP.append((e, pos, s.sid, s.sem.name, s.cnt,
                                   sorted((d.eng, d.sid, d.sem.name, d.cnt)
                                          for d in s.deps)))
        for e in plan_engines:
            if not order[e]:
                continue

            def body(eng_h, steps_=order[e]):
                hw = {}
                for s in steps_:
                    dd = {}
                    for d in s.deps:
                        sn = d.sem.name
                        if dd.get(sn, 0) < d.cnt:
                            dd[sn] = d.cnt
                    for sn, c in sorted(dd.items()):
                        if hw.get(sn, 0) < c:
                            eng_h.wait_ge(SEMS[sn], c)
                            hw[sn] = c
                    rr = s.fn(eng_h)
                    if s.dma_sem is not None:
                        lst = rr if isinstance(rr, (list, tuple)) else [rr]
                        assert len(lst) == s.ndma, (len(lst), s.ndma)
                        for ins in lst:
                            ins.then_inc(s.dma_sem, 16)
                    else:
                        rr.then_inc(s.sem, 1)
            getattr(blk, e)(body)

    plan_engines = ("sync", "gpsimd", "tensor", "scalar", "vector")


    step("sync", [], lambda e: [
        e.dma_start(out=t_ident[:], in_=d_ident),
        e.dma_start(out=t_anchT[:], in_=d_anchT),
        e.dma_start(out=t_bout[:], in_=d_bout),
        e.dma_start(out=t_projw[:], in_=d_projw),
        e.dma_start(out=t_rt[:], in_=d_rtemp),
        e.dma_start(out=t_ci[:], in_=d_cinfo),
    ], dma_sem=s_dma, ndma=6, w=("ident", "anchT", "bout", "projw", "smallw"))

    def f_init_gp(e):
        r = []
        for x_ in range(NEXP):
            r.append(e.dma_start(out=t_w1[x_][:].rearrange("p (c j) -> p c j", c=8), in_=d_w1[x_].rearrange("c p j -> p c j")))
            r.append(e.dma_start(out=t_w2[x_][:].rearrange("p (c j) -> p c j", c=4), in_=d_w2[x_].rearrange("c p j -> p c j")))
        r.append(e.dma_start(out=t_wqa[:].rearrange("p (c j) -> p c j", c=8), in_=d_wqa.rearrange("c p j -> p c j")))
        r.append(e.dma_start(out=t_wqb[:], in_=d_wqb))
        r.append(e.dma_start(out=t_wkva[:].rearrange("p (c j) -> p c j", c=8), in_=d_wkva.rearrange("c p j -> p c j")))
        r.append(e.dma_start(out=t_wkvbk[:], in_=d_wkvbk))
        r.append(e.dma_start(out=t_wkvbv[:], in_=d_wkvbv))
        return r
    step("gpsimd", [], f_init_gp, dma_sem=s_dmag, ndma=17, w=("wsp",))

    def f_init_dve(e):
        e.memset(t_onesr[:], 1.0)
        e.memset(t_onesm[:], -1.0)
        e.memset(t_onescol[:], 1.0)
        e.memset(t_eps[:], EPS)
        e.memset(t_onescb[:], 1.0)
        vones = t_v[:].rearrange("p (t k j) -> p t k j", t=NB, k=4)[:, :, :, 64:65]
        return e.memset(vones, 1.0)
    step("vector", [], f_init_dve, w=("consts", "v"))

    def f_init_pe(e):
        for c in range(8):
            e.matmul(bk(0, NEXP, 0, 1), t_onescol[:, 0:1],
                     t_projw[:, c*NEXP:(c+1)*NEXP], start=(c == 0), stop=(c == 7))
        e.matmul(bk(1, 1), t_onesr[0:1, 0:128], t_rt[0:1, 0:1],
                 start=True, stop=True)
        return e.matmul(bk(1, 1, off=1), t_onesr[0:1, 0:128], t_ci[0:1, 0:1],
                        start=True, stop=True)
    step("tensor", [s_dma, s_dmag], f_init_pe, r=("projw", "consts", "smallw"), w=("pb0", "pb1"))

    def f_init_dve2(e):
        e.tensor_copy(t_pw[0:1, 0:NEXP], bk(0, NEXP, 0, 1))
        e.tensor_copy(t_hf[:, 0:1], bk(1, 1, off=1))
        e.tensor_scalar_max(t_rsc[:, 0:1], bk(1, 1), 0.1)
        e.drain()
        e.reciprocal(t_rsc[:, 1:2], t_rsc[:, 0:1])
        e.drain()
        e.tensor_scalar_mul(t_alpha[:, 0:1], t_rsc[:, 1:2], 0.5)
        e.drain()
        return e.tensor_scalar_mul(t_alpham[:, 0:1], t_alpha[:, 0:1], -1.0)
    step("vector", [s_pe], f_init_dve2, r=("pb0", "pb1"), w=("pw", "hf", "alpha", "rsc"))

    def ln_block(src_c, dst, router_extras, srcres, dstres, src_t=None):
        n = GRP

        def f_sums(e):
            r = None
            for c in range(8):
                r = e.matmul(bk(0, n, 0, 1), t_onescol[:, 0:1], src_c(c),
                             start=(c == 0), stop=(c == 7))
            return r
        step("tensor", [s_dma, s_dve], f_sums, r=srcres + ("consts",), w=("pb0",))
        for c in range(8):
            pp = c % 2
            step("scalar", [], lambda e, c=c, pp=pp: e.activation(
                t_xsqp[:, pp*n:(pp+1)*n], src_c(c), ACT.Square),
                 r=srcres, w=("xsqp%d" % pp,))
            step("tensor", [], lambda e, c=c, pp=pp: e.matmul(
                bk(0, n, 0, 1, off=n), t_onescb[:, 0:1], t_xsqp[:, pp*n:(pp+1)*n],
                start=(c == 0), stop=(c == 7)),
                 r=("xsqp%d" % pp, "consts"), w=("pb0",))

        def f_mv(e):
            e.tensor_scalar_mul(t_stat[0:1, 0:n], bk(0, n, 0, 1), 1.0 / E)
            e.drain()
            e.tensor_mul(t_stat[0:1, 256:256+n], t_stat[0:1, 0:n], t_stat[0:1, 0:n])
            e.drain()
            return e.scalar_tensor_tensor(
                t_stat[0:1, 512:512+n], bk(0, n, 0, 1, off=n), 1.0 / E,
                t_stat[0:1, 256:256+n], OP.mult, OP.subtract)
        step("vector", [s_pe], f_mv, r=("pb0",), w=("stat",))

        def f_rstd(e):
            e.activation(t_stat[0:1, 256:256+n], t_stat[0:1, 512:512+n], ACT.Ln,
                         bias=t_eps[0:1, 0:1])
            e.drain()
            r = e.activation(t_stat[0:1, 768:768+n], t_stat[0:1, 256:256+n],
                             ACT.Exp, scale=-0.5)
            if router_extras:
                r = e.activation(t_stat[0:1, 1280:1280+n], t_stat[0:1, 0:n],
                                 ACT.Copy, scale=-1.0)
            return r
        step("scalar", [s_dve], f_rstd, r=("stat", "consts"), w=("stat",))

        def f_chain3(e):
            return e.tensor_mul(t_stat[0:1, 1024:1024+n], t_stat[0:1, 0:n],
                                t_stat[0:1, 768:768+n])
        step("vector", [s_act], f_chain3, r=("stat",), w=("stat",))

        def f_bc(e):
            e.matmul(bk(1, n), t_onesr[0:1, 0:128], t_stat[0:1, 768:768+n],
                     start=True, stop=True)
            return e.matmul(bk(1, n, off=256), t_onesm[0:1, 0:128],
                            t_stat[0:1, 1024:1024+n], start=True, stop=True)
        step("tensor", [s_dve], f_bc, r=("stat", "consts"), w=("pb1",))

        def f_apply(e):
            e.tensor_copy(t_lnbc[:, 0:512], bk(1, 512))
            e.drain()
            if src_t is not None:
                s3 = src_t[:, 0:8*n].rearrange("p (c j) -> p c j", c=8)
                d3 = dst[:, 0:8*n].rearrange("p (c j) -> p c j", c=8)
                rb3 = t_lnbc[:, 0:n].unsqueeze(1).to_broadcast([128, 8, n])
                mb3 = t_lnbc[:, 256:256+n].unsqueeze(1).to_broadcast([128, 8, n])
                e.tensor_tensor(d3, s3, rb3, OP.mult)
                e.drain()
                return e.tensor_tensor(d3, d3, mb3, OP.add)
            def mul(c):
                e.tensor_tensor(t_rtmp[:, (c % 2)*n:(c % 2)*n+n], src_c(c),
                                t_lnbc[:, 0:n], OP.mult)
            def add(c):
                return e.tensor_tensor(dst[:, c*n:(c+1)*n],
                                       t_rtmp[:, (c % 2)*n:(c % 2)*n+n],
                                       t_lnbc[:, 256:256+n], OP.add)
            mul(0)
            for c in range(7):
                mul(c + 1)
                add(c)
            return add(7)
        step("vector", [s_pe], f_apply, r=srcres + ("pb1",), w=("lnbc", dstres, "rtmp"))

    def router_block(src_c, srcres):
        n = GRP

        def f_z(e):
            for c in range(8):
                e.matmul(bk(2, n, 0, NEXP), t_projw[:, c*NEXP:(c+1)*NEXP], src_c(c),
                         start=(c == 0), stop=False)
            return e.matmul(bk(2, n, 0, NEXP), t_pw[0:1, 0:NEXP],
                            t_stat[0:1, 1280:1280+n], start=False, stop=True)
        step("tensor", [s_dve], f_z, r=srcres + ("projw", "pw", "stat"), w=("pb2",))

        def f_zev(e):
            e.tensor_copy(t_q6[0:NEXP, 0:n], bk(2, n, 0, NEXP))
            e.drain()
            return e.tensor_tensor(t_z[0:NEXP, 0:n], t_q6[0:NEXP, 0:n],
                                   bk(1, n, 0, NEXP), OP.mult)
        step("vector", [s_pe], f_zev, r=("pb2", "pb1"), w=("q6", "z"))

        step("scalar", [s_dve], lambda e: e.activation(
            t_q6[0:NEXP, 0:n], t_z[0:NEXP, 0:n], ACT.Exp, scale=2.0 * QTEMP_INV),
             r=("z",), w=("q6",))

        def f_tanh2(e):
            e.tensor_scalar_add(t_z[0:NEXP, 0:n], t_q6[0:NEXP, 0:n], 1.0)
            e.drain()
            e.reciprocal(t_q6[0:NEXP, 0:n], t_z[0:NEXP, 0:n])
            e.drain()
            return e.tensor_scalar(t_q6[0:NEXP, 0:n], t_q6[0:NEXP, 0:n], -2.0, 1.0,
                                   OP.mult, OP.add)
        step("vector", [s_act], f_tanh2, r=("q6", "z"), w=("q6", "z"))

        def f_dot(e):
            r = None
            for b2 in range(2):
                r = e.matmul(bk(2, NEXP, 0, 128, off=256 + b2*NEXP),
                             t_q6[0:NEXP, b2*128:(b2+1)*128], t_anchT[0:NEXP, 0:NEXP],
                             start=True, stop=True)
            return r
        step("tensor", [s_act], f_dot, r=("q6", "anchT"), w=("pb2",))

        def f_tok(e):
            R = lambda b2: t_r6[:, b2*8:b2*8+NEXP]
            for b2 in range(2):
                e.tensor_copy(R(b2), bk(2, NEXP, 0, 128, off=256 + b2*NEXP))
            for b2 in range(2):
                e.tensor_reduce(t_rs2[:, b2*8+0:b2*8+1], R(b2), AX, OP.max)
            for b2 in range(2):
                e.tensor_scalar(t_rsc[:, b2*32+8:b2*32+8+NEXP], R(b2),
                                t_rs2[:, b2*8+0:b2*8+1], 0.0, OP.subtract, OP.is_equal)
            for b2 in range(2):
                e.scalar_tensor_tensor(t_rsc[:, b2*32+16:b2*32+16+NEXP],
                                       t_rsc[:, b2*32+8:b2*32+8+NEXP], -1e30, R(b2),
                                       OP.mult, OP.add)
            for b2 in range(2):
                e.tensor_reduce(t_rs2[:, b2*8+1:b2*8+2],
                                t_rsc[:, b2*32+16:b2*32+16+NEXP], AX, OP.max)
            for b2 in range(2):
                rr = e.tensor_tensor(t_rs2[:, b2*8+2:b2*8+3], t_rs2[:, b2*8+0:b2*8+1],
                                     t_rs2[:, b2*8+1:b2*8+2], OP.subtract)
            return rr
        step("vector", [s_pe], f_tok, r=("pb2",), w=("r6", "rsc", "rs2"))

        def f_sig(e):
            for b2 in range(2):
                r = e.activation(t_rs2[:, b2*8+5:b2*8+6], t_rs2[:, b2*8+2:b2*8+3],
                                 ACT.Exp, scale=t_alpham[:, 0:1])
            return r
        step("scalar", [s_dve], f_sig, r=("rs2", "alpha"), w=("rs2",))

        def f_ew(e):
            for b2 in range(2):
                e.tensor_scalar_add(t_rsc[:, b2*32:b2*32+1], t_rs2[:, b2*8+5:b2*8+6], 1.0)
            for b2 in range(2):
                e.reciprocal(t_rs2[:, b2*8+3:b2*8+4], t_rsc[:, b2*32:b2*32+1])
            for b2 in range(2):
                e.tensor_scalar(t_rs2[:, b2*8+4:b2*8+5], t_rs2[:, b2*8+3:b2*8+4],
                                -1.0, 1.0, OP.mult, OP.add)
            for b2 in range(2):
                e.tensor_scalar(t_rsc[:, b2*32+24:b2*32+24+NEXP],
                                t_rsc[:, b2*32+16:b2*32+16+NEXP],
                                t_rs2[:, b2*8+1:b2*8+2], 0.0, OP.subtract, OP.is_equal)
            for b2 in range(2):
                e.tensor_scalar(t_rsc[:, b2*32+16:b2*32+16+NEXP],
                                t_rsc[:, b2*32+24:b2*32+24+NEXP],
                                t_rs2[:, b2*8+4:b2*8+5], None, OP.mult)
            for b2 in range(2):
                r = e.scalar_tensor_tensor(t_ew[:, b2*8:b2*8+NEXP],
                                           t_rsc[:, b2*32+8:b2*32+8+NEXP],
                                           t_rs2[:, b2*8+3:b2*8+4],
                                           t_rsc[:, b2*32+16:b2*32+16+NEXP],
                                           OP.mult, OP.add)
            return r
        step("vector", [s_act], f_ew, r=("rsc", "rs2"), w=("rsc", "rs2", "ew"))

        def f_ewT(e):
            r = None
            for x_ in range(NEXP):
                for b2 in range(2):
                    r = e.transpose(psA[0:1, 1024 + x_*256 + b2*128:
                                        1024 + x_*256 + b2*128 + 128],
                                    t_ew[:, b2*8 + x_: b2*8 + x_ + 1], t_ident[:])
            return r
        step("tensor", [s_dve], f_ewT, r=("ew", "ident"), w=("pb2", "pb3", "pb4"))

        def f_ewf(e):
            r = None
            for x_ in range(NEXP):
                for b2 in range(2):
                    o = x_*256 + b2*128
                    r = e.tensor_copy(t_ewf6[0:1, o:o+128],
                                      psA[0:1, 1024 + o: 1024 + o + 128])
            return r
        step("vector", [s_pe], f_ewf, r=("pb2", "pb3", "pb4"), w=("ewf",))

    def ffn_block(evac, xgres):
        n = GRP
        for x0 in (0, 2, 4):
            eb = 2 + (x0 // 2) % 2
            def f_ewbc(e, x0=x0, eb=eb):
                e.matmul(bk(eb, n), t_onesr[0:1, 0:128], t_ewf6[0:1, x0*n:(x0+1)*n],
                         start=True, stop=True)
                return e.matmul(bk(eb, n, off=256), t_onesr[0:1, 0:128],
                                t_ewf6[0:1, (x0+1)*n:(x0+2)*n], start=True, stop=True)
            step("tensor", [s_dve], f_ewbc, r=("ewf", "consts"), w=("pb%d" % eb,))

            def f_ewbev(e, x0=x0, eb=eb):
                e.tensor_copy(t_ewb[:, x0*n:(x0+1)*n], bk(eb, n))
                return e.tensor_copy(t_ewb[:, (x0+1)*n:(x0+2)*n], bk(eb, n, off=256))
            step("vector", [s_pe], f_ewbev, r=("pb%d" % eb,), w=("ewb",))

        for x_ in range(NEXP):
            hb = 4 if x_ % 2 == 0 else 6

            def f_w1(e, x_=x_, hb=hb):
                r = None
                for m_ in range(4):
                    for c in range(8):
                        r = e.matmul(psA[0:128, hb*512 + m_*n: hb*512 + (m_+1)*n],
                                     t_w1[x_][:, c*512 + m_*128: c*512 + (m_+1)*128],
                                     t_xh[:, c*n:(c+1)*n], start=(c == 0), stop=(c == 7))
                return r
            step("tensor", [s_dve, s_act], f_w1, r=("xh", "wsp"),
                 w=(("pb4", "pb5") if hb == 4 else ("pb6", "pb7")))

            def f_gelu(e, x_=x_, hb=hb):
                return e.activation(t_h[:, 0:4*n], psA[0:128, hb*512: hb*512 + 4*n],
                                    ACT.Gelu_apprx_tanh)
            step("scalar", [s_pe], f_gelu,
                 r=(("pb4", "pb5") if hb == 4 else ("pb6", "pb7")), w=("h",))

            def f_ht(e, x_=x_):
                base = t_ewb[:, x_*n:(x_+1)*n]
                bc = base.unsqueeze(1).to_broadcast([128, 4, n])
                hh = t_h[:, 0:4*n].rearrange("p (m j) -> p m j", m=4)
                dd = t_ht[:, x_*4*n:(x_+1)*4*n].rearrange("p (m j) -> p m j", m=4)
                return e.tensor_tensor(dd, hh, bc, OP.mult)
            step("vector", [s_act], f_ht, r=("h", "ewb"), w=("ht",))

        for oc in range(8):
            ob = oc % 2
            def f_w2(e, oc=oc, ob=ob):
                first = True
                r = None
                for x_ in range(NEXP):
                    for k_ in range(4):
                        r = e.matmul(bk(ob, n), t_w2[x_][:, k_*E + oc*128: k_*E + (oc+1)*128],
                                     t_ht[:, (x_*4 + k_)*n:(x_*4 + k_ + 1)*n],
                                     start=first, stop=(x_ == NEXP-1 and k_ == 3))
                        first = False
                return r
            step("tensor", [s_dve], f_w2, r=("ht", "wsp"), w=("pb%d" % ob,))
            step("vector", [s_pe], lambda e, oc=oc, ob=ob: evac(e, oc, ob),
                 r=("pb%d" % ob,) + xgres, w=("x1g",))

    _sidA = nc.enter_named_scope("phaseA", False)[0]
    src_x1g = lambda c: t_x1g[:, c*GRP:(c+1)*GRP]

    def mk_src(t):
        return lambda c: t[:, c*GRP:(c+1)*GRP]

    def mk_evac(t):
        def ev(e, oc, ob):
            return e.scalar_tensor_tensor(t_x1g[:, oc*GRP:(oc+1)*GRP], bk(ob, GRP),
                                          0.5, t[:, oc*GRP:(oc+1)*GRP],
                                          OP.mult, OP.add)
        return ev

    for g in range(ngrp_a):
        t_xg = t_xga if g % 2 == 0 else t_xgb
        xgres = ("xga", "xgah") if g % 2 == 0 else ("xgb", "xgbh")
        src_xg = mk_src(t_xg)

        if g < 2:
            def f_load(e, g=g, t_xg=t_xg):
                return [
                    e.dma_start(out=t_xg[:].rearrange("p (c j) -> p c j", c=8),
                                in_=d_x[:, :, g*GRP:(g+1)*GRP].rearrange("c p j -> p c j")),
                    e.dma_start(out=t_cs[:], in_=d_csk[:, g*512:(g+1)*512]),
                ]
            step("sync", COMPUTE, f_load, dma_sem=s_dma, ndma=2,
                 w=xgres + ("cs",))
        else:
            def f_load(e, g=g, t_xg=t_xg):
                return [
                    e.dma_start(out=t_xg[:, 0:4*GRP].rearrange("p (c j) -> p c j", c=4),
                                in_=d_x[0:4, :, g*GRP:(g+1)*GRP].rearrange("c p j -> p c j")),
                    e.dma_start(out=t_cs[:], in_=d_csk[:, g*512:(g+1)*512]),
                ]
            step("sync", COMPUTE, f_load, dma_sem=s_dma, ndma=2,
                 w=(xgres[0], "cs"))

            def f_loadh(e, g=g, t_xg=t_xg):
                return e.dma_start(
                    out=t_xg[:, 4*GRP:8*GRP].rearrange("p (c j) -> p c j", c=4),
                    in_=d_x[4:8, :, g*GRP:(g+1)*GRP].rearrange("c p j -> p c j"))
            step("gpsimd", COMPUTE, f_loadh, dma_sem=s_dmag, ndma=1,
                 w=(xgres[1],))

        ln_block(src_xg, t_xh, True, xgres, "xh", src_t=t_xg)
        router_block(src_xg, xgres)
        ffn_block(mk_evac(t_xg), xgres)

        s0 = (g % 2) * 128
        s1 = 128 - s0
        def f_blendx(e, g=g, s0=s0, s1=s1):
            def sub(c):
                e.tensor_tensor(t_osp[:, c*128:(c+1)*128], t_x1g[:, c*GRP+s1:c*GRP+s1+128],
                                t_x1g[:, c*GRP+s0:c*GRP+s0+128], OP.subtract)
            def stt(c):
                return e.scalar_tensor_tensor(t_osp[:, c*128:(c+1)*128],
                                              t_osp[:, c*128:(c+1)*128], t_hf[:, 0:1],
                                              t_x1g[:, c*GRP+s0:c*GRP+s0+128],
                                              OP.mult, OP.add)
            sub(0)
            for c in range(7):
                sub(c + 1)
                stt(c)
            return stt(7)
        step("vector", [s_dve], f_blendx, r=("x1g", "hf"), w=("ht",))

        def f_spill_x1(e, g=g):
            return e.dma_start(out=d_x1own[:, :, g*128:(g+1)*128].rearrange("c p j -> p c j"),
                               in_=t_osp[:].rearrange("p (c j) -> p c j", c=8))
        step("sync", [s_dve], f_spill_x1, dma_sem=s_sp, r=("ht",), w=("d_x1own",))

        if "ew" in taps:
            def f_ewtap(e, g=g):
                return [
                    e.dma_start(out=d_ewdbg[0:1, g*1536:(g+1)*1536],
                                in_=t_ewf6[0:1, 0:1536]),
                    e.dma_start(out=d_zdbg[:, g*GRP:(g+1)*GRP], in_=t_z[0:NEXP, 0:GRP]),
                    e.dma_start(out=d_q6dbg[:, g*GRP:(g+1)*GRP], in_=t_q6[0:NEXP, 0:GRP]),
                    e.dma_start(out=d_r6dbg[:, g*16:(g+1)*16], in_=t_r6[:, 0:16]),
                    e.dma_start(out=d_ewtdbg[:, g*16:(g+1)*16], in_=t_ew[:, 0:16]),
                    e.dma_start(out=d_rscdbg[:, g*64:g*64+16], in_=t_rs2[:, 0:16]),
                    e.dma_start(out=d_rscdbg[:, g*64+16:(g+1)*64], in_=t_rsc[:, 0:48]),
                ]
            step("sync", [s_dve], f_ewtap, dma_sem=s_sp, ndma=7)

        ln_block(src_x1g, t_yh, False, ("x1g",), "yh", src_t=t_x1g)

        def f_blendy(e, g=g, s0=s0, s1=s1):
            yb = t_osp[:, 0:512].bitcast(BF16)
            def sub(c):
                e.tensor_tensor(yb[:, c*128:(c+1)*128], t_yh[:, c*GRP+s1:c*GRP+s1+128],
                                t_yh[:, c*GRP+s0:c*GRP+s0+128], OP.subtract)
            def stt(c):
                return e.scalar_tensor_tensor(yb[:, c*128:(c+1)*128],
                                              yb[:, c*128:(c+1)*128], t_hf[:, 0:1],
                                              t_yh[:, c*GRP+s0:c*GRP+s0+128],
                                              OP.mult, OP.add)
            sub(0)
            for c in range(7):
                sub(c + 1)
                stt(c)
            return stt(7)
        step("vector", [s_dve], f_blendy, r=("yh", "hf"), w=("ht",))

        def f_spill_y(e, g=g):
            yb = t_osp[:, 0:512].bitcast(BF16)
            return e.dma_start(out=d_yown[:, :, g*128:(g+1)*128].rearrange("c p j -> p c j"),
                               in_=yb.rearrange("p (c j) -> p c j", c=8))
        step("sync", [s_dve], f_spill_y, dma_sem=s_sp, r=("ht",), w=("d_yown",))

        def f_kva(e):
            r = None
            for c in range(8):
                r = e.matmul(bk(2, GRP, 0, KVL), t_wkva[:, c*KVL:(c+1)*KVL],
                             t_yh[:, c*GRP:(c+1)*GRP], start=(c == 0), stop=(c == 7))
            return r
        step("tensor", [s_dve], f_kva, r=("yh", "wsp"), w=("pb2",))
        step("vector", [s_pe], lambda e: e.tensor_copy(
            t_kva[0:KVL, 0:GRP], bk(2, GRP, 0, KVL)), r=("pb2",), w=("kva",))

        for base, dst, kres in ((0, t_ka, "ka"), (256, t_kb, "kb")):
            def f_k(e, base=base):
                e.matmul(bk(3, 256), t_wkvbk[:, base:base+128], t_kva[0:KVL, 0:GRP],
                         start=True, stop=True)
                return e.matmul(bk(3, 256, off=256), t_wkvbk[:, base+128:base+256],
                                t_kva[0:KVL, 0:GRP], start=True, stop=True)
            step("tensor", [s_dve], f_k, r=("wsp", "kva"), w=("pb3",))

            def f_krope(e, dst=dst, g=g):
                e.tensor_tensor(t_rtmp[:, 0:512], bk(3, 512), t_cs[:, 0:512], OP.mult)
                e.drain()
                return e.tensor_tensor(dst[:, g*GRP:(g+1)*GRP], t_rtmp[:, 0:256],
                                       t_rtmp[:, 256:512], OP.add)
            step("vector", [s_pe], f_krope, r=("pb3", "cs"), w=("rtmp", kres))

        for tl in range(2):
            def f_v(e, tl=tl):
                return e.matmul(bk(2, 256, off=256), t_kva[0:KVL, tl*128:(tl+1)*128],
                                t_wkvbv[0:KVL, 0:256], start=True, stop=True)
            step("tensor", [s_dve], f_v, r=("kva", "wsp"), w=("pb2",))

            def f_vev(e, tl=tl, g=g):
                dstv = t_v[:].rearrange("p (t k j) -> p t k j", t=NB, k=4,
                                        j=65)[:, 2*g+tl, :, 0:64]
                return e.tensor_copy(dstv, bk(2, 256, off=256).rearrange(
                    "p (k j) -> p k j", k=4))
            step("vector", [s_pe], f_vev, r=("pb2",), w=("v",))

    if "kv" in taps:
        step("sync", [s_dve], lambda e: [
            e.dma_start(out=d_kadbg, in_=t_ka[:]),
            e.dma_start(out=d_kbdbg, in_=t_kb[:]),
            e.dma_start(out=d_vdbg, in_=t_v[:]),
        ], dma_sem=s_sp, ndma=3)

    nc.leave_named_scope("phaseA", _sidA, False)
    _sidB = nc.enter_named_scope("phaseB", False)[0]
    barrier()
    for nm_ in ("q", "p0", "p1", "a"):
        if "ht" in writers:
            writers[nm_] = writers["ht"]
        readers[nm_] = list(readers.get("ht", ()))
    def f_woutload(e):
        return [e.dma_start(out=wout_chunk(ec), in_=d_wout[ec])
                for ec in range(8)]
    step("gpsimd", [], f_woutload, dma_sem=s_dmag, ndma=8, w=("wsp",))

    if not do_b:
        nqg = 0
    step("sync", COMPUTE + [s_dma], lambda e: [
        e.dma_start(out=t_cs[:], in_=d_csq[:, 0:512])], dma_sem=s_dma, w=("cs",))

    for i in range(nqg):
        p_ = i % 2
        vx1 = t_xga if p_ == 0 else t_xgb
        xres = "xga" if p_ == 0 else "xgb"
        xresh = "xgah" if p_ == 0 else "xgbh"
        vmk = t_ewb[:, 0:1024] if p_ == 0 else t_xh[:, 0:1024]
        mres = "ewb" if p_ == 0 else "ewb2"
        vcs = t_cs if p_ == 0 else t_xh[:, 1024:1536]
        cres = "cs" if p_ == 0 else "cs2"

        def f_bload(e, i=i, vx1=vx1, vmk=vmk, vcs=vcs):
            r = [
                e.dma_start(
                    out=vx1[:].rearrange("p (c j) -> p c j", c=8),
                    in_=d_x1own[:, :, i*256:(i+1)*256].rearrange("c p j -> p c j")),
                e.dma_start(
                    out=vmk.rearrange("p (t j) -> p t j", t=4),
                    in_=d_maskt[4*i:4*i+4].rearrange("t p j -> p t j")),
            ]
            if i > 0:
                r.append(e.dma_start(out=vcs[:, 0:512], in_=d_csq[:, i*512:(i+1)*512]))
            return r
        step("sync", COMPUTE, f_bload, dma_sem=s_dma, est=6000,
             ndma=3 if i > 0 else 2, r=("d_x1own",),
             w=(xres, xresh, mres) + ((cres,) if i > 0 else ()))

        def f_bloady(e, i=i):
            return e.dma_start(
                out=v_yb[:].rearrange("p (c j) -> p c j", c=8),
                in_=d_yown[:, :, i*256:(i+1)*256].rearrange("c p j -> p c j"))
        step("gpsimd", COMPUTE, f_bloady, dma_sem=s_dmag, ndma=1,
             r=("d_yown",), w=("yh",), est=5000)

        def f_qa(e):
            r = None
            for c in range(8):
                r = e.matmul(bk(0, 256, 0, QL), t_wqa[:, c*QL:(c+1)*QL],
                             v_yb[:, c*256:(c+1)*256], start=(c == 0), stop=(c == 7))
            return r
        step("tensor", [s_dma], f_qa, r=("yh", "wsp"), w=("pb0",))
        step("vector", [s_pe], lambda e: e.tensor_copy(t_qa[0:QL, 0:256],
                                                       bk(0, 256, 0, QL)),
             r=("pb0",), w=("qa",))

        for c in range(8):
            def f_qc(e, c=c):
                e.matmul(bk(1, 256), t_wqb[0:QL, c*128:(c+1)*128], t_qa[0:QL, 0:256],
                         start=True, stop=True)
                return e.matmul(bk(1, 256, off=256),
                                t_wqb[0:QL, 1024 + c*128: 1024 + (c+1)*128],
                                t_qa[0:QL, 0:256], start=True, stop=True)
            step("tensor", [s_dve], f_qc, r=("wsp", "qa"), w=("pb1",))

            def f_qrope(e, c=c, vcs=vcs):
                e.tensor_tensor(t_rtmp[:, 0:512], bk(1, 512), vcs[:, 0:512], OP.mult)
                e.drain()
                return e.tensor_tensor(v_q[:, c*256:(c+1)*256], t_rtmp[:, 0:256],
                                       t_rtmp[:, 256:512], OP.add)
            step("vector", [s_pe], f_qrope, r=("pb1", cres), w=("rtmp", "q"))

        ktmax = 4 * i + 4
        for cp in range(4):
            Ksrc = t_ka if cp < 2 else t_kb
            kvlo = cp // 2
            kvhi = kvlo + 2

            def qk(e, kt, cp=cp, Ksrc=Ksrc):
                sb = (kt % 2) * 2
                e.matmul(bk(sb, 256), Ksrc[0:64, kt*128:(kt+1)*128],
                         v_q[0:64, (2*cp)*256:(2*cp+1)*256], start=True, stop=True,
                         tile_position=(0, 0))
                e.matmul(bk(sb, 256, off=256), Ksrc[0:64, kt*128:(kt+1)*128],
                         v_q[0:64, (2*cp+1)*256:(2*cp+2)*256], start=True, stop=True,
                         tile_position=(0, 0))
                e.matmul(bk(sb + 1, 256), Ksrc[64:128, kt*128:(kt+1)*128],
                         v_q[64:128, (2*cp)*256:(2*cp+1)*256], start=True, stop=True,
                         tile_position=(64, 0))
                return e.matmul(bk(sb + 1, 256, off=256), Ksrc[64:128, kt*128:(kt+1)*128],
                                v_q[64:128, (2*cp+1)*256:(2*cp+2)*256], start=True,
                                stop=True, tile_position=(64, 0))

            def sres(kt):
                return ("pb0", "pb1") if kt % 2 == 0 else ("pb2", "pb3")

            def f_tzero(e):
                e.memset(psA[0:128, 4*512:5*512], 0.0)
                return e.memset(psA[0:128, 5*512:6*512], 0.0)
            step("vector", [], f_tzero, w=("pb4", "pb5"))

            step("tensor", [], lambda e, qk=qk: qk(e, 0),
                 r=("q", "ka", "kb"), w=sres(0))
            if ktmax > 1:
                step("tensor", [], lambda e, qk=qk: qk(e, 1),
                     r=("q", "ka", "kb"), w=sres(1))

            for kt in range(ktmax):
                pb = kt % 2
                step("scalar", [], lambda e, pb=pb: e.activation(
                    v_ps[pb][:, 0:1024], psA[0:128, pb*1024:(pb+1)*1024], ACT.Exp,
                    scale=SCALE),
                     r=sres(kt), w=("p%d" % pb,))

                if kt >= 4 * i:
                    j = kt - 4 * i

                    def f_mask(e, j=j, pb=pb, vmk=vmk):
                        mb = vmk[:, j*256:(j+1)*256]
                        bc4 = mb.unsqueeze(1).to_broadcast([128, 4, 256])
                        pv = v_ps[pb][:, 0:1024].rearrange("p (s j) -> p s j", s=4)
                        return e.tensor_tensor(pv, pv, bc4, OP.mult)
                    step("vector", [], f_mask, r=("p%d" % pb, mres), w=("p%d" % pb,))

                def f_av(e, kt=kt, cp=cp, kvlo=kvlo, kvhi=kvhi, qk=qk, pb=pb,
                         last=(kt == ktmax - 1), pre=(kt + 2 < ktmax)):
                    st = (kt == 0)
                    sp_ = last
                    vp = v_ps[pb]
                    e.matmul(psA[0:64, 4*512: 4*512+256],
                             t_v[:, kt*260 + kvlo*65: kt*260 + kvlo*65 + 64],
                             vp[:, 0:256], start=False, stop=sp_, tile_position=(0, 0),
                             skip_group_check=True)
                    e.matmul(psA[64:128, 4*512: 4*512+256],
                             t_v[:, kt*260 + kvlo*65: kt*260 + kvlo*65 + 64],
                             vp[:, 256:512], start=False, stop=sp_, tile_position=(0, 64),
                             skip_group_check=True)
                    e.matmul(psA[0:64, 5*512: 5*512+256],
                             t_v[:, kt*260 + kvhi*65: kt*260 + kvhi*65 + 64],
                             vp[:, 512:768], start=False, stop=sp_, tile_position=(0, 0),
                             skip_group_check=True)
                    e.matmul(psA[64:128, 5*512: 5*512+256],
                             t_v[:, kt*260 + kvhi*65: kt*260 + kvhi*65 + 64],
                             vp[:, 768:1024], start=False, stop=sp_, tile_position=(0, 64),
                             skip_group_check=True)
                    e.matmul(bk(6, 512, 0, 1), t_onescb[:, 0:1], vp[:, 0:512],
                             start=st, stop=sp_)
                    r = e.matmul(bk(7, 512, 0, 1), t_onescb[:, 0:1], vp[:, 512:1024],
                                 start=st, stop=sp_)
                    if pre:
                        r = qk(e, kt + 2)
                    return r
                avw = ("pb4", "pb5", "pb6", "pb7") + (sres(kt) if kt + 2 < ktmax else ())
                step("tensor", [], f_av, r=("p%d" % pb, "v", "consts", "q", "ka", "kb"),
                     w=avw)

            def f_rc(e):
                return e.reciprocal(t_rc[0:1, 0:1024], psA[0:1, 6*512:8*512])
            step("vector", [], f_rc, r=("pb6", "pb7"), w=("stat",))

            def f_rb(e):
                e.matmul(bk(6, 256, 0, 64), t_onesr[0:1, 0:64], t_rc[0:1, 0:256],
                         start=True, stop=True, tile_position=(0, 0))
                e.matmul(bk(6, 256, 64, 128), t_onesr[0:1, 0:64], t_rc[0:1, 256:512],
                         start=True, stop=True, tile_position=(0, 64))
                e.matmul(bk(6, 256, 0, 64, off=256), t_onesr[0:1, 0:64],
                         t_rc[0:1, 512:768], start=True, stop=True, tile_position=(0, 0))
                return e.matmul(bk(6, 256, 64, 128, off=256), t_onesr[0:1, 0:64],
                                t_rc[0:1, 768:1024], start=True, stop=True,
                                tile_position=(0, 64))
            step("tensor", [], f_rb, r=("stat", "consts"), w=("pb6",))

            def f_norm(e, cp=cp):
                e.tensor_copy(v_rb[:, 0:512], bk(6, 512))
                e.drain()
                e.tensor_tensor(v_a[0:64, (2*cp)*256:(2*cp+1)*256],
                                psA[0:64, 4*512: 4*512+256], v_rb[0:64, 0:256], OP.mult)
                e.tensor_tensor(v_a[64:128, (2*cp)*256:(2*cp+1)*256],
                                psA[64:128, 4*512: 4*512+256], v_rb[64:128, 0:256],
                                OP.mult)
                e.tensor_tensor(v_a[0:64, (2*cp+1)*256:(2*cp+2)*256],
                                psA[0:64, 5*512: 5*512+256], v_rb[0:64, 256:512],
                                OP.mult)
                return e.tensor_tensor(v_a[64:128, (2*cp+1)*256:(2*cp+2)*256],
                                       psA[64:128, 5*512: 5*512+256],
                                       v_rb[64:128, 256:512], OP.mult)
            step("vector", [], f_norm,
                 r=("pb6", "pb4", "pb5"), w=("rtmp", "a"))

        for oc in range(8):
            wbk = 1 if oc % 2 == 0 else 0
            wres = "pb1" if oc % 2 == 0 else "pb0"

            def f_wo(e, oc=oc, wbk=wbk):
                r = None
                for ec in range(8):
                    r = e.matmul(bk(wbk, 256),
                                 wout_chunk(ec)[:, oc*128:(oc+1)*128],
                                 v_a[:, ec*256:(ec+1)*256], start=(ec == 0), stop=(ec == 7),
                                 skip_group_check=True)
                return r
            step("tensor", [s_dve], f_wo, r=("a", "wsp"), w=(wres,))

            def f_x2(e, oc=oc, vx1=vx1, wbk=wbk):
                return e.scalar_tensor_tensor(
                    t_x1g[:, oc*256:(oc+1)*256], bk(wbk, 256),
                    t_bout[:, oc:oc+1],
                    vx1[:, oc*256:(oc+1)*256], OP.add, OP.add)
            step("vector", [s_pe], f_x2, r=(wres, xres, xresh, "bout"), w=("x1g",))

        def f_x2sp(e, i=i):
            return e.dma_start(out=d_x2sp[:, :, i*256:(i+1)*256].rearrange("c p j -> p c j"),
                               in_=t_x1g[:].rearrange("p (c j) -> p c j", c=8))
        step("sync", [s_dve], f_x2sp, dma_sem=s_sp, r=("x1g",), w=("x2sp%d" % i,))

    nc.leave_named_scope("phaseB", _sidB, False)
    _sidC = nc.enter_named_scope("phaseC", False)[0]
    for dst_, srcs_ in (("ht", ("q", "p0", "p1", "a")),
                        ("xh", ("ewb2", "cs2"))):
        for s__ in srcs_:
            if s__ in writers:
                readers.setdefault(dst_, []).append(writers[s__])
            readers.setdefault(dst_, []).extend(readers.get(s__, ()))
    def f_wreload(e):
        r = []
        for x_ in range(NEXP):
            r.append(e.dma_start(
                out=t_w1[x_][:, 0:1024].rearrange("p (c j) -> p c j", c=2),
                in_=d_w1[x_][0:2].rearrange("c p j -> p c j")))
        for x_ in range(2):
            r.append(e.dma_start(
                out=t_w2[x_][:, 0:1024].rearrange("p (c j) -> p c j", c=1),
                in_=d_w2[x_][0:1].rearrange("c p j -> p c j")))
        return r
    step("gpsimd", [], f_wreload, dma_sem=s_dmag, ndma=8, w=("wsp",))

    for j in range(NQG if do_c else 0):
        t_xg = t_xga if j % 2 == 0 else t_xgb
        xgres = ("xga", "xgah") if j % 2 == 0 else ("xgb", "xgbh")
        src_xg = mk_src(t_xg)

        if j == 0:
            def f_cload(e, j=j, t_xg=t_xg):
                return [e.dma_start(out=t_xg[:].rearrange("p (c j) -> p c j", c=8),
                                    in_=d_x2sp[:, :, j*GRP:(j+1)*GRP].rearrange("c p j -> p c j"))]
            step("sync", COMPUTE + [s_sp], f_cload, dma_sem=s_dma,
                 r=("x2sp%d" % j,), w=xgres)
        else:
            def f_cload(e, j=j, t_xg=t_xg):
                return [e.dma_start(
                    out=t_xg[:, 0:4*GRP].rearrange("p (c j) -> p c j", c=4),
                    in_=d_x2sp[0:4, :, j*GRP:(j+1)*GRP].rearrange("c p j -> p c j"))]
            step("sync", COMPUTE + [s_sp], f_cload, dma_sem=s_dma,
                 r=("x2sp%d" % j,), w=(xgres[0],))

            def f_cloadh(e, j=j, t_xg=t_xg):
                return e.dma_start(
                    out=t_xg[:, 4*GRP:8*GRP].rearrange("p (c j) -> p c j", c=4),
                    in_=d_x2sp[4:8, :, j*GRP:(j+1)*GRP].rearrange("c p j -> p c j"))
            step("gpsimd", COMPUTE + [s_sp], f_cloadh, dma_sem=s_dmag, ndma=1,
                 r=("x2sp%d" % j,), w=(xgres[1],))

        ln_block(src_xg, t_xh, True, xgres, "xh", src_t=t_xg)
        router_block(src_xg, xgres)
        ffn_block(mk_evac(t_xg), xgres)

        def f_store(e, j=j):
            return e.dma_start(out=d_out[:, :, j*GRP:(j+1)*GRP].rearrange("c p j -> p c j"),
                               in_=t_x1g[:].rearrange("p (c j) -> p c j", c=8))
        step("sync", [s_dve], f_store, dma_sem=s_sp, r=("x1g",), w=("d_out",))

    nc.leave_named_scope("phaseC", _sidC, False)
    flush_plan()
    _sp_total = sum(s.amt for s in all_steps if s.sem is s_sp)

    def f_final(e):
        e.wait_ge(s_sp, _sp_total)
    blk.sync(f_final)

    es.close()
    return nc



def kernel(**inputs):
    in_maps, metas = host_prepare(inputs)
    nc = build_program()
    res = run_bass_kernel_spmd(nc, in_maps, core_ids=list(range(8)))
    out = np.zeros((4, S, E), dtype=np.float32)
    for core in range(8):
        b, own = metas[core]
        o = res.results[core]["out"].reshape(E, OWN)
        for j, g in enumerate(own):
            out[b, g*128:(g+1)*128, :] = o[:, j*128:(j+1)*128].T
    return out


# version 21
# speedup vs baseline: 1.0299x; 1.0136x over previous
import numpy as np
import ml_dtypes
import concourse.bass as bass
import concourse.mybir as mybir
from concourse.bass_utils import run_bass_kernel_spmd

F32 = mybir.dt.float32
BF16 = mybir.dt.bfloat16
I32 = mybir.dt.int32
OP = mybir.AluOpType
ACT = mybir.ActivationFunctionType
AX = mybir.AxisListType.X
BF = ml_dtypes.bfloat16

E, S, NB = 1024, 4096, 32
H, KVH, D, QL, KVL = 16, 4, 64, 64, 32
NEXP, DEXP = 6, 512
EPS = 1e-5
GRP = 256
NGRP = S // GRP
NQG = 8
OWN = 2048
SCALE = 0.125
QTEMP_INV = 1.0 / 0.3
STEP_LIMIT = [10**9]
SCHED_DUMP = []
PRED = [0.0]


def own_blocks(half):
    keep = (0, 3) if half == 0 else (1, 2)
    return [g for g in range(NB) if g % 4 in keep]



def _swap64(Wcols):
    r = Wcols.reshape(Wcols.shape[0], -1, 32, 2)
    return np.ascontiguousarray(r[..., ::-1].reshape(Wcols.shape))


def _rope_rows():
    inv = 1.0 / (10000.0 ** (np.arange(0, D, 2, dtype=np.float64) / D))
    ang = np.outer(inv, np.arange(S, dtype=np.float64))
    cos = np.repeat(np.cos(ang), 2, axis=0)
    sin = np.repeat(np.sin(ang), 2, axis=0).copy()
    sin[0::2] *= -1.0
    C = np.concatenate([cos, cos], 0).astype(np.float32)
    Sg = np.concatenate([sin, sin], 0).astype(np.float32)
    return C, Sg


def host_prepare(inputs):
    x = np.asarray(inputs["L0_attn_x"], dtype=np.float32)
    Wqa = np.asarray(inputs["L0_attn_Wqa"], np.float32)
    Wqb = np.asarray(inputs["L0_attn_Wqb"], np.float32)
    Wkva = np.asarray(inputs["L0_attn_Wkva"], np.float32)
    Wkvb = np.asarray(inputs["L0_attn_Wkvb"], np.float32)
    Wout = np.asarray(inputs["L0_attn_Wout"], np.float32)
    bout = np.asarray(inputs["L0_attn_bout"], np.float32)
    projw = np.asarray(inputs["L0_ffn_proj_w"], np.float32)
    anch = np.asarray(inputs["L0_ffn_anchors"], np.float32)
    temp = np.asarray(inputs["L0_ffn_routing_temp"], np.float32).reshape(1, 1)
    w1 = np.asarray(inputs["L0_ffn_w1"], np.float32)
    w2 = np.asarray(inputs["L0_ffn_w2"], np.float32)

    sh = {}
    sh["w1r"] = np.ascontiguousarray(w1.reshape(NEXP, 8, 128, DEXP))
    sh["w2r"] = np.ascontiguousarray(w2.reshape(NEXP, 4, 128, E))
    sh["wqa"] = np.ascontiguousarray(Wqa.reshape(8, 128, QL))
    WqbP = np.empty_like(Wqb)
    for c in range(8):
        WqbP[:, c*128:c*128+64] = Wqb[:, c*64:(c+1)*64]
        WqbP[:, c*128+64:(c+1)*128] = Wqb[:, (c+8)*64:(c+9)*64]
    sh["wqb"] = np.ascontiguousarray(np.concatenate([WqbP, _swap64(WqbP)], 1))
    sh["wkva"] = np.ascontiguousarray(Wkva.reshape(8, 128, KVL))
    kvr = Wkvb.reshape(KVL, KVH, D, 2)
    WK = np.ascontiguousarray(kvr[..., 0].reshape(KVL, KVH * D))
    WV = np.ascontiguousarray(kvr[..., 1].reshape(KVL, KVH * D))
    WKs = _swap64(WK)
    sh["wkvbk"] = np.ascontiguousarray(np.concatenate(
        [WK[:, 0:64], WK[:, 128:192],
         WKs[:, 0:64], WKs[:, 128:192],
         WK[:, 64:128], WK[:, 192:256],
         WKs[:, 64:128], WKs[:, 192:256]], 1))
    sh["wkvbv"] = WV
    wop = np.empty_like(Wout)
    for j, oc in enumerate([0, 4, 1, 5, 2, 6, 3, 7]):
        wop[j*128:(j+1)*128] = Wout[oc*128:(oc+1)*128]
    sh["wout"] = np.ascontiguousarray(wop.reshape(8, 128, E))
    sh["bout"] = np.ascontiguousarray(bout.reshape(8, 128).T)
    sh["projw"] = np.ascontiguousarray(
        projw.T.reshape(8, 128, NEXP).transpose(1, 0, 2).reshape(128, 8 * NEXP))
    sh["anchT"] = np.ascontiguousarray(anch.T)
    sh["rtemp"] = temp
    sh["ident"] = np.eye(128, dtype=np.float32)

    C, Sg = _rope_rows()
    csk = np.zeros((128, 2 * S), dtype=np.float32)
    for g in range(NGRP):
        csk[:, g*512:g*512+256] = C[:, g*256:(g+1)*256]
        csk[:, g*512+256:(g+1)*512] = Sg[:, g*256:(g+1)*256]
    sh["csk"] = csk.astype(BF)

    in_maps, metas = [], []
    for core in range(8):
        b, half = core // 2, core % 2
        own = own_blocks(half)
        m = dict(sh)
        m["x"] = np.ascontiguousarray(x[b].T.reshape(8, 128, S))
        m["coreinfof"] = np.array([[half]], dtype=np.float32)
        csq = np.zeros((128, 2 * OWN), dtype=np.float32)
        for i in range(NQG):
            cols = list(range(own[2*i]*128, own[2*i]*128+128)) + \
                   list(range(own[2*i+1]*128, own[2*i+1]*128+128))
            csq[:, i*512:i*512+256] = C[:, cols]
            csq[:, i*512+256:(i+1)*512] = Sg[:, cols]
        m["csq"] = csq.astype(BF)
        mt = np.zeros((NQG * 4, 128, 256), dtype=np.float32)
        kk = np.arange(128)[:, None]
        qq = np.arange(128)[None, :]
        for i in range(NQG):
            g0, g1 = own[2*i], own[2*i+1]
            for j in range(4):
                kt = 4 * i + j
                for bi, g in ((0, g0), (1, g1)):
                    mt[4*i+j, :, bi*128:(bi+1)*128] = (kt*128 + kk <= g*128 + qq)
        m["maskt"] = mt.astype(BF)
        in_maps.append(m)
        metas.append((b, own))
    return in_maps, metas



class _Counts:
    def __init__(self):
        self.c = {}

    def inc(self, instr, semh, amt=1):
        instr.then_inc(semh, amt)
        self.c[semh.name] = self.c.get(semh.name, 0) + amt

    def val(self, semh):
        return self.c.get(semh.name, 0)


def _shift_ap(ap, delta):
    return bass.AP(ap.tensor, ap.offset + delta, ap.ap)


def build_program(taps=(), ngrp_a=NGRP, do_b=True, do_c=True, nqg=NQG, nc_only=False):
    nc = bass.Bass("TRN2", target_bir_lowering=False, dynamic_dma_scratch_size=16384)
    nt = _Counts()

    def dram(n, s, d, k="ExternalInput"):
        return nc.dram_tensor(n, s, d, kind=k).ap()

    d_x = dram("x", [8, 128, S], F32)
    d_w1 = dram("w1r", [NEXP, 8, 128, DEXP], F32)
    d_w2 = dram("w2r", [NEXP, 4, 128, E], F32)
    d_wqa = dram("wqa", [8, 128, QL], F32)
    d_wqb = dram("wqb", [QL, 2048], F32)
    d_wkva = dram("wkva", [8, 128, KVL], F32)
    d_wkvbk = dram("wkvbk", [KVL, 512], F32)
    d_wkvbv = dram("wkvbv", [KVL, 256], F32)
    d_wout = dram("wout", [8, 128, E], F32)
    d_bout = dram("bout", [128, 8], F32)
    d_projw = dram("projw", [128, 8 * NEXP], F32)
    d_anchT = dram("anchT", [NEXP, NEXP], F32)
    d_rtemp = dram("rtemp", [1, 1], F32)
    d_ident = dram("ident", [128, 128], F32)
    d_csk = dram("csk", [128, 2 * S], BF16)
    d_csq = dram("csq", [128, 2 * OWN], BF16)
    d_maskt = dram("maskt", [NQG * 4, 128, 256], BF16)
    d_cinfo = dram("coreinfof", [1, 1], F32)
    d_out = dram("out", [8, 128, OWN], F32, "ExternalOutput")

    spk = "ExternalOutput" if "spill" in taps else "Internal"
    d_x1own = dram("x1own", [8, 128, OWN], F32, spk)
    d_yown = dram("yown", [8, 128, OWN], BF16, spk)
    x2k = "ExternalOutput" if "x2" in taps else "Internal"
    d_x2sp = dram("x2sp", [8, 128, OWN], F32, x2k)
    if "kv" in taps:
        d_kadbg = dram("kadbg", [128, S], BF16, "ExternalOutput")
        d_kbdbg = dram("kbdbg", [128, S], BF16, "ExternalOutput")
        d_vdbg = dram("vdbg", [128, NB * 260], BF16, "ExternalOutput")
    if "ew" in taps:
        d_ewdbg = dram("ewdbg", [1, NGRP * 1536], F32, "ExternalOutput")
        d_zdbg = dram("zdbg", [NEXP, S], F32, "ExternalOutput")
        d_r6dbg = dram("r6dbg", [128, 16 * NGRP], F32, "ExternalOutput")
        d_ewtdbg = dram("ewtdbg", [128, 16 * NGRP], F32, "ExternalOutput")
        d_q6dbg = dram("q6dbg", [NEXP, S], F32, "ExternalOutput")
        d_rscdbg = dram("rscdbg", [128, 64 * NGRP], F32, "ExternalOutput")

    from contextlib import ExitStack
    es = ExitStack()

    def sbt(n, s, d):
        return es.enter_context(nc.sbuf_tensor(n, s, d))

    def mksem(n):
        return es.enter_context(nc.semaphore(n))

    t_w1 = [sbt(f"w1_{e_}", [128, 8 * DEXP], BF16) for e_ in range(NEXP)]
    t_w2 = [sbt(f"w2_{e_}", [128, 4 * E], BF16) for e_ in range(NEXP)]
    t_wqa = sbt("wqas", [128, 8 * QL], BF16)
    t_wqb = sbt("wqbs", [QL, 2048], BF16)
    t_wkva = sbt("wkvas", [128, 8 * KVL], BF16)
    t_wkvbk = sbt("wkvbks", [KVL, 512], BF16)
    t_wkvbv = sbt("wkvbvs", [KVL, 256], BF16)
    t_bout = sbt("bouts", [128, 8], F32)
    t_projw = sbt("projws", [128, 8 * NEXP], F32)
    t_anchT = sbt("anchTs", [NEXP, NEXP], F32)
    t_ident = sbt("idents", [128, 128], F32)
    t_ka = sbt("ka", [128, S], BF16)
    t_kb = sbt("kb", [128, S], BF16)
    t_v = sbt("v", [128, NB * 260], BF16)
    t_onesr = sbt("onesr", [128, 128], F32)
    t_onesm = sbt("onesm", [1, 128], F32)
    t_onescol = sbt("onescol", [128, 1], F32)
    t_eps = sbt("epsc", [128, 1], F32)
    t_onescb = sbt("onescb", [128, 1], BF16)
    t_pw = sbt("pwsum", [1, NEXP], F32)
    t_alpha = sbt("alpha", [128, 1], F32)
    t_alpham = sbt("alpham", [128, 1], F32)
    t_rt = sbt("rtemps", [1, 1], F32)
    t_ci = sbt("cis", [1, 1], F32)
    t_hf = sbt("hflag", [128, 1], F32)

    def wout_chunk(ec):
        t = t_w1[ec] if ec < NEXP else t_w2[ec - NEXP]
        return t[:, 0:E]

    t_xga = sbt("xga", [128, 8 * GRP], F32)
    t_xgb = sbt("xgb", [128, 8 * GRP], F32)
    t_xsqp = sbt("xsqp", [128, 1024], BF16)
    t_lnbc = sbt("lnbc", [128, 512], BF16)
    t_xh = sbt("xh", [128, 8 * GRP], BF16)
    t_stat = sbt("stat", [1, 1536], F32)
    t_z = sbt("z", [NEXP, GRP], F32)
    t_q6 = sbt("q6", [NEXP, GRP], F32)
    t_r6 = sbt("r6", [128, 16], F32)
    t_rsc = sbt("rsc", [128, 64], F32)
    t_rs2 = sbt("rs2", [128, 16], F32)
    t_ew = sbt("ew", [128, 16], F32)
    t_ewf6 = sbt("ewf6", [1, NEXP * GRP], F32)
    t_ewb = sbt("ewb", [128, NEXP * GRP], BF16)
    t_h = sbt("h", [128, 4 * GRP], BF16)
    t_ht = sbt("ht", [128, NEXP * 4 * GRP], BF16)
    t_x1g = sbt("x1g", [128, 8 * GRP], F32)
    t_yh = sbt("yhg", [128, 8 * GRP], BF16)
    t_kva = sbt("kvag", [KVL, GRP], BF16)
    t_cs = sbt("cs", [128, 512], BF16)
    t_rtmp = sbt("rtmp", [128, 512], BF16)
    t_qa = sbt("qag", [QL, 256], BF16)

    t_xsq = t_ht[:, 0:4096].bitcast(F32)
    t_rc = t_stat[0:1, 0:1024]
    t_osp = t_ht[:, 0:2048].bitcast(F32)
    v_q = t_ht[:, 0:2048]
    v_ps = (t_ht[:, 2048:3072], t_ht[:, 3072:4096])
    v_a = t_ht[:, 4096:6144]
    v_rb = t_rtmp
    v_masks = t_ewb[:, 0:1024]
    v_x1b = t_xga
    v_yb = t_yh

    psA = es.enter_context(nc.psum_tensor("arena", [128, 4096], F32))

    def bk(i, w=512, p0=0, p1=128, off=0):
        return psA[p0:p1, i * 512 + off: i * 512 + off + w]

    s_dma = mksem("s_dma")
    s_dmag = mksem("s_dmag")
    s_sp = mksem("s_sp")
    s_pe = mksem("s_pe")
    s_act = mksem("s_act")
    s_dve = mksem("s_dve")
    s_fen = mksem("s_fen")
    fenc = [0]

    blk = es.enter_context(nc.Block())
    SEM_OF = {"tensor": s_pe, "scalar": s_act, "vector": s_dve}
    COMPUTE = [s_pe, s_act, s_dve]
    ALLS = [s_dma, s_dmag, s_sp, s_pe, s_act, s_dve]

    stepno = [0]
    writers = {}
    readers = {}
    SEMS = {s.name: s for s in ALLS}
    all_steps = []
    epoch = [0]

    class _Step:
        __slots__ = ("sid", "eng", "fn", "dma_sem", "ndma", "deps", "est",
                     "epoch", "sem", "amt", "cnt", "start", "finish")

        def __init__(s, sid, eng, fn, dma_sem, ndma, deps, est, ep):
            s.sid, s.eng, s.fn, s.dma_sem, s.ndma = sid, eng, fn, dma_sem, ndma
            s.deps, s.est, s.epoch = deps, est, ep
            s.sem = dma_sem if dma_sem is not None else SEM_OF[eng]
            s.amt = 16 * ndma if dma_sem is not None else 1

    EST = {"tensor": 1500, "scalar": 1200, "vector": 800, "sync": 4000,
           "gpsimd": 4000}
    import os
    SCHED_MODE = [os.environ.get("KSCHED", "list")]
    last_on_sem = {}
    bar_deps = [{}]

    def step(eng, deps_unused, fn, dma_sem=None, ndma=1, r=(), w=(), est=None):
        stepno[0] += 1
        if stepno[0] > STEP_LIMIT[0]:
            return
        annotated = bool(r) or bool(w)
        deps = set(bar_deps[0].values())
        if annotated:
            for res in r:
                if res in writers:
                    deps.add(writers[res])
            for res in w:
                if res in writers:
                    deps.add(writers[res])
                deps.update(readers.get(res, ()))
        else:
            deps.update(last_on_sem.values())
        st = _Step(stepno[0], eng, fn, dma_sem, ndma, deps,
                   est if est is not None else EST[eng], epoch[0])
        all_steps.append(st)
        last_on_sem[st.sem.name] = st
        if annotated:
            for res in r:
                readers.setdefault(res, []).append(st)
            for res in w:
                writers[res] = st
                readers[res] = []
        else:
            for res in set(list(writers) + list(readers)):
                writers[res] = st
                readers[res] = []
        return st

    def barrier():
        epoch[0] += 1
        bar_deps[0] = dict(last_on_sem)

    def flush_plan():
        import heapq
        order = {e: [] for e in plan_engines}
        eng_free = {e: 0.0 for e in plan_engines}
        maxep = max((s.epoch for s in all_steps), default=0)
        ep_end = 0.0
        mode = SCHED_MODE[0]
        for ep in range(maxep + 1):
            steps_ = [s for s in all_steps if s.epoch == ep]
            if not steps_:
                continue
            indeg = {}
            for s in steps_:
                indeg[s] = sum(1 for d in s.deps if d.epoch == ep)
            succs = {s: [] for s in steps_}
            for s in steps_:
                for d in s.deps:
                    if d.epoch == ep:
                        succs[d].append(s)
            lp = {}
            for s in sorted(steps_, key=lambda t: -t.sid):
                lp[s] = s.est + max((lp[t] for t in succs[s]), default=0.0)

            def key(s, rt):
                if mode == 'fifo':
                    return s.sid
                if mode == 'cp':
                    return (rt, -lp[s])
                if mode == 'cpb':
                    return (int(rt // 2000), -lp[s])
                return rt
            ready = []
            for s in steps_:
                if indeg[s] == 0:
                    rt = max([ep_end] + [d.finish for d in s.deps if d.epoch < ep])
                    heapq.heappush(ready, (key(s, rt), s.sid, s))
            scheduled = 0
            while ready:
                _, _, s = heapq.heappop(ready)
                rt = max([ep_end] + [d.finish for d in s.deps if d.epoch == ep] +
                         [d.finish for d in s.deps if d.epoch < ep])
                start = max(rt, eng_free[s.eng])
                s.start = start
                s.finish = start + s.est
                eng_free[s.eng] = s.finish
                order[s.eng].append(s)
                scheduled += 1
                for t in succs[s]:
                    indeg[t] -= 1
                    if indeg[t] == 0:
                        rt2 = max([ep_end] + [d.finish for d in t.deps])
                        heapq.heappush(ready, (key(t, rt2), t.sid, t))
            assert scheduled == len(steps_), (scheduled, len(steps_))
            ep_end = max(eng_free.values())
        PRED[:] = [ep_end]
        sem_eng = {}
        for e in plan_engines:
            c = {}
            for s in order[e]:
                assert sem_eng.setdefault(s.sem.name, e) == e, s.sem.name
                c[s.sem.name] = c.get(s.sem.name, 0) + s.amt
                s.cnt = c[s.sem.name]
        SCHED_DUMP.clear()
        for e in plan_engines:
            for pos, s in enumerate(order[e]):
                SCHED_DUMP.append((e, pos, s.sid, s.sem.name, s.cnt,
                                   sorted((d.eng, d.sid, d.sem.name, d.cnt)
                                          for d in s.deps)))
        for e in plan_engines:
            if not order[e]:
                continue

            def body(eng_h, steps_=order[e]):
                hw = {}
                for s in steps_:
                    dd = {}
                    for d in s.deps:
                        sn = d.sem.name
                        if dd.get(sn, 0) < d.cnt:
                            dd[sn] = d.cnt
                    for sn, c in sorted(dd.items()):
                        if hw.get(sn, 0) < c:
                            eng_h.wait_ge(SEMS[sn], c)
                            hw[sn] = c
                    rr = s.fn(eng_h)
                    if s.dma_sem is not None:
                        lst = rr if isinstance(rr, (list, tuple)) else [rr]
                        assert len(lst) == s.ndma, (len(lst), s.ndma)
                        for ins in lst:
                            ins.then_inc(s.dma_sem, 16)
                    else:
                        rr.then_inc(s.sem, 1)
            getattr(blk, e)(body)

    plan_engines = ("sync", "gpsimd", "tensor", "scalar", "vector")


    step("sync", [], lambda e: [
        e.dma_start(out=t_ident[:], in_=d_ident),
        e.dma_start(out=t_anchT[:], in_=d_anchT),
        e.dma_start(out=t_bout[:], in_=d_bout),
        e.dma_start(out=t_projw[:], in_=d_projw),
        e.dma_start(out=t_rt[:], in_=d_rtemp),
        e.dma_start(out=t_ci[:], in_=d_cinfo),
    ], dma_sem=s_dma, ndma=6, w=("ident", "anchT", "bout", "projw", "smallw"))

    def f_init_gp(e):
        r = []
        for x_ in range(NEXP):
            r.append(e.dma_start(out=t_w1[x_][:].rearrange("p (c j) -> p c j", c=8), in_=d_w1[x_].rearrange("c p j -> p c j")))
            r.append(e.dma_start(out=t_w2[x_][:].rearrange("p (c j) -> p c j", c=4), in_=d_w2[x_].rearrange("c p j -> p c j")))
        r.append(e.dma_start(out=t_wqa[:].rearrange("p (c j) -> p c j", c=8), in_=d_wqa.rearrange("c p j -> p c j")))
        r.append(e.dma_start(out=t_wqb[:], in_=d_wqb))
        r.append(e.dma_start(out=t_wkva[:].rearrange("p (c j) -> p c j", c=8), in_=d_wkva.rearrange("c p j -> p c j")))
        r.append(e.dma_start(out=t_wkvbk[:], in_=d_wkvbk))
        r.append(e.dma_start(out=t_wkvbv[:], in_=d_wkvbv))
        return r
    step("gpsimd", [], f_init_gp, dma_sem=s_dmag, ndma=17, w=("wsp",))

    def f_init_dve(e):
        e.memset(t_onesr[:], 1.0)
        e.memset(t_onesm[:], -1.0)
        e.memset(t_onescol[:], 1.0)
        e.memset(t_eps[:], EPS)
        e.memset(t_onescb[:], 1.0)
        vones = t_v[:].rearrange("p (t k j) -> p t k j", t=NB, k=4)[:, :, :, 64:65]
        return e.memset(vones, 1.0)
    step("vector", [], f_init_dve, w=("consts", "v"))

    def f_init_pe(e):
        for c in range(8):
            e.matmul(bk(0, NEXP, 0, 1), t_onescol[:, 0:1],
                     t_projw[:, c*NEXP:(c+1)*NEXP], start=(c == 0), stop=(c == 7))
        e.matmul(bk(1, 1), t_onesr[0:1, 0:128], t_rt[0:1, 0:1],
                 start=True, stop=True)
        return e.matmul(bk(1, 1, off=1), t_onesr[0:1, 0:128], t_ci[0:1, 0:1],
                        start=True, stop=True)
    step("tensor", [s_dma, s_dmag], f_init_pe, r=("projw", "consts", "smallw"), w=("pb0", "pb1"))

    def f_init_dve2(e):
        e.tensor_copy(t_pw[0:1, 0:NEXP], bk(0, NEXP, 0, 1))
        e.tensor_copy(t_hf[:, 0:1], bk(1, 1, off=1))
        e.tensor_scalar_max(t_rsc[:, 0:1], bk(1, 1), 0.1)
        e.drain()
        e.reciprocal(t_rsc[:, 1:2], t_rsc[:, 0:1])
        e.drain()
        e.tensor_scalar_mul(t_alpha[:, 0:1], t_rsc[:, 1:2], 0.5)
        e.drain()
        return e.tensor_scalar_mul(t_alpham[:, 0:1], t_alpha[:, 0:1], -1.0)
    step("vector", [s_pe], f_init_dve2, r=("pb0", "pb1"), w=("pw", "hf", "alpha", "rsc"))

    def ln_block(src_c, dst, router_extras, srcres, dstres, src_t=None):
        n = GRP

        def f_sums(e):
            r = None
            for c in range(8):
                r = e.matmul(bk(0, n, 0, 1), t_onescol[:, 0:1], src_c(c),
                             start=(c == 0), stop=(c == 7))
            return r
        step("tensor", [s_dma, s_dve], f_sums, r=srcres + ("consts",), w=("pb0",))
        for h in range(4):
            pp = h % 2
            step("scalar", [], lambda e, h=h, pp=pp: e.activation(
                t_xsqp[:, pp*512:pp*512+512], src_t[:, 2*h*n:(2*h+2)*n],
                ACT.Square),
                 r=srcres, w=("xsqp%d" % pp,))

            def f_ssum(e, h=h, pp=pp):
                e.matmul(bk(0, n, 0, 1, off=n), t_onescb[:, 0:1],
                         t_xsqp[:, pp*512:pp*512+256],
                         start=(h == 0), stop=False)
                return e.matmul(bk(0, n, 0, 1, off=n), t_onescb[:, 0:1],
                                t_xsqp[:, pp*512+256:pp*512+512],
                                start=False, stop=(h == 3))
            step("tensor", [], f_ssum, r=("xsqp%d" % pp, "consts"), w=("pb0",))

        def f_mv(e):
            e.tensor_scalar_mul(t_stat[0:1, 0:n], bk(0, n, 0, 1), 1.0 / E)
            e.drain()
            e.tensor_mul(t_stat[0:1, 256:256+n], t_stat[0:1, 0:n], t_stat[0:1, 0:n])
            e.drain()
            return e.scalar_tensor_tensor(
                t_stat[0:1, 512:512+n], bk(0, n, 0, 1, off=n), 1.0 / E,
                t_stat[0:1, 256:256+n], OP.mult, OP.subtract)
        step("vector", [s_pe], f_mv, r=("pb0",), w=("stat",))

        def f_rstd(e):
            e.activation(t_stat[0:1, 256:256+n], t_stat[0:1, 512:512+n], ACT.Ln,
                         bias=t_eps[0:1, 0:1])
            e.drain()
            r = e.activation(t_stat[0:1, 768:768+n], t_stat[0:1, 256:256+n],
                             ACT.Exp, scale=-0.5)
            if router_extras:
                r = e.activation(t_stat[0:1, 1280:1280+n], t_stat[0:1, 0:n],
                                 ACT.Copy, scale=-1.0)
            return r
        step("scalar", [s_dve], f_rstd, r=("stat", "consts"), w=("stat",))

        def f_chain3(e):
            return e.tensor_mul(t_stat[0:1, 1024:1024+n], t_stat[0:1, 0:n],
                                t_stat[0:1, 768:768+n])
        step("vector", [s_act], f_chain3, r=("stat",), w=("stat",))

        def f_bc(e):
            e.matmul(bk(1, n), t_onesr[0:1, 0:128], t_stat[0:1, 768:768+n],
                     start=True, stop=True)
            return e.matmul(bk(1, n, off=256), t_onesm[0:1, 0:128],
                            t_stat[0:1, 1024:1024+n], start=True, stop=True)
        step("tensor", [s_dve], f_bc, r=("stat", "consts"), w=("pb1",))

        def f_apply(e):
            e.tensor_copy(t_lnbc[:, 0:512], bk(1, 512))
            e.drain()
            if src_t is not None:
                s3 = src_t[:, 0:8*n].rearrange("p (c j) -> p c j", c=8)
                d3 = dst[:, 0:8*n].rearrange("p (c j) -> p c j", c=8)
                rb3 = t_lnbc[:, 0:n].unsqueeze(1).to_broadcast([128, 8, n])
                mb3 = t_lnbc[:, 256:256+n].unsqueeze(1).to_broadcast([128, 8, n])
                e.tensor_tensor(d3, s3, rb3, OP.mult)
                e.drain()
                return e.tensor_tensor(d3, d3, mb3, OP.add)
            def mul(c):
                e.tensor_tensor(t_rtmp[:, (c % 2)*n:(c % 2)*n+n], src_c(c),
                                t_lnbc[:, 0:n], OP.mult)
            def add(c):
                return e.tensor_tensor(dst[:, c*n:(c+1)*n],
                                       t_rtmp[:, (c % 2)*n:(c % 2)*n+n],
                                       t_lnbc[:, 256:256+n], OP.add)
            mul(0)
            for c in range(7):
                mul(c + 1)
                add(c)
            return add(7)
        step("vector", [s_pe], f_apply, r=srcres + ("pb1",), w=("lnbc", dstres, "rtmp"))

    def router_block(src_c, srcres):
        n = GRP

        def f_z(e):
            for c in range(8):
                e.matmul(bk(2, n, 0, NEXP), t_projw[:, c*NEXP:(c+1)*NEXP], src_c(c),
                         start=(c == 0), stop=False)
            return e.matmul(bk(2, n, 0, NEXP), t_pw[0:1, 0:NEXP],
                            t_stat[0:1, 1280:1280+n], start=False, stop=True)
        step("tensor", [s_dve], f_z, r=srcres + ("projw", "pw", "stat"), w=("pb2",))

        def f_zev(e):
            e.tensor_copy(t_q6[0:NEXP, 0:n], bk(2, n, 0, NEXP))
            e.drain()
            return e.tensor_tensor(t_z[0:NEXP, 0:n], t_q6[0:NEXP, 0:n],
                                   bk(1, n, 0, NEXP), OP.mult)
        step("vector", [s_pe], f_zev, r=("pb2", "pb1"), w=("q6", "z"))

        step("scalar", [s_dve], lambda e: e.activation(
            t_q6[0:NEXP, 0:n], t_z[0:NEXP, 0:n], ACT.Exp, scale=2.0 * QTEMP_INV),
             r=("z",), w=("q6",))

        def f_tanh2(e):
            e.tensor_scalar_add(t_z[0:NEXP, 0:n], t_q6[0:NEXP, 0:n], 1.0)
            e.drain()
            e.reciprocal(t_q6[0:NEXP, 0:n], t_z[0:NEXP, 0:n])
            e.drain()
            return e.tensor_scalar(t_q6[0:NEXP, 0:n], t_q6[0:NEXP, 0:n], -2.0, 1.0,
                                   OP.mult, OP.add)
        step("vector", [s_act], f_tanh2, r=("q6", "z"), w=("q6", "z"))

        def f_dot(e):
            r = None
            for b2 in range(2):
                r = e.matmul(bk(2, NEXP, 0, 128, off=256 + b2*NEXP),
                             t_q6[0:NEXP, b2*128:(b2+1)*128], t_anchT[0:NEXP, 0:NEXP],
                             start=True, stop=True)
            return r
        step("tensor", [s_act], f_dot, r=("q6", "anchT"), w=("pb2",))

        def f_tok(e):
            R = lambda b2: t_r6[:, b2*8:b2*8+NEXP]
            for b2 in range(2):
                e.tensor_copy(R(b2), bk(2, NEXP, 0, 128, off=256 + b2*NEXP))
            for b2 in range(2):
                e.tensor_reduce(t_rs2[:, b2*8+0:b2*8+1], R(b2), AX, OP.max)
            for b2 in range(2):
                e.tensor_scalar(t_rsc[:, b2*32+8:b2*32+8+NEXP], R(b2),
                                t_rs2[:, b2*8+0:b2*8+1], 0.0, OP.subtract, OP.is_equal)
            for b2 in range(2):
                e.scalar_tensor_tensor(t_rsc[:, b2*32+16:b2*32+16+NEXP],
                                       t_rsc[:, b2*32+8:b2*32+8+NEXP], -1e30, R(b2),
                                       OP.mult, OP.add)
            for b2 in range(2):
                e.tensor_reduce(t_rs2[:, b2*8+1:b2*8+2],
                                t_rsc[:, b2*32+16:b2*32+16+NEXP], AX, OP.max)
            for b2 in range(2):
                rr = e.tensor_tensor(t_rs2[:, b2*8+2:b2*8+3], t_rs2[:, b2*8+0:b2*8+1],
                                     t_rs2[:, b2*8+1:b2*8+2], OP.subtract)
            return rr
        step("vector", [s_pe], f_tok, r=("pb2",), w=("r6", "rsc", "rs2"))

        def f_sig(e):
            for b2 in range(2):
                r = e.activation(t_rs2[:, b2*8+5:b2*8+6], t_rs2[:, b2*8+2:b2*8+3],
                                 ACT.Exp, scale=t_alpham[:, 0:1])
            return r
        step("scalar", [s_dve], f_sig, r=("rs2", "alpha"), w=("rs2",))

        def f_ew(e):
            for b2 in range(2):
                e.tensor_scalar_add(t_rsc[:, b2*32:b2*32+1], t_rs2[:, b2*8+5:b2*8+6], 1.0)
            for b2 in range(2):
                e.reciprocal(t_rs2[:, b2*8+3:b2*8+4], t_rsc[:, b2*32:b2*32+1])
            for b2 in range(2):
                e.tensor_scalar(t_rs2[:, b2*8+4:b2*8+5], t_rs2[:, b2*8+3:b2*8+4],
                                -1.0, 1.0, OP.mult, OP.add)
            for b2 in range(2):
                e.tensor_scalar(t_rsc[:, b2*32+24:b2*32+24+NEXP],
                                t_rsc[:, b2*32+16:b2*32+16+NEXP],
                                t_rs2[:, b2*8+1:b2*8+2], 0.0, OP.subtract, OP.is_equal)
            for b2 in range(2):
                e.tensor_scalar(t_rsc[:, b2*32+16:b2*32+16+NEXP],
                                t_rsc[:, b2*32+24:b2*32+24+NEXP],
                                t_rs2[:, b2*8+4:b2*8+5], None, OP.mult)
            for b2 in range(2):
                r = e.scalar_tensor_tensor(t_ew[:, b2*8:b2*8+NEXP],
                                           t_rsc[:, b2*32+8:b2*32+8+NEXP],
                                           t_rs2[:, b2*8+3:b2*8+4],
                                           t_rsc[:, b2*32+16:b2*32+16+NEXP],
                                           OP.mult, OP.add)
            return r
        step("vector", [s_act], f_ew, r=("rsc", "rs2"), w=("rsc", "rs2", "ew"))

        def f_ewT(e):
            r = None
            for x_ in range(NEXP):
                for b2 in range(2):
                    r = e.transpose(psA[0:1, 1024 + x_*256 + b2*128:
                                        1024 + x_*256 + b2*128 + 128],
                                    t_ew[:, b2*8 + x_: b2*8 + x_ + 1], t_ident[:])
            return r
        step("tensor", [s_dve], f_ewT, r=("ew", "ident"), w=("pb2", "pb3", "pb4"))

        def f_ewf(e):
            r = None
            for x_ in range(NEXP):
                for b2 in range(2):
                    o = x_*256 + b2*128
                    r = e.tensor_copy(t_ewf6[0:1, o:o+128],
                                      psA[0:1, 1024 + o: 1024 + o + 128])
            return r
        step("vector", [s_pe], f_ewf, r=("pb2", "pb3", "pb4"), w=("ewf",))

    def ffn_block(evac, xgres):
        n = GRP
        for x0 in (0, 2, 4):
            eb = 2 + (x0 // 2) % 2
            def f_ewbc(e, x0=x0, eb=eb):
                e.matmul(bk(eb, n), t_onesr[0:1, 0:128], t_ewf6[0:1, x0*n:(x0+1)*n],
                         start=True, stop=True)
                return e.matmul(bk(eb, n, off=256), t_onesr[0:1, 0:128],
                                t_ewf6[0:1, (x0+1)*n:(x0+2)*n], start=True, stop=True)
            step("tensor", [s_dve], f_ewbc, r=("ewf", "consts"), w=("pb%d" % eb,))

            def f_ewbev(e, x0=x0, eb=eb):
                e.tensor_copy(t_ewb[:, x0*n:(x0+1)*n], bk(eb, n))
                return e.tensor_copy(t_ewb[:, (x0+1)*n:(x0+2)*n], bk(eb, n, off=256))
            step("vector", [s_pe], f_ewbev, r=("pb%d" % eb,), w=("ewb",))

        for x_ in range(NEXP):
            hb = 4 if x_ % 2 == 0 else 6

            def f_w1(e, x_=x_, hb=hb):
                r = None
                for m_ in range(4):
                    for c in range(8):
                        r = e.matmul(psA[0:128, hb*512 + m_*n: hb*512 + (m_+1)*n],
                                     t_w1[x_][:, c*512 + m_*128: c*512 + (m_+1)*128],
                                     t_xh[:, c*n:(c+1)*n], start=(c == 0), stop=(c == 7))
                return r
            step("tensor", [s_dve, s_act], f_w1, r=("xh", "wsp"),
                 w=(("pb4", "pb5") if hb == 4 else ("pb6", "pb7")))

            def f_gelu(e, x_=x_, hb=hb):
                return e.activation(t_h[:, 0:4*n], psA[0:128, hb*512: hb*512 + 4*n],
                                    ACT.Gelu_apprx_tanh)
            step("scalar", [s_pe], f_gelu,
                 r=(("pb4", "pb5") if hb == 4 else ("pb6", "pb7")), w=("h",))

            def f_ht(e, x_=x_):
                base = t_ewb[:, x_*n:(x_+1)*n]
                bc = base.unsqueeze(1).to_broadcast([128, 4, n])
                hh = t_h[:, 0:4*n].rearrange("p (m j) -> p m j", m=4)
                dd = t_ht[:, x_*4*n:(x_+1)*4*n].rearrange("p (m j) -> p m j", m=4)
                return e.tensor_tensor(dd, hh, bc, OP.mult)
            step("vector", [s_act], f_ht, r=("h", "ewb"), w=("ht",))

        for oc in range(8):
            ob = oc % 2
            def f_w2(e, oc=oc, ob=ob):
                first = True
                r = None
                for x_ in range(NEXP):
                    for k_ in range(4):
                        r = e.matmul(bk(ob, n), t_w2[x_][:, k_*E + oc*128: k_*E + (oc+1)*128],
                                     t_ht[:, (x_*4 + k_)*n:(x_*4 + k_ + 1)*n],
                                     start=first, stop=(x_ == NEXP-1 and k_ == 3))
                        first = False
                return r
            step("tensor", [s_dve], f_w2, r=("ht", "wsp"), w=("pb%d" % ob,))
            step("vector", [s_pe], lambda e, oc=oc, ob=ob: evac(e, oc, ob),
                 r=("pb%d" % ob,) + xgres, w=("x1g",))

    _sidA = nc.enter_named_scope("phaseA", False)[0]
    src_x1g = lambda c: t_x1g[:, c*GRP:(c+1)*GRP]

    def mk_src(t):
        return lambda c: t[:, c*GRP:(c+1)*GRP]

    def mk_evac(t):
        def ev(e, oc, ob):
            return e.scalar_tensor_tensor(t_x1g[:, oc*GRP:(oc+1)*GRP], bk(ob, GRP),
                                          0.5, t[:, oc*GRP:(oc+1)*GRP],
                                          OP.mult, OP.add)
        return ev

    for g in range(ngrp_a):
        t_xg = t_xga if g % 2 == 0 else t_xgb
        xgres = ("xga", "xgah") if g % 2 == 0 else ("xgb", "xgbh")
        src_xg = mk_src(t_xg)

        if g < 2:
            def f_load(e, g=g, t_xg=t_xg):
                return [
                    e.dma_start(out=t_xg[:].rearrange("p (c j) -> p c j", c=8),
                                in_=d_x[:, :, g*GRP:(g+1)*GRP].rearrange("c p j -> p c j")),
                    e.dma_start(out=t_cs[:], in_=d_csk[:, g*512:(g+1)*512]),
                ]
            step("sync", COMPUTE, f_load, dma_sem=s_dma, ndma=2,
                 w=xgres + ("cs",))
        else:
            def f_load(e, g=g, t_xg=t_xg):
                return [
                    e.dma_start(out=t_xg[:, 0:4*GRP].rearrange("p (c j) -> p c j", c=4),
                                in_=d_x[0:4, :, g*GRP:(g+1)*GRP].rearrange("c p j -> p c j")),
                    e.dma_start(out=t_cs[:], in_=d_csk[:, g*512:(g+1)*512]),
                ]
            step("sync", COMPUTE, f_load, dma_sem=s_dma, ndma=2,
                 w=(xgres[0], "cs"))

            def f_loadh(e, g=g, t_xg=t_xg):
                return e.dma_start(
                    out=t_xg[:, 4*GRP:8*GRP].rearrange("p (c j) -> p c j", c=4),
                    in_=d_x[4:8, :, g*GRP:(g+1)*GRP].rearrange("c p j -> p c j"))
            step("gpsimd", COMPUTE, f_loadh, dma_sem=s_dmag, ndma=1,
                 w=(xgres[1],))

        ln_block(src_xg, t_xh, True, xgres, "xh", src_t=t_xg)
        router_block(src_xg, xgres)
        ffn_block(mk_evac(t_xg), xgres)

        s0 = (g % 2) * 128
        s1 = 128 - s0
        def f_blendx(e, g=g, s0=s0, s1=s1):
            def sub(c):
                e.tensor_tensor(t_osp[:, c*128:(c+1)*128], t_x1g[:, c*GRP+s1:c*GRP+s1+128],
                                t_x1g[:, c*GRP+s0:c*GRP+s0+128], OP.subtract)
            def stt(c):
                return e.scalar_tensor_tensor(t_osp[:, c*128:(c+1)*128],
                                              t_osp[:, c*128:(c+1)*128], t_hf[:, 0:1],
                                              t_x1g[:, c*GRP+s0:c*GRP+s0+128],
                                              OP.mult, OP.add)
            sub(0)
            for c in range(7):
                sub(c + 1)
                stt(c)
            return stt(7)
        step("vector", [s_dve], f_blendx, r=("x1g", "hf"), w=("ht",))

        def f_spill_x1(e, g=g):
            return e.dma_start(out=d_x1own[:, :, g*128:(g+1)*128].rearrange("c p j -> p c j"),
                               in_=t_osp[:].rearrange("p (c j) -> p c j", c=8))
        step("sync", [s_dve], f_spill_x1, dma_sem=s_sp, r=("ht",), w=("d_x1own",))

        if "ew" in taps:
            def f_ewtap(e, g=g):
                return [
                    e.dma_start(out=d_ewdbg[0:1, g*1536:(g+1)*1536],
                                in_=t_ewf6[0:1, 0:1536]),
                    e.dma_start(out=d_zdbg[:, g*GRP:(g+1)*GRP], in_=t_z[0:NEXP, 0:GRP]),
                    e.dma_start(out=d_q6dbg[:, g*GRP:(g+1)*GRP], in_=t_q6[0:NEXP, 0:GRP]),
                    e.dma_start(out=d_r6dbg[:, g*16:(g+1)*16], in_=t_r6[:, 0:16]),
                    e.dma_start(out=d_ewtdbg[:, g*16:(g+1)*16], in_=t_ew[:, 0:16]),
                    e.dma_start(out=d_rscdbg[:, g*64:g*64+16], in_=t_rs2[:, 0:16]),
                    e.dma_start(out=d_rscdbg[:, g*64+16:(g+1)*64], in_=t_rsc[:, 0:48]),
                ]
            step("sync", [s_dve], f_ewtap, dma_sem=s_sp, ndma=7)

        ln_block(src_x1g, t_yh, False, ("x1g",), "yh", src_t=t_x1g)

        def f_blendy(e, g=g, s0=s0, s1=s1):
            yb = t_osp[:, 0:512].bitcast(BF16)
            def sub(c):
                e.tensor_tensor(yb[:, c*128:(c+1)*128], t_yh[:, c*GRP+s1:c*GRP+s1+128],
                                t_yh[:, c*GRP+s0:c*GRP+s0+128], OP.subtract)
            def stt(c):
                return e.scalar_tensor_tensor(yb[:, c*128:(c+1)*128],
                                              yb[:, c*128:(c+1)*128], t_hf[:, 0:1],
                                              t_yh[:, c*GRP+s0:c*GRP+s0+128],
                                              OP.mult, OP.add)
            sub(0)
            for c in range(7):
                sub(c + 1)
                stt(c)
            return stt(7)
        step("vector", [s_dve], f_blendy, r=("yh", "hf"), w=("ht",))

        def f_spill_y(e, g=g):
            yb = t_osp[:, 0:512].bitcast(BF16)
            return e.dma_start(out=d_yown[:, :, g*128:(g+1)*128].rearrange("c p j -> p c j"),
                               in_=yb.rearrange("p (c j) -> p c j", c=8))
        step("sync", [s_dve], f_spill_y, dma_sem=s_sp, r=("ht",), w=("d_yown",))

        def f_kva(e):
            r = None
            for c in range(8):
                r = e.matmul(bk(2, GRP, 0, KVL), t_wkva[:, c*KVL:(c+1)*KVL],
                             t_yh[:, c*GRP:(c+1)*GRP], start=(c == 0), stop=(c == 7))
            return r
        step("tensor", [s_dve], f_kva, r=("yh", "wsp"), w=("pb2",))
        step("vector", [s_pe], lambda e: e.tensor_copy(
            t_kva[0:KVL, 0:GRP], bk(2, GRP, 0, KVL)), r=("pb2",), w=("kva",))

        for base, dst, kres in ((0, t_ka, "ka"), (256, t_kb, "kb")):
            def f_k(e, base=base):
                e.matmul(bk(3, 256), t_wkvbk[:, base:base+128], t_kva[0:KVL, 0:GRP],
                         start=True, stop=True)
                return e.matmul(bk(3, 256, off=256), t_wkvbk[:, base+128:base+256],
                                t_kva[0:KVL, 0:GRP], start=True, stop=True)
            step("tensor", [s_dve], f_k, r=("wsp", "kva"), w=("pb3",))

            def f_krope(e, dst=dst, g=g):
                e.tensor_tensor(t_rtmp[:, 0:512], bk(3, 512), t_cs[:, 0:512], OP.mult)
                e.drain()
                return e.tensor_tensor(dst[:, g*GRP:(g+1)*GRP], t_rtmp[:, 0:256],
                                       t_rtmp[:, 256:512], OP.add)
            step("vector", [s_pe], f_krope, r=("pb3", "cs"), w=("rtmp", kres))

        for tl in range(2):
            def f_v(e, tl=tl):
                return e.matmul(bk(2, 256, off=256), t_kva[0:KVL, tl*128:(tl+1)*128],
                                t_wkvbv[0:KVL, 0:256], start=True, stop=True)
            step("tensor", [s_dve], f_v, r=("kva", "wsp"), w=("pb2",))

            def f_vev(e, tl=tl, g=g):
                dstv = t_v[:].rearrange("p (t k j) -> p t k j", t=NB, k=4,
                                        j=65)[:, 2*g+tl, :, 0:64]
                return e.tensor_copy(dstv, bk(2, 256, off=256).rearrange(
                    "p (k j) -> p k j", k=4))
            step("vector", [s_pe], f_vev, r=("pb2",), w=("v",))

    if "kv" in taps:
        step("sync", [s_dve], lambda e: [
            e.dma_start(out=d_kadbg, in_=t_ka[:]),
            e.dma_start(out=d_kbdbg, in_=t_kb[:]),
            e.dma_start(out=d_vdbg, in_=t_v[:]),
        ], dma_sem=s_sp, ndma=3)

    nc.leave_named_scope("phaseA", _sidA, False)
    _sidB = nc.enter_named_scope("phaseB", False)[0]
    barrier()
    for nm_ in ("q", "p0", "p1", "a"):
        if "ht" in writers:
            writers[nm_] = writers["ht"]
        readers[nm_] = list(readers.get("ht", ()))
    def f_woutload(e):
        return [e.dma_start(out=wout_chunk(ec), in_=d_wout[ec])
                for ec in range(8)]
    step("gpsimd", [], f_woutload, dma_sem=s_dmag, ndma=8, w=("wsp",))

    if not do_b:
        nqg = 0
    step("sync", COMPUTE + [s_dma], lambda e: [
        e.dma_start(out=t_cs[:], in_=d_csq[:, 0:512])], dma_sem=s_dma, w=("cs",))

    for i in range(nqg):
        p_ = i % 2
        vx1 = t_xga if p_ == 0 else t_xgb
        xres = "xga" if p_ == 0 else "xgb"
        xresh = "xgah" if p_ == 0 else "xgbh"
        vmk = t_ewb[:, 0:1024] if p_ == 0 else t_xh[:, 0:1024]
        mres = "ewb" if p_ == 0 else "ewb2"
        vcs = t_cs if p_ == 0 else t_xh[:, 1024:1536]
        cres = "cs" if p_ == 0 else "cs2"

        def f_bload(e, i=i, vx1=vx1, vmk=vmk, vcs=vcs):
            r = [
                e.dma_start(
                    out=vx1[:].rearrange("p (c j) -> p c j", c=8),
                    in_=d_x1own[:, :, i*256:(i+1)*256].rearrange("c p j -> p c j")),
                e.dma_start(
                    out=vmk.rearrange("p (t j) -> p t j", t=4),
                    in_=d_maskt[4*i:4*i+4].rearrange("t p j -> p t j")),
            ]
            if i > 0:
                r.append(e.dma_start(out=vcs[:, 0:512], in_=d_csq[:, i*512:(i+1)*512]))
            return r
        step("sync", COMPUTE, f_bload, dma_sem=s_dma, est=6000,
             ndma=3 if i > 0 else 2, r=("d_x1own",),
             w=(xres, xresh, mres) + ((cres,) if i > 0 else ()))

        def f_bloady(e, i=i):
            return e.dma_start(
                out=v_yb[:].rearrange("p (c j) -> p c j", c=8),
                in_=d_yown[:, :, i*256:(i+1)*256].rearrange("c p j -> p c j"))
        step("gpsimd", COMPUTE, f_bloady, dma_sem=s_dmag, ndma=1,
             r=("d_yown",), w=("yh",), est=5000)

        def f_qa(e):
            r = None
            for c in range(8):
                r = e.matmul(bk(0, 256, 0, QL), t_wqa[:, c*QL:(c+1)*QL],
                             v_yb[:, c*256:(c+1)*256], start=(c == 0), stop=(c == 7))
            return r
        step("tensor", [s_dma], f_qa, r=("yh", "wsp"), w=("pb0",))
        step("vector", [s_pe], lambda e: e.tensor_copy(t_qa[0:QL, 0:256],
                                                       bk(0, 256, 0, QL)),
             r=("pb0",), w=("qa",))

        for c in range(8):
            def f_qc(e, c=c):
                e.matmul(bk(1, 256), t_wqb[0:QL, c*128:(c+1)*128], t_qa[0:QL, 0:256],
                         start=True, stop=True)
                return e.matmul(bk(1, 256, off=256),
                                t_wqb[0:QL, 1024 + c*128: 1024 + (c+1)*128],
                                t_qa[0:QL, 0:256], start=True, stop=True)
            step("tensor", [s_dve], f_qc, r=("wsp", "qa"), w=("pb1",))

            def f_qrope(e, c=c, vcs=vcs):
                e.tensor_tensor(t_rtmp[:, 0:512], bk(1, 512), vcs[:, 0:512], OP.mult)
                e.drain()
                return e.tensor_tensor(v_q[:, c*256:(c+1)*256], t_rtmp[:, 0:256],
                                       t_rtmp[:, 256:512], OP.add)
            step("vector", [s_pe], f_qrope, r=("pb1", cres), w=("rtmp", "q"))

        ktmax = 4 * i + 4
        for cp in range(4):
            Ksrc = t_ka if cp < 2 else t_kb
            kvlo = cp // 2
            kvhi = kvlo + 2

            def qk(e, kt, cp=cp, Ksrc=Ksrc):
                sb = (kt % 2) * 2
                e.matmul(bk(sb, 256), Ksrc[0:64, kt*128:(kt+1)*128],
                         v_q[0:64, (2*cp)*256:(2*cp+1)*256], start=True, stop=True,
                         tile_position=(0, 0))
                e.matmul(bk(sb, 256, off=256), Ksrc[0:64, kt*128:(kt+1)*128],
                         v_q[0:64, (2*cp+1)*256:(2*cp+2)*256], start=True, stop=True,
                         tile_position=(0, 0))
                e.matmul(bk(sb + 1, 256), Ksrc[64:128, kt*128:(kt+1)*128],
                         v_q[64:128, (2*cp)*256:(2*cp+1)*256], start=True, stop=True,
                         tile_position=(64, 0))
                return e.matmul(bk(sb + 1, 256, off=256), Ksrc[64:128, kt*128:(kt+1)*128],
                                v_q[64:128, (2*cp+1)*256:(2*cp+2)*256], start=True,
                                stop=True, tile_position=(64, 0))

            def sres(kt):
                return ("pb0", "pb1") if kt % 2 == 0 else ("pb2", "pb3")

            def f_tzero(e):
                e.memset(psA[0:128, 4*512:5*512], 0.0)
                return e.memset(psA[0:128, 5*512:6*512], 0.0)
            step("vector", [], f_tzero, w=("pb4", "pb5"))

            step("tensor", [], lambda e, qk=qk: qk(e, 0),
                 r=("q", "ka", "kb"), w=sres(0))
            if ktmax > 1:
                step("tensor", [], lambda e, qk=qk: qk(e, 1),
                     r=("q", "ka", "kb"), w=sres(1))

            for kt in range(ktmax):
                pb = kt % 2
                step("scalar", [], lambda e, pb=pb: e.activation(
                    v_ps[pb][:, 0:1024], psA[0:128, pb*1024:(pb+1)*1024], ACT.Exp,
                    scale=SCALE),
                     r=sres(kt), w=("p%d" % pb,))

                if kt >= 4 * i:
                    j = kt - 4 * i

                    def f_mask(e, j=j, pb=pb, vmk=vmk):
                        mb = vmk[:, j*256:(j+1)*256]
                        bc4 = mb.unsqueeze(1).to_broadcast([128, 4, 256])
                        pv = v_ps[pb][:, 0:1024].rearrange("p (s j) -> p s j", s=4)
                        return e.tensor_tensor(pv, pv, bc4, OP.mult)
                    step("vector", [], f_mask, r=("p%d" % pb, mres), w=("p%d" % pb,))

                def f_av(e, kt=kt, cp=cp, kvlo=kvlo, kvhi=kvhi, qk=qk, pb=pb,
                         last=(kt == ktmax - 1), pre=(kt + 2 < ktmax)):
                    st = (kt == 0)
                    sp_ = last
                    vp = v_ps[pb]
                    e.matmul(psA[0:64, 4*512: 4*512+256],
                             t_v[:, kt*260 + kvlo*65: kt*260 + kvlo*65 + 64],
                             vp[:, 0:256], start=False, stop=sp_, tile_position=(0, 0),
                             skip_group_check=True)
                    e.matmul(psA[64:128, 4*512: 4*512+256],
                             t_v[:, kt*260 + kvlo*65: kt*260 + kvlo*65 + 64],
                             vp[:, 256:512], start=False, stop=sp_, tile_position=(0, 64),
                             skip_group_check=True)
                    e.matmul(psA[0:64, 5*512: 5*512+256],
                             t_v[:, kt*260 + kvhi*65: kt*260 + kvhi*65 + 64],
                             vp[:, 512:768], start=False, stop=sp_, tile_position=(0, 0),
                             skip_group_check=True)
                    e.matmul(psA[64:128, 5*512: 5*512+256],
                             t_v[:, kt*260 + kvhi*65: kt*260 + kvhi*65 + 64],
                             vp[:, 768:1024], start=False, stop=sp_, tile_position=(0, 64),
                             skip_group_check=True)
                    e.matmul(bk(6, 512, 0, 1), t_onescb[:, 0:1], vp[:, 0:512],
                             start=st, stop=sp_)
                    r = e.matmul(bk(7, 512, 0, 1), t_onescb[:, 0:1], vp[:, 512:1024],
                                 start=st, stop=sp_)
                    if pre:
                        r = qk(e, kt + 2)
                    return r
                avw = ("pb4", "pb5", "pb6", "pb7") + (sres(kt) if kt + 2 < ktmax else ())
                step("tensor", [], f_av, r=("p%d" % pb, "v", "consts", "q", "ka", "kb"),
                     w=avw)

            def f_rc(e):
                return e.reciprocal(t_rc[0:1, 0:1024], psA[0:1, 6*512:8*512])
            step("vector", [], f_rc, r=("pb6", "pb7"), w=("stat",))

            def f_rb(e):
                e.matmul(bk(6, 256, 0, 64), t_onesr[0:1, 0:64], t_rc[0:1, 0:256],
                         start=True, stop=True, tile_position=(0, 0))
                e.matmul(bk(6, 256, 64, 128), t_onesr[0:1, 0:64], t_rc[0:1, 256:512],
                         start=True, stop=True, tile_position=(0, 64))
                e.matmul(bk(6, 256, 0, 64, off=256), t_onesr[0:1, 0:64],
                         t_rc[0:1, 512:768], start=True, stop=True, tile_position=(0, 0))
                return e.matmul(bk(6, 256, 64, 128, off=256), t_onesr[0:1, 0:64],
                                t_rc[0:1, 768:1024], start=True, stop=True,
                                tile_position=(0, 64))
            step("tensor", [], f_rb, r=("stat", "consts"), w=("pb6",))

            def f_norm(e, cp=cp):
                e.tensor_copy(v_rb[:, 0:512], bk(6, 512))
                e.drain()
                e.tensor_tensor(v_a[0:64, (2*cp)*256:(2*cp+1)*256],
                                psA[0:64, 4*512: 4*512+256], v_rb[0:64, 0:256], OP.mult)
                e.tensor_tensor(v_a[64:128, (2*cp)*256:(2*cp+1)*256],
                                psA[64:128, 4*512: 4*512+256], v_rb[64:128, 0:256],
                                OP.mult)
                e.tensor_tensor(v_a[0:64, (2*cp+1)*256:(2*cp+2)*256],
                                psA[0:64, 5*512: 5*512+256], v_rb[0:64, 256:512],
                                OP.mult)
                return e.tensor_tensor(v_a[64:128, (2*cp+1)*256:(2*cp+2)*256],
                                       psA[64:128, 5*512: 5*512+256],
                                       v_rb[64:128, 256:512], OP.mult)
            step("vector", [], f_norm,
                 r=("pb6", "pb4", "pb5"), w=("rtmp", "a"))

        for oc in range(8):
            wbk = 1 if oc % 2 == 0 else 0
            wres = "pb1" if oc % 2 == 0 else "pb0"

            def f_wo(e, oc=oc, wbk=wbk):
                r = None
                for ec in range(8):
                    r = e.matmul(bk(wbk, 256),
                                 wout_chunk(ec)[:, oc*128:(oc+1)*128],
                                 v_a[:, ec*256:(ec+1)*256], start=(ec == 0), stop=(ec == 7),
                                 skip_group_check=True)
                return r
            step("tensor", [s_dve], f_wo, r=("a", "wsp"), w=(wres,))

            def f_x2(e, oc=oc, vx1=vx1, wbk=wbk):
                return e.scalar_tensor_tensor(
                    t_x1g[:, oc*256:(oc+1)*256], bk(wbk, 256),
                    t_bout[:, oc:oc+1],
                    vx1[:, oc*256:(oc+1)*256], OP.add, OP.add)
            step("vector", [s_pe], f_x2, r=(wres, xres, xresh, "bout"), w=("x1g",))

        def f_x2sp(e, i=i):
            return e.dma_start(out=d_x2sp[:, :, i*256:(i+1)*256].rearrange("c p j -> p c j"),
                               in_=t_x1g[:].rearrange("p (c j) -> p c j", c=8))
        step("sync", [s_dve], f_x2sp, dma_sem=s_sp, r=("x1g",), w=("x2sp%d" % i,))

    nc.leave_named_scope("phaseB", _sidB, False)
    _sidC = nc.enter_named_scope("phaseC", False)[0]
    for dst_, srcs_ in (("ht", ("q", "p0", "p1", "a")),
                        ("xh", ("ewb2", "cs2"))):
        for s__ in srcs_:
            if s__ in writers:
                readers.setdefault(dst_, []).append(writers[s__])
            readers.setdefault(dst_, []).extend(readers.get(s__, ()))
    def f_wreload(e):
        r = []
        for x_ in range(NEXP):
            r.append(e.dma_start(
                out=t_w1[x_][:, 0:1024].rearrange("p (c j) -> p c j", c=2),
                in_=d_w1[x_][0:2].rearrange("c p j -> p c j")))
        for x_ in range(2):
            r.append(e.dma_start(
                out=t_w2[x_][:, 0:1024].rearrange("p (c j) -> p c j", c=1),
                in_=d_w2[x_][0:1].rearrange("c p j -> p c j")))
        return r
    step("gpsimd", [], f_wreload, dma_sem=s_dmag, ndma=8, w=("wsp",))

    for j in range(NQG if do_c else 0):
        t_xg = t_xga if j % 2 == 0 else t_xgb
        xgres = ("xga", "xgah") if j % 2 == 0 else ("xgb", "xgbh")
        src_xg = mk_src(t_xg)

        if j == 0:
            def f_cload(e, j=j, t_xg=t_xg):
                return [e.dma_start(out=t_xg[:].rearrange("p (c j) -> p c j", c=8),
                                    in_=d_x2sp[:, :, j*GRP:(j+1)*GRP].rearrange("c p j -> p c j"))]
            step("sync", COMPUTE + [s_sp], f_cload, dma_sem=s_dma,
                 r=("x2sp%d" % j,), w=xgres)
        else:
            def f_cload(e, j=j, t_xg=t_xg):
                return [e.dma_start(
                    out=t_xg[:, 0:4*GRP].rearrange("p (c j) -> p c j", c=4),
                    in_=d_x2sp[0:4, :, j*GRP:(j+1)*GRP].rearrange("c p j -> p c j"))]
            step("sync", COMPUTE + [s_sp], f_cload, dma_sem=s_dma,
                 r=("x2sp%d" % j,), w=(xgres[0],))

            def f_cloadh(e, j=j, t_xg=t_xg):
                return e.dma_start(
                    out=t_xg[:, 4*GRP:8*GRP].rearrange("p (c j) -> p c j", c=4),
                    in_=d_x2sp[4:8, :, j*GRP:(j+1)*GRP].rearrange("c p j -> p c j"))
            step("gpsimd", COMPUTE + [s_sp], f_cloadh, dma_sem=s_dmag, ndma=1,
                 r=("x2sp%d" % j,), w=(xgres[1],))

        ln_block(src_xg, t_xh, True, xgres, "xh", src_t=t_xg)
        router_block(src_xg, xgres)
        ffn_block(mk_evac(t_xg), xgres)

        def f_store(e, j=j):
            return e.dma_start(out=d_out[:, :, j*GRP:(j+1)*GRP].rearrange("c p j -> p c j"),
                               in_=t_x1g[:].rearrange("p (c j) -> p c j", c=8))
        step("sync", [s_dve], f_store, dma_sem=s_sp, r=("x1g",), w=("d_out",))

    nc.leave_named_scope("phaseC", _sidC, False)
    flush_plan()
    _sp_total = sum(s.amt for s in all_steps if s.sem is s_sp)

    def f_final(e):
        e.wait_ge(s_sp, _sp_total)
    blk.sync(f_final)

    es.close()
    return nc



def kernel(**inputs):
    in_maps, metas = host_prepare(inputs)
    nc = build_program()
    res = run_bass_kernel_spmd(nc, in_maps, core_ids=list(range(8)))
    out = np.zeros((4, S, E), dtype=np.float32)
    for core in range(8):
        b, own = metas[core]
        o = res.results[core]["out"].reshape(E, OWN)
        for j, g in enumerate(own):
            out[b, g*128:(g+1)*128, :] = o[:, j*128:(j+1)*128].T
    return out


# version 22 (final)
# speedup vs baseline: 1.0456x; 1.0152x over previous
import numpy as np
import ml_dtypes
import concourse.bass as bass
import concourse.mybir as mybir
from concourse.bass_utils import run_bass_kernel_spmd

F32 = mybir.dt.float32
BF16 = mybir.dt.bfloat16
I32 = mybir.dt.int32
OP = mybir.AluOpType
ACT = mybir.ActivationFunctionType
AX = mybir.AxisListType.X
BF = ml_dtypes.bfloat16

E, S, NB = 1024, 4096, 32
H, KVH, D, QL, KVL = 16, 4, 64, 64, 32
NEXP, DEXP = 6, 512
EPS = 1e-5
GRP = 256
NGRP = S // GRP
NQG = 8
OWN = 2048
SCALE = 0.125
QTEMP_INV = 1.0 / 0.3
STEP_LIMIT = [10**9]
SCHED_DUMP = []
PRED = [0.0]


def own_blocks(half):
    keep = (0, 3) if half == 0 else (1, 2)
    return [g for g in range(NB) if g % 4 in keep]



def _swap64(Wcols):
    r = Wcols.reshape(Wcols.shape[0], -1, 32, 2)
    return np.ascontiguousarray(r[..., ::-1].reshape(Wcols.shape))


def _rope_rows():
    inv = 1.0 / (10000.0 ** (np.arange(0, D, 2, dtype=np.float64) / D))
    ang = np.outer(inv, np.arange(S, dtype=np.float64))
    cos = np.repeat(np.cos(ang), 2, axis=0)
    sin = np.repeat(np.sin(ang), 2, axis=0).copy()
    sin[0::2] *= -1.0
    C = np.concatenate([cos, cos], 0).astype(np.float32)
    Sg = np.concatenate([sin, sin], 0).astype(np.float32)
    return C, Sg


def host_prepare(inputs):
    x = np.asarray(inputs["L0_attn_x"], dtype=np.float32)
    Wqa = np.asarray(inputs["L0_attn_Wqa"], np.float32)
    Wqb = np.asarray(inputs["L0_attn_Wqb"], np.float32)
    Wkva = np.asarray(inputs["L0_attn_Wkva"], np.float32)
    Wkvb = np.asarray(inputs["L0_attn_Wkvb"], np.float32)
    Wout = np.asarray(inputs["L0_attn_Wout"], np.float32)
    bout = np.asarray(inputs["L0_attn_bout"], np.float32)
    projw = np.asarray(inputs["L0_ffn_proj_w"], np.float32)
    anch = np.asarray(inputs["L0_ffn_anchors"], np.float32)
    temp = np.asarray(inputs["L0_ffn_routing_temp"], np.float32).reshape(1, 1)
    w1 = np.asarray(inputs["L0_ffn_w1"], np.float32)
    w2 = np.asarray(inputs["L0_ffn_w2"], np.float32)

    sh = {}
    sh["w1r"] = np.ascontiguousarray(w1.reshape(NEXP, 8, 128, DEXP))
    sh["w2r"] = np.ascontiguousarray(w2.reshape(NEXP, 4, 128, E))
    sh["wqa"] = np.ascontiguousarray(Wqa.reshape(8, 128, QL))
    WqbP = np.empty_like(Wqb)
    for c in range(8):
        WqbP[:, c*128:c*128+64] = Wqb[:, c*64:(c+1)*64]
        WqbP[:, c*128+64:(c+1)*128] = Wqb[:, (c+8)*64:(c+9)*64]
    sh["wqb"] = np.ascontiguousarray(np.concatenate([WqbP, _swap64(WqbP)], 1))
    sh["wkva"] = np.ascontiguousarray(Wkva.reshape(8, 128, KVL))
    kvr = Wkvb.reshape(KVL, KVH, D, 2)
    WK = np.ascontiguousarray(kvr[..., 0].reshape(KVL, KVH * D))
    WV = np.ascontiguousarray(kvr[..., 1].reshape(KVL, KVH * D))
    WKs = _swap64(WK)
    sh["wkvbk"] = np.ascontiguousarray(np.concatenate(
        [WK[:, 0:64], WK[:, 128:192],
         WKs[:, 0:64], WKs[:, 128:192],
         WK[:, 64:128], WK[:, 192:256],
         WKs[:, 64:128], WKs[:, 192:256]], 1))
    sh["wkvbv"] = WV
    wop = np.empty_like(Wout)
    for j, oc in enumerate([0, 4, 1, 5, 2, 6, 3, 7]):
        wop[j*128:(j+1)*128] = Wout[oc*128:(oc+1)*128]
    sh["wout"] = np.ascontiguousarray(wop.reshape(8, 128, E))
    sh["bout"] = np.ascontiguousarray(bout.reshape(8, 128).T)
    sh["projw"] = np.ascontiguousarray(
        projw.T.reshape(8, 128, NEXP).transpose(1, 0, 2).reshape(128, 8 * NEXP))
    sh["anchT"] = np.ascontiguousarray(anch.T)
    sh["rtemp"] = temp
    sh["ident"] = np.eye(128, dtype=np.float32)

    C, Sg = _rope_rows()
    csk = np.zeros((128, 2 * S), dtype=np.float32)
    for g in range(NGRP):
        csk[:, g*512:g*512+256] = C[:, g*256:(g+1)*256]
        csk[:, g*512+256:(g+1)*512] = Sg[:, g*256:(g+1)*256]
    sh["csk"] = csk.astype(BF)

    in_maps, metas = [], []
    for core in range(8):
        b, half = core // 2, core % 2
        own = own_blocks(half)
        m = dict(sh)
        m["x"] = np.ascontiguousarray(x[b].T.reshape(8, 128, S))
        m["coreinfof"] = np.array([[half]], dtype=np.float32)
        csq = np.zeros((128, 2 * OWN), dtype=np.float32)
        for i in range(NQG):
            cols = list(range(own[2*i]*128, own[2*i]*128+128)) + \
                   list(range(own[2*i+1]*128, own[2*i+1]*128+128))
            csq[:, i*512:i*512+256] = C[:, cols]
            csq[:, i*512+256:(i+1)*512] = Sg[:, cols]
        m["csq"] = csq.astype(BF)
        mt = np.zeros((NQG * 4, 128, 256), dtype=np.float32)
        kk = np.arange(128)[:, None]
        qq = np.arange(128)[None, :]
        for i in range(NQG):
            g0, g1 = own[2*i], own[2*i+1]
            for j in range(4):
                kt = 4 * i + j
                for bi, g in ((0, g0), (1, g1)):
                    mt[4*i+j, :, bi*128:(bi+1)*128] = (kt*128 + kk <= g*128 + qq)
        m["maskt"] = mt.astype(BF)
        in_maps.append(m)
        metas.append((b, own))
    return in_maps, metas



class _Counts:
    def __init__(self):
        self.c = {}

    def inc(self, instr, semh, amt=1):
        instr.then_inc(semh, amt)
        self.c[semh.name] = self.c.get(semh.name, 0) + amt

    def val(self, semh):
        return self.c.get(semh.name, 0)


def _shift_ap(ap, delta):
    return bass.AP(ap.tensor, ap.offset + delta, ap.ap)


def build_program(taps=(), ngrp_a=NGRP, do_b=True, do_c=True, nqg=NQG, nc_only=False):
    nc = bass.Bass("TRN2", target_bir_lowering=False, dynamic_dma_scratch_size=16384)
    nt = _Counts()

    def dram(n, s, d, k="ExternalInput"):
        return nc.dram_tensor(n, s, d, kind=k).ap()

    d_x = dram("x", [8, 128, S], F32)
    d_w1 = dram("w1r", [NEXP, 8, 128, DEXP], F32)
    d_w2 = dram("w2r", [NEXP, 4, 128, E], F32)
    d_wqa = dram("wqa", [8, 128, QL], F32)
    d_wqb = dram("wqb", [QL, 2048], F32)
    d_wkva = dram("wkva", [8, 128, KVL], F32)
    d_wkvbk = dram("wkvbk", [KVL, 512], F32)
    d_wkvbv = dram("wkvbv", [KVL, 256], F32)
    d_wout = dram("wout", [8, 128, E], F32)
    d_bout = dram("bout", [128, 8], F32)
    d_projw = dram("projw", [128, 8 * NEXP], F32)
    d_anchT = dram("anchT", [NEXP, NEXP], F32)
    d_rtemp = dram("rtemp", [1, 1], F32)
    d_ident = dram("ident", [128, 128], F32)
    d_csk = dram("csk", [128, 2 * S], BF16)
    d_csq = dram("csq", [128, 2 * OWN], BF16)
    d_maskt = dram("maskt", [NQG * 4, 128, 256], BF16)
    d_cinfo = dram("coreinfof", [1, 1], F32)
    d_out = dram("out", [8, 128, OWN], F32, "ExternalOutput")

    spk = "ExternalOutput" if "spill" in taps else "Internal"
    d_x1own = dram("x1own", [8, 128, OWN], F32, spk)
    d_yown = dram("yown", [8, 128, OWN], BF16, spk)
    x2k = "ExternalOutput" if "x2" in taps else "Internal"
    d_x2sp = dram("x2sp", [8, 128, OWN], F32, x2k)
    if "kv" in taps:
        d_kadbg = dram("kadbg", [128, S], BF16, "ExternalOutput")
        d_kbdbg = dram("kbdbg", [128, S], BF16, "ExternalOutput")
        d_vdbg = dram("vdbg", [128, NB * 260], BF16, "ExternalOutput")
    if "ew" in taps:
        d_ewdbg = dram("ewdbg", [1, NGRP * 1536], F32, "ExternalOutput")
        d_zdbg = dram("zdbg", [NEXP, S], F32, "ExternalOutput")
        d_r6dbg = dram("r6dbg", [128, 16 * NGRP], F32, "ExternalOutput")
        d_ewtdbg = dram("ewtdbg", [128, 16 * NGRP], F32, "ExternalOutput")
        d_q6dbg = dram("q6dbg", [NEXP, S], F32, "ExternalOutput")
        d_rscdbg = dram("rscdbg", [128, 64 * NGRP], F32, "ExternalOutput")

    from contextlib import ExitStack
    es = ExitStack()

    def sbt(n, s, d):
        return es.enter_context(nc.sbuf_tensor(n, s, d))

    def mksem(n):
        return es.enter_context(nc.semaphore(n))

    t_w1 = [sbt(f"w1_{e_}", [128, 8 * DEXP], BF16) for e_ in range(NEXP)]
    t_w2 = [sbt(f"w2_{e_}", [128, 4 * E], BF16) for e_ in range(NEXP)]
    t_wqa = sbt("wqas", [128, 8 * QL], BF16)
    t_wqb = sbt("wqbs", [QL, 2048], BF16)
    t_wkva = sbt("wkvas", [128, 8 * KVL], BF16)
    t_wkvbk = sbt("wkvbks", [KVL, 512], BF16)
    t_wkvbv = sbt("wkvbvs", [KVL, 256], BF16)
    t_bout = sbt("bouts", [128, 8], F32)
    t_projw = sbt("projws", [128, 8 * NEXP], F32)
    t_anchT = sbt("anchTs", [NEXP, NEXP], F32)
    t_ident = sbt("idents", [128, 128], F32)
    t_ka = sbt("ka", [128, S], BF16)
    t_kb = sbt("kb", [128, S], BF16)
    t_v = sbt("v", [128, NB * 260], BF16)
    t_onesr = sbt("onesr", [128, 128], F32)
    t_onesm = sbt("onesm", [1, 128], F32)
    t_onescol = sbt("onescol", [128, 1], F32)
    t_eps = sbt("epsc", [128, 1], F32)
    t_onescb = sbt("onescb", [128, 1], BF16)
    t_pw = sbt("pwsum", [1, NEXP], F32)
    t_alpha = sbt("alpha", [128, 1], F32)
    t_alpham = sbt("alpham", [128, 1], F32)
    t_rt = sbt("rtemps", [1, 1], F32)
    t_ci = sbt("cis", [1, 1], F32)
    t_hf = sbt("hflag", [128, 1], F32)

    def wout_chunk(ec):
        t = t_w1[ec] if ec < NEXP else t_w2[ec - NEXP]
        return t[:, 0:E]

    t_xga = sbt("xga", [128, 8 * GRP], F32)
    t_xgb = sbt("xgb", [128, 8 * GRP], F32)
    t_xsqp = sbt("xsqp", [128, 1024], BF16)
    t_lnbc = sbt("lnbc", [128, 512], BF16)
    t_xh = sbt("xh", [128, 8 * GRP], BF16)
    t_stat = sbt("stat", [1, 1536], F32)
    t_z = sbt("z", [NEXP, GRP], F32)
    t_q6 = sbt("q6", [NEXP, GRP], F32)
    t_r6 = sbt("r6", [128, 16], F32)
    t_rsc = sbt("rsc", [128, 64], F32)
    t_rs2 = sbt("rs2", [128, 16], F32)
    t_ew = sbt("ew", [128, 16], F32)
    t_ewf6 = sbt("ewf6", [1, NEXP * GRP], F32)
    t_ewb = sbt("ewb", [128, NEXP * GRP], BF16)
    t_h = sbt("h", [128, 4 * GRP], BF16)
    t_ht = sbt("ht", [128, NEXP * 4 * GRP], BF16)
    t_x1g = sbt("x1g", [128, 8 * GRP], F32)
    t_yh = sbt("yhg", [128, 8 * GRP], BF16)
    t_kva = sbt("kvag", [KVL, GRP], BF16)
    t_cs = sbt("cs", [128, 512], BF16)
    t_rtmp = sbt("rtmp", [128, 512], BF16)
    t_qa = sbt("qag", [QL, 256], BF16)

    t_xsq = t_ht[:, 0:4096].bitcast(F32)
    t_rc = t_stat[0:1, 0:1024]
    t_osp = t_ht[:, 0:2048].bitcast(F32)
    v_q = t_ht[:, 0:2048]
    v_ps = (t_ht[:, 2048:3072], t_ht[:, 3072:4096])
    v_a = t_ht[:, 4096:6144]
    v_rb = t_rtmp
    v_masks = t_ewb[:, 0:1024]
    v_x1b = t_xga
    v_yb = t_yh

    psA = es.enter_context(nc.psum_tensor("arena", [128, 4096], F32))

    def bk(i, w=512, p0=0, p1=128, off=0):
        return psA[p0:p1, i * 512 + off: i * 512 + off + w]

    s_dma = mksem("s_dma")
    s_dmag = mksem("s_dmag")
    s_sp = mksem("s_sp")
    s_pe = mksem("s_pe")
    s_act = mksem("s_act")
    s_dve = mksem("s_dve")
    s_fen = mksem("s_fen")
    fenc = [0]

    blk = es.enter_context(nc.Block())
    SEM_OF = {"tensor": s_pe, "scalar": s_act, "vector": s_dve}
    COMPUTE = [s_pe, s_act, s_dve]
    ALLS = [s_dma, s_dmag, s_sp, s_pe, s_act, s_dve]

    stepno = [0]
    writers = {}
    readers = {}
    SEMS = {s.name: s for s in ALLS}
    all_steps = []
    epoch = [0]

    class _Step:
        __slots__ = ("sid", "eng", "fn", "dma_sem", "ndma", "deps", "est",
                     "epoch", "sem", "amt", "cnt", "start", "finish")

        def __init__(s, sid, eng, fn, dma_sem, ndma, deps, est, ep):
            s.sid, s.eng, s.fn, s.dma_sem, s.ndma = sid, eng, fn, dma_sem, ndma
            s.deps, s.est, s.epoch = deps, est, ep
            s.sem = dma_sem if dma_sem is not None else SEM_OF[eng]
            s.amt = 16 * ndma if dma_sem is not None else 1

    EST = {"tensor": 1500, "scalar": 1200, "vector": 800, "sync": 4000,
           "gpsimd": 4000}
    import os
    SCHED_MODE = [os.environ.get("KSCHED", "list")]
    last_on_sem = {}
    bar_deps = [{}]

    def step(eng, deps_unused, fn, dma_sem=None, ndma=1, r=(), w=(), est=None):
        stepno[0] += 1
        if stepno[0] > STEP_LIMIT[0]:
            return
        annotated = bool(r) or bool(w)
        deps = set(bar_deps[0].values())
        if annotated:
            for res in r:
                if res in writers:
                    deps.add(writers[res])
            for res in w:
                if res in writers:
                    deps.add(writers[res])
                deps.update(readers.get(res, ()))
        else:
            deps.update(last_on_sem.values())
        st = _Step(stepno[0], eng, fn, dma_sem, ndma, deps,
                   est if est is not None else EST[eng], epoch[0])
        all_steps.append(st)
        last_on_sem[st.sem.name] = st
        if annotated:
            for res in r:
                readers.setdefault(res, []).append(st)
            for res in w:
                writers[res] = st
                readers[res] = []
        else:
            for res in set(list(writers) + list(readers)):
                writers[res] = st
                readers[res] = []
        return st

    def barrier():
        epoch[0] += 1
        bar_deps[0] = dict(last_on_sem)

    def flush_plan():
        import heapq
        order = {e: [] for e in plan_engines}
        eng_free = {e: 0.0 for e in plan_engines}
        maxep = max((s.epoch for s in all_steps), default=0)
        ep_end = 0.0
        mode = SCHED_MODE[0]
        for ep in range(maxep + 1):
            steps_ = [s for s in all_steps if s.epoch == ep]
            if not steps_:
                continue
            indeg = {}
            for s in steps_:
                indeg[s] = sum(1 for d in s.deps if d.epoch == ep)
            succs = {s: [] for s in steps_}
            for s in steps_:
                for d in s.deps:
                    if d.epoch == ep:
                        succs[d].append(s)
            lp = {}
            for s in sorted(steps_, key=lambda t: -t.sid):
                lp[s] = s.est + max((lp[t] for t in succs[s]), default=0.0)

            def key(s, rt):
                if mode == 'fifo':
                    return s.sid
                if mode == 'cp':
                    return (rt, -lp[s])
                if mode == 'cpb':
                    return (int(rt // 2000), -lp[s])
                return rt
            ready = []
            for s in steps_:
                if indeg[s] == 0:
                    rt = max([ep_end] + [d.finish for d in s.deps if d.epoch < ep])
                    heapq.heappush(ready, (key(s, rt), s.sid, s))
            scheduled = 0
            while ready:
                _, _, s = heapq.heappop(ready)
                rt = max([ep_end] + [d.finish for d in s.deps if d.epoch == ep] +
                         [d.finish for d in s.deps if d.epoch < ep])
                start = max(rt, eng_free[s.eng])
                s.start = start
                s.finish = start + s.est
                eng_free[s.eng] = s.finish
                order[s.eng].append(s)
                scheduled += 1
                for t in succs[s]:
                    indeg[t] -= 1
                    if indeg[t] == 0:
                        rt2 = max([ep_end] + [d.finish for d in t.deps])
                        heapq.heappush(ready, (key(t, rt2), t.sid, t))
            assert scheduled == len(steps_), (scheduled, len(steps_))
            ep_end = max(eng_free.values())
        PRED[:] = [ep_end]
        sem_eng = {}
        for e in plan_engines:
            c = {}
            for s in order[e]:
                assert sem_eng.setdefault(s.sem.name, e) == e, s.sem.name
                c[s.sem.name] = c.get(s.sem.name, 0) + s.amt
                s.cnt = c[s.sem.name]
        SCHED_DUMP.clear()
        for e in plan_engines:
            for pos, s in enumerate(order[e]):
                SCHED_DUMP.append((e, pos, s.sid, s.sem.name, s.cnt,
                                   sorted((d.eng, d.sid, d.sem.name, d.cnt)
                                          for d in s.deps)))
        for e in plan_engines:
            if not order[e]:
                continue

            def body(eng_h, steps_=order[e]):
                hw = {}
                for s in steps_:
                    dd = {}
                    for d in s.deps:
                        sn = d.sem.name
                        if dd.get(sn, 0) < d.cnt:
                            dd[sn] = d.cnt
                    for sn, c in sorted(dd.items()):
                        if hw.get(sn, 0) < c:
                            eng_h.wait_ge(SEMS[sn], c)
                            hw[sn] = c
                    rr = s.fn(eng_h)
                    if s.dma_sem is not None:
                        lst = rr if isinstance(rr, (list, tuple)) else [rr]
                        assert len(lst) == s.ndma, (len(lst), s.ndma)
                        for ins in lst:
                            ins.then_inc(s.dma_sem, 16)
                    else:
                        rr.then_inc(s.sem, 1)
            getattr(blk, e)(body)

    plan_engines = ("sync", "gpsimd", "tensor", "scalar", "vector")


    step("sync", [], lambda e: [
        e.dma_start(out=t_ident[:], in_=d_ident),
        e.dma_start(out=t_anchT[:], in_=d_anchT),
        e.dma_start(out=t_bout[:], in_=d_bout),
        e.dma_start(out=t_projw[:], in_=d_projw),
        e.dma_start(out=t_rt[:], in_=d_rtemp),
        e.dma_start(out=t_ci[:], in_=d_cinfo),
    ], dma_sem=s_dma, ndma=6, w=("ident", "anchT", "bout", "projw", "smallw"))

    def f_init_gp(e):
        r = []
        for x_ in range(NEXP):
            r.append(e.dma_start(out=t_w1[x_][:].rearrange("p (c j) -> p c j", c=8), in_=d_w1[x_].rearrange("c p j -> p c j")))
            r.append(e.dma_start(out=t_w2[x_][:].rearrange("p (c j) -> p c j", c=4), in_=d_w2[x_].rearrange("c p j -> p c j")))
        r.append(e.dma_start(out=t_wqa[:].rearrange("p (c j) -> p c j", c=8), in_=d_wqa.rearrange("c p j -> p c j")))
        r.append(e.dma_start(out=t_wqb[:], in_=d_wqb))
        r.append(e.dma_start(out=t_wkva[:].rearrange("p (c j) -> p c j", c=8), in_=d_wkva.rearrange("c p j -> p c j")))
        r.append(e.dma_start(out=t_wkvbk[:], in_=d_wkvbk))
        r.append(e.dma_start(out=t_wkvbv[:], in_=d_wkvbv))
        return r
    step("gpsimd", [], f_init_gp, dma_sem=s_dmag, ndma=17, w=("wsp",))

    def f_init_dve(e):
        e.memset(t_onesr[:], 1.0)
        e.memset(t_onesm[:], -1.0)
        e.memset(t_onescol[:], 1.0)
        e.memset(t_eps[:], EPS)
        e.memset(t_onescb[:], 1.0)
        vones = t_v[:].rearrange("p (t k j) -> p t k j", t=NB, k=4)[:, :, :, 64:65]
        return e.memset(vones, 1.0)
    step("vector", [], f_init_dve, w=("consts", "v"))

    def f_init_pe(e):
        for c in range(8):
            e.matmul(bk(0, NEXP, 0, 1), t_onescol[:, 0:1],
                     t_projw[:, c*NEXP:(c+1)*NEXP], start=(c == 0), stop=(c == 7))
        e.matmul(bk(1, 1), t_onesr[0:1, 0:128], t_rt[0:1, 0:1],
                 start=True, stop=True)
        return e.matmul(bk(1, 1, off=1), t_onesr[0:1, 0:128], t_ci[0:1, 0:1],
                        start=True, stop=True)
    step("tensor", [s_dma, s_dmag], f_init_pe, r=("projw", "consts", "smallw"), w=("pb0", "pb1"))

    def f_init_dve2(e):
        e.tensor_copy(t_pw[0:1, 0:NEXP], bk(0, NEXP, 0, 1))
        e.tensor_copy(t_hf[:, 0:1], bk(1, 1, off=1))
        e.tensor_scalar_max(t_rsc[:, 0:1], bk(1, 1), 0.1)
        e.drain()
        e.reciprocal(t_rsc[:, 1:2], t_rsc[:, 0:1])
        e.drain()
        e.tensor_scalar_mul(t_alpha[:, 0:1], t_rsc[:, 1:2], 0.5)
        e.drain()
        return e.tensor_scalar_mul(t_alpham[:, 0:1], t_alpha[:, 0:1], -1.0)
    step("vector", [s_pe], f_init_dve2, r=("pb0", "pb1"), w=("pw", "hf", "alpha", "rsc"))

    def ln_block(src_c, dst, router_extras, srcres, dstres, src_t=None):
        n = GRP

        def f_sums(e):
            r = None
            for c in range(8):
                r = e.matmul(bk(0, n, 0, 1), t_onescol[:, 0:1], src_c(c),
                             start=(c == 0), stop=(c == 7))
            return r
        step("tensor", [s_dma, s_dve], f_sums, r=srcres + ("consts",), w=("pb0",))
        for h in range(4):
            pp = h % 2
            step("scalar", [], lambda e, h=h, pp=pp: e.activation(
                t_xsqp[:, pp*512:pp*512+512], src_t[:, 2*h*n:(2*h+2)*n],
                ACT.Square),
                 r=srcres, w=("xsqp%d" % pp,))

            def f_ssum(e, h=h, pp=pp):
                e.matmul(bk(0, n, 0, 1, off=n), t_onescb[:, 0:1],
                         t_xsqp[:, pp*512:pp*512+256],
                         start=(h == 0), stop=False)
                return e.matmul(bk(0, n, 0, 1, off=n), t_onescb[:, 0:1],
                                t_xsqp[:, pp*512+256:pp*512+512],
                                start=False, stop=(h == 3))
            step("tensor", [], f_ssum, r=("xsqp%d" % pp, "consts"), w=("pb0",))

        def f_mv(e):
            e.tensor_scalar_mul(t_stat[0:1, 0:n], bk(0, n, 0, 1), 1.0 / E)
            e.drain()
            e.tensor_mul(t_stat[0:1, 256:256+n], t_stat[0:1, 0:n], t_stat[0:1, 0:n])
            e.drain()
            return e.scalar_tensor_tensor(
                t_stat[0:1, 512:512+n], bk(0, n, 0, 1, off=n), 1.0 / E,
                t_stat[0:1, 256:256+n], OP.mult, OP.subtract)
        step("vector", [s_pe], f_mv, r=("pb0",), w=("stat",))

        def f_rstd(e):
            e.activation(t_stat[0:1, 256:256+n], t_stat[0:1, 512:512+n], ACT.Ln,
                         bias=t_eps[0:1, 0:1])
            e.drain()
            r = e.activation(t_stat[0:1, 768:768+n], t_stat[0:1, 256:256+n],
                             ACT.Exp, scale=-0.5)
            if router_extras:
                r = e.activation(t_stat[0:1, 1280:1280+n], t_stat[0:1, 0:n],
                                 ACT.Copy, scale=-1.0)
            return r
        step("scalar", [s_dve], f_rstd, r=("stat", "consts"), w=("stat",))

        def f_chain3(e):
            return e.tensor_mul(t_stat[0:1, 1024:1024+n], t_stat[0:1, 0:n],
                                t_stat[0:1, 768:768+n])
        step("vector", [s_act], f_chain3, r=("stat",), w=("stat",))

        def f_bc(e):
            e.matmul(bk(1, n), t_onesr[0:1, 0:128], t_stat[0:1, 768:768+n],
                     start=True, stop=True)
            return e.matmul(bk(1, n, off=256), t_onesm[0:1, 0:128],
                            t_stat[0:1, 1024:1024+n], start=True, stop=True)
        step("tensor", [s_dve], f_bc, r=("stat", "consts"), w=("pb1",))

        def f_apply(e):
            e.tensor_copy(t_lnbc[:, 0:512], bk(1, 512))
            e.drain()
            if src_t is not None:
                s3 = src_t[:, 0:8*n].rearrange("p (c j) -> p c j", c=8)
                d3 = dst[:, 0:8*n].rearrange("p (c j) -> p c j", c=8)
                rb3 = t_lnbc[:, 0:n].unsqueeze(1).to_broadcast([128, 8, n])
                mb3 = t_lnbc[:, 256:256+n].unsqueeze(1).to_broadcast([128, 8, n])
                e.tensor_tensor(d3, s3, rb3, OP.mult)
                e.drain()
                return e.tensor_tensor(d3, d3, mb3, OP.add)
            def mul(c):
                e.tensor_tensor(t_rtmp[:, (c % 2)*n:(c % 2)*n+n], src_c(c),
                                t_lnbc[:, 0:n], OP.mult)
            def add(c):
                return e.tensor_tensor(dst[:, c*n:(c+1)*n],
                                       t_rtmp[:, (c % 2)*n:(c % 2)*n+n],
                                       t_lnbc[:, 256:256+n], OP.add)
            mul(0)
            for c in range(7):
                mul(c + 1)
                add(c)
            return add(7)
        step("vector", [s_pe], f_apply, r=srcres + ("pb1",), w=("lnbc", dstres, "rtmp"))

    def router_block(src_c, srcres):
        n = GRP

        def f_z(e):
            for c in range(8):
                e.matmul(bk(2, n, 0, NEXP), t_projw[:, c*NEXP:(c+1)*NEXP], src_c(c),
                         start=(c == 0), stop=False)
            return e.matmul(bk(2, n, 0, NEXP), t_pw[0:1, 0:NEXP],
                            t_stat[0:1, 1280:1280+n], start=False, stop=True)
        step("tensor", [s_dve], f_z, r=srcres + ("projw", "pw", "stat"), w=("pb2",))

        def f_zev(e):
            e.tensor_copy(t_q6[0:NEXP, 0:n], bk(2, n, 0, NEXP))
            e.drain()
            return e.tensor_tensor(t_z[0:NEXP, 0:n], t_q6[0:NEXP, 0:n],
                                   bk(1, n, 0, NEXP), OP.mult)
        step("vector", [s_pe], f_zev, r=("pb2", "pb1"), w=("q6", "z"))

        step("scalar", [s_dve], lambda e: e.activation(
            t_q6[0:NEXP, 0:n], t_z[0:NEXP, 0:n], ACT.Exp, scale=2.0 * QTEMP_INV),
             r=("z",), w=("q6",))

        def f_tanh2(e):
            e.tensor_scalar_add(t_z[0:NEXP, 0:n], t_q6[0:NEXP, 0:n], 1.0)
            e.drain()
            e.reciprocal(t_q6[0:NEXP, 0:n], t_z[0:NEXP, 0:n])
            e.drain()
            return e.tensor_scalar(t_q6[0:NEXP, 0:n], t_q6[0:NEXP, 0:n], -2.0, 1.0,
                                   OP.mult, OP.add)
        step("vector", [s_act], f_tanh2, r=("q6", "z"), w=("q6", "z"))

        def f_dot(e):
            r = None
            for b2 in range(2):
                r = e.matmul(bk(2, NEXP, 0, 128, off=256 + b2*NEXP),
                             t_q6[0:NEXP, b2*128:(b2+1)*128], t_anchT[0:NEXP, 0:NEXP],
                             start=True, stop=True)
            return r
        step("tensor", [s_act], f_dot, r=("q6", "anchT"), w=("pb2",))

        def f_tok(e):
            R = lambda b2: t_r6[:, b2*8:b2*8+NEXP]
            for b2 in range(2):
                e.tensor_copy(R(b2), bk(2, NEXP, 0, 128, off=256 + b2*NEXP))
            for b2 in range(2):
                e.tensor_reduce(t_rs2[:, b2*8+0:b2*8+1], R(b2), AX, OP.max)
            for b2 in range(2):
                e.tensor_scalar(t_rsc[:, b2*32+8:b2*32+8+NEXP], R(b2),
                                t_rs2[:, b2*8+0:b2*8+1], 0.0, OP.subtract, OP.is_equal)
            for b2 in range(2):
                e.scalar_tensor_tensor(t_rsc[:, b2*32+16:b2*32+16+NEXP],
                                       t_rsc[:, b2*32+8:b2*32+8+NEXP], -1e30, R(b2),
                                       OP.mult, OP.add)
            for b2 in range(2):
                e.tensor_reduce(t_rs2[:, b2*8+1:b2*8+2],
                                t_rsc[:, b2*32+16:b2*32+16+NEXP], AX, OP.max)
            for b2 in range(2):
                rr = e.tensor_tensor(t_rs2[:, b2*8+2:b2*8+3], t_rs2[:, b2*8+0:b2*8+1],
                                     t_rs2[:, b2*8+1:b2*8+2], OP.subtract)
            return rr
        step("vector", [s_pe], f_tok, r=("pb2",), w=("r6", "rsc", "rs2"))

        def f_sig(e):
            for b2 in range(2):
                r = e.activation(t_rs2[:, b2*8+5:b2*8+6], t_rs2[:, b2*8+2:b2*8+3],
                                 ACT.Exp, scale=t_alpham[:, 0:1])
            return r
        step("scalar", [s_dve], f_sig, r=("rs2", "alpha"), w=("rs2",))

        def f_ew(e):
            for b2 in range(2):
                e.tensor_scalar_add(t_rsc[:, b2*32:b2*32+1], t_rs2[:, b2*8+5:b2*8+6], 1.0)
            for b2 in range(2):
                e.reciprocal(t_rs2[:, b2*8+3:b2*8+4], t_rsc[:, b2*32:b2*32+1])
            for b2 in range(2):
                e.tensor_scalar(t_rs2[:, b2*8+4:b2*8+5], t_rs2[:, b2*8+3:b2*8+4],
                                -1.0, 1.0, OP.mult, OP.add)
            for b2 in range(2):
                e.tensor_scalar(t_rsc[:, b2*32+24:b2*32+24+NEXP],
                                t_rsc[:, b2*32+16:b2*32+16+NEXP],
                                t_rs2[:, b2*8+1:b2*8+2], 0.0, OP.subtract, OP.is_equal)
            for b2 in range(2):
                e.tensor_scalar(t_rsc[:, b2*32+16:b2*32+16+NEXP],
                                t_rsc[:, b2*32+24:b2*32+24+NEXP],
                                t_rs2[:, b2*8+4:b2*8+5], None, OP.mult)
            for b2 in range(2):
                r = e.scalar_tensor_tensor(t_ew[:, b2*8:b2*8+NEXP],
                                           t_rsc[:, b2*32+8:b2*32+8+NEXP],
                                           t_rs2[:, b2*8+3:b2*8+4],
                                           t_rsc[:, b2*32+16:b2*32+16+NEXP],
                                           OP.mult, OP.add)
            return r
        step("vector", [s_act], f_ew, r=("rsc", "rs2"), w=("rsc", "rs2", "ew"))

        def f_ewT(e):
            r = None
            for x_ in range(NEXP):
                for b2 in range(2):
                    r = e.transpose(psA[0:1, 1024 + x_*256 + b2*128:
                                        1024 + x_*256 + b2*128 + 128],
                                    t_ew[:, b2*8 + x_: b2*8 + x_ + 1], t_ident[:])
            return r
        step("tensor", [s_dve], f_ewT, r=("ew", "ident"), w=("pb2", "pb3", "pb4"))

        def f_ewf(e):
            r = None
            for x_ in range(NEXP):
                for b2 in range(2):
                    o = x_*256 + b2*128
                    r = e.tensor_copy(t_ewf6[0:1, o:o+128],
                                      psA[0:1, 1024 + o: 1024 + o + 128])
            return r
        step("vector", [s_pe], f_ewf, r=("pb2", "pb3", "pb4"), w=("ewf",))

    def ffn_block(evac, xgres):
        n = GRP
        for x0 in (0, 2, 4):
            eb = 2 + (x0 // 2) % 2
            def f_ewbc(e, x0=x0, eb=eb):
                e.matmul(bk(eb, n), t_onesr[0:1, 0:128], t_ewf6[0:1, x0*n:(x0+1)*n],
                         start=True, stop=True)
                return e.matmul(bk(eb, n, off=256), t_onesr[0:1, 0:128],
                                t_ewf6[0:1, (x0+1)*n:(x0+2)*n], start=True, stop=True)
            step("tensor", [s_dve], f_ewbc, r=("ewf", "consts"), w=("pb%d" % eb,))

            def f_ewbev(e, x0=x0, eb=eb):
                e.tensor_copy(t_ewb[:, x0*n:(x0+1)*n], bk(eb, n))
                return e.tensor_copy(t_ewb[:, (x0+1)*n:(x0+2)*n], bk(eb, n, off=256))
            step("vector", [s_pe], f_ewbev, r=("pb%d" % eb,), w=("ewb",))

        for x_ in range(NEXP):
            hb = 4 if x_ % 2 == 0 else 6

            def f_w1(e, x_=x_, hb=hb):
                r = None
                for m_ in range(4):
                    for c in range(8):
                        r = e.matmul(psA[0:128, hb*512 + m_*n: hb*512 + (m_+1)*n],
                                     t_w1[x_][:, c*512 + m_*128: c*512 + (m_+1)*128],
                                     t_xh[:, c*n:(c+1)*n], start=(c == 0), stop=(c == 7))
                return r
            step("tensor", [s_dve, s_act], f_w1, r=("xh", "wsp"),
                 w=(("pb4", "pb5") if hb == 4 else ("pb6", "pb7")))

            def f_gelu(e, x_=x_, hb=hb):
                return e.activation(t_h[:, 0:4*n], psA[0:128, hb*512: hb*512 + 4*n],
                                    ACT.Gelu_apprx_tanh)
            step("scalar", [s_pe], f_gelu,
                 r=(("pb4", "pb5") if hb == 4 else ("pb6", "pb7")), w=("h",))

            def f_ht(e, x_=x_):
                base = t_ewb[:, x_*n:(x_+1)*n]
                bc = base.unsqueeze(1).to_broadcast([128, 4, n])
                hh = t_h[:, 0:4*n].rearrange("p (m j) -> p m j", m=4)
                dd = t_ht[:, x_*4*n:(x_+1)*4*n].rearrange("p (m j) -> p m j", m=4)
                return e.tensor_tensor(dd, hh, bc, OP.mult)
            step("vector", [s_act], f_ht, r=("h", "ewb"), w=("ht",))

        for oc in range(8):
            ob = oc % 2
            def f_w2(e, oc=oc, ob=ob):
                first = True
                r = None
                for x_ in range(NEXP):
                    for k_ in range(4):
                        r = e.matmul(bk(ob, n), t_w2[x_][:, k_*E + oc*128: k_*E + (oc+1)*128],
                                     t_ht[:, (x_*4 + k_)*n:(x_*4 + k_ + 1)*n],
                                     start=first, stop=(x_ == NEXP-1 and k_ == 3))
                        first = False
                return r
            step("tensor", [s_dve], f_w2, r=("ht", "wsp"), w=("pb%d" % ob,))
            step("vector", [s_pe], lambda e, oc=oc, ob=ob: evac(e, oc, ob),
                 r=("pb%d" % ob,) + xgres, w=("x1g",))

    _sidA = nc.enter_named_scope("phaseA", False)[0]
    src_x1g = lambda c: t_x1g[:, c*GRP:(c+1)*GRP]

    def mk_src(t):
        return lambda c: t[:, c*GRP:(c+1)*GRP]

    def mk_evac(t):
        def ev(e, oc, ob):
            return e.scalar_tensor_tensor(t_x1g[:, oc*GRP:(oc+1)*GRP], bk(ob, GRP),
                                          0.5, t[:, oc*GRP:(oc+1)*GRP],
                                          OP.mult, OP.add)
        return ev

    for g in range(ngrp_a):
        t_xg = t_xga if g % 2 == 0 else t_xgb
        xgres = ("xga", "xgah") if g % 2 == 0 else ("xgb", "xgbh")
        src_xg = mk_src(t_xg)

        if g < 2:
            def f_load(e, g=g, t_xg=t_xg):
                return [
                    e.dma_start(out=t_xg[:].rearrange("p (c j) -> p c j", c=8),
                                in_=d_x[:, :, g*GRP:(g+1)*GRP].rearrange("c p j -> p c j")),
                    e.dma_start(out=t_cs[:], in_=d_csk[:, g*512:(g+1)*512]),
                ]
            step("sync", COMPUTE, f_load, dma_sem=s_dma, ndma=2,
                 w=xgres + ("cs",))
        else:
            def f_load(e, g=g, t_xg=t_xg):
                return [
                    e.dma_start(out=t_xg[:, 0:4*GRP].rearrange("p (c j) -> p c j", c=4),
                                in_=d_x[0:4, :, g*GRP:(g+1)*GRP].rearrange("c p j -> p c j")),
                    e.dma_start(out=t_cs[:], in_=d_csk[:, g*512:(g+1)*512]),
                ]
            step("sync", COMPUTE, f_load, dma_sem=s_dma, ndma=2,
                 w=(xgres[0], "cs"))

            def f_loadh(e, g=g, t_xg=t_xg):
                return e.dma_start(
                    out=t_xg[:, 4*GRP:8*GRP].rearrange("p (c j) -> p c j", c=4),
                    in_=d_x[4:8, :, g*GRP:(g+1)*GRP].rearrange("c p j -> p c j"))
            step("gpsimd", COMPUTE, f_loadh, dma_sem=s_dmag, ndma=1,
                 w=(xgres[1],))

        ln_block(src_xg, t_xh, True, xgres, "xh", src_t=t_xg)
        router_block(src_xg, xgres)
        ffn_block(mk_evac(t_xg), xgres)

        s0 = (g % 2) * 128
        s1 = 128 - s0
        def f_blendx(e, g=g, s0=s0, s1=s1):
            x3 = t_x1g[:, 0:8*GRP].rearrange("p (c j) -> p c j", c=8)
            o3 = t_osp[:, 0:1024].rearrange("p (c j) -> p c j", c=8)
            e.tensor_tensor(o3, x3[:, :, s1:s1+128], x3[:, :, s0:s0+128],
                            OP.subtract)
            e.drain()
            return e.scalar_tensor_tensor(o3, o3, t_hf[:, 0:1],
                                          x3[:, :, s0:s0+128], OP.mult, OP.add)
        step("vector", [s_dve], f_blendx, r=("x1g", "hf"), w=("ht",))

        def f_spill_x1(e, g=g):
            return e.dma_start(out=d_x1own[:, :, g*128:(g+1)*128].rearrange("c p j -> p c j"),
                               in_=t_osp[:].rearrange("p (c j) -> p c j", c=8))
        step("sync", [s_dve], f_spill_x1, dma_sem=s_sp, r=("ht",), w=("d_x1own",))

        if "ew" in taps:
            def f_ewtap(e, g=g):
                return [
                    e.dma_start(out=d_ewdbg[0:1, g*1536:(g+1)*1536],
                                in_=t_ewf6[0:1, 0:1536]),
                    e.dma_start(out=d_zdbg[:, g*GRP:(g+1)*GRP], in_=t_z[0:NEXP, 0:GRP]),
                    e.dma_start(out=d_q6dbg[:, g*GRP:(g+1)*GRP], in_=t_q6[0:NEXP, 0:GRP]),
                    e.dma_start(out=d_r6dbg[:, g*16:(g+1)*16], in_=t_r6[:, 0:16]),
                    e.dma_start(out=d_ewtdbg[:, g*16:(g+1)*16], in_=t_ew[:, 0:16]),
                    e.dma_start(out=d_rscdbg[:, g*64:g*64+16], in_=t_rs2[:, 0:16]),
                    e.dma_start(out=d_rscdbg[:, g*64+16:(g+1)*64], in_=t_rsc[:, 0:48]),
                ]
            step("sync", [s_dve], f_ewtap, dma_sem=s_sp, ndma=7)

        ln_block(src_x1g, t_yh, False, ("x1g",), "yh", src_t=t_x1g)

        def f_blendy(e, g=g, s0=s0, s1=s1):
            yb = t_osp[:, 0:512].bitcast(BF16)
            y3 = t_yh[:, 0:8*GRP].rearrange("p (c j) -> p c j", c=8)
            o3 = yb[:, 0:1024].rearrange("p (c j) -> p c j", c=8)
            e.tensor_tensor(o3, y3[:, :, s1:s1+128], y3[:, :, s0:s0+128],
                            OP.subtract)
            e.drain()
            return e.scalar_tensor_tensor(o3, o3, t_hf[:, 0:1],
                                          y3[:, :, s0:s0+128], OP.mult, OP.add)
        step("vector", [s_dve], f_blendy, r=("yh", "hf"), w=("ht",))

        def f_spill_y(e, g=g):
            yb = t_osp[:, 0:512].bitcast(BF16)
            return e.dma_start(out=d_yown[:, :, g*128:(g+1)*128].rearrange("c p j -> p c j"),
                               in_=yb.rearrange("p (c j) -> p c j", c=8))
        step("sync", [s_dve], f_spill_y, dma_sem=s_sp, r=("ht",), w=("d_yown",))

        def f_kva(e):
            r = None
            for c in range(8):
                r = e.matmul(bk(2, GRP, 0, KVL), t_wkva[:, c*KVL:(c+1)*KVL],
                             t_yh[:, c*GRP:(c+1)*GRP], start=(c == 0), stop=(c == 7))
            return r
        step("tensor", [s_dve], f_kva, r=("yh", "wsp"), w=("pb2",))
        step("vector", [s_pe], lambda e: e.tensor_copy(
            t_kva[0:KVL, 0:GRP], bk(2, GRP, 0, KVL)), r=("pb2",), w=("kva",))

        for base, dst, kres in ((0, t_ka, "ka"), (256, t_kb, "kb")):
            def f_k(e, base=base):
                e.matmul(bk(3, 256), t_wkvbk[:, base:base+128], t_kva[0:KVL, 0:GRP],
                         start=True, stop=True)
                return e.matmul(bk(3, 256, off=256), t_wkvbk[:, base+128:base+256],
                                t_kva[0:KVL, 0:GRP], start=True, stop=True)
            step("tensor", [s_dve], f_k, r=("wsp", "kva"), w=("pb3",))

            def f_krope(e, dst=dst, g=g):
                e.tensor_tensor(t_rtmp[:, 0:512], bk(3, 512), t_cs[:, 0:512], OP.mult)
                e.drain()
                return e.tensor_tensor(dst[:, g*GRP:(g+1)*GRP], t_rtmp[:, 0:256],
                                       t_rtmp[:, 256:512], OP.add)
            step("vector", [s_pe], f_krope, r=("pb3", "cs"), w=("rtmp", kres))

        for tl in range(2):
            def f_v(e, tl=tl):
                return e.matmul(bk(2, 256, off=256), t_kva[0:KVL, tl*128:(tl+1)*128],
                                t_wkvbv[0:KVL, 0:256], start=True, stop=True)
            step("tensor", [s_dve], f_v, r=("kva", "wsp"), w=("pb2",))

            def f_vev(e, tl=tl, g=g):
                dstv = t_v[:].rearrange("p (t k j) -> p t k j", t=NB, k=4,
                                        j=65)[:, 2*g+tl, :, 0:64]
                return e.tensor_copy(dstv, bk(2, 256, off=256).rearrange(
                    "p (k j) -> p k j", k=4))
            step("vector", [s_pe], f_vev, r=("pb2",), w=("v",))

    if "kv" in taps:
        step("sync", [s_dve], lambda e: [
            e.dma_start(out=d_kadbg, in_=t_ka[:]),
            e.dma_start(out=d_kbdbg, in_=t_kb[:]),
            e.dma_start(out=d_vdbg, in_=t_v[:]),
        ], dma_sem=s_sp, ndma=3)

    nc.leave_named_scope("phaseA", _sidA, False)
    _sidB = nc.enter_named_scope("phaseB", False)[0]
    barrier()
    for nm_ in ("q", "p0", "p1", "a"):
        if "ht" in writers:
            writers[nm_] = writers["ht"]
        readers[nm_] = list(readers.get("ht", ()))
    def f_woutload(e):
        return [e.dma_start(out=wout_chunk(ec), in_=d_wout[ec])
                for ec in range(8)]
    step("gpsimd", [], f_woutload, dma_sem=s_dmag, ndma=8, w=("wsp",))

    if not do_b:
        nqg = 0
    step("sync", COMPUTE + [s_dma], lambda e: [
        e.dma_start(out=t_cs[:], in_=d_csq[:, 0:512])], dma_sem=s_dma, w=("cs",))

    for i in range(nqg):
        p_ = i % 2
        vx1 = t_xga if p_ == 0 else t_xgb
        xres = "xga" if p_ == 0 else "xgb"
        xresh = "xgah" if p_ == 0 else "xgbh"
        vmk = t_ewb[:, 0:1024] if p_ == 0 else t_xh[:, 0:1024]
        mres = "ewb" if p_ == 0 else "ewb2"
        vcs = t_cs if p_ == 0 else t_xh[:, 1024:1536]
        cres = "cs" if p_ == 0 else "cs2"

        def f_bload(e, i=i, vx1=vx1, vmk=vmk, vcs=vcs):
            r = [
                e.dma_start(
                    out=vx1[:].rearrange("p (c j) -> p c j", c=8),
                    in_=d_x1own[:, :, i*256:(i+1)*256].rearrange("c p j -> p c j")),
                e.dma_start(
                    out=vmk.rearrange("p (t j) -> p t j", t=4),
                    in_=d_maskt[4*i:4*i+4].rearrange("t p j -> p t j")),
            ]
            if i > 0:
                r.append(e.dma_start(out=vcs[:, 0:512], in_=d_csq[:, i*512:(i+1)*512]))
            return r
        step("sync", COMPUTE, f_bload, dma_sem=s_dma, est=6000,
             ndma=3 if i > 0 else 2, r=("d_x1own",),
             w=(xres, xresh, mres) + ((cres,) if i > 0 else ()))

        def f_bloady(e, i=i):
            return e.dma_start(
                out=v_yb[:].rearrange("p (c j) -> p c j", c=8),
                in_=d_yown[:, :, i*256:(i+1)*256].rearrange("c p j -> p c j"))
        step("gpsimd", COMPUTE, f_bloady, dma_sem=s_dmag, ndma=1,
             r=("d_yown",), w=("yh",), est=5000)

        def f_qa(e):
            r = None
            for c in range(8):
                r = e.matmul(bk(0, 256, 0, QL), t_wqa[:, c*QL:(c+1)*QL],
                             v_yb[:, c*256:(c+1)*256], start=(c == 0), stop=(c == 7))
            return r
        step("tensor", [s_dma], f_qa, r=("yh", "wsp"), w=("pb0",))
        step("vector", [s_pe], lambda e: e.tensor_copy(t_qa[0:QL, 0:256],
                                                       bk(0, 256, 0, QL)),
             r=("pb0",), w=("qa",))

        for c in range(8):
            def f_qc(e, c=c):
                e.matmul(bk(1, 256), t_wqb[0:QL, c*128:(c+1)*128], t_qa[0:QL, 0:256],
                         start=True, stop=True)
                return e.matmul(bk(1, 256, off=256),
                                t_wqb[0:QL, 1024 + c*128: 1024 + (c+1)*128],
                                t_qa[0:QL, 0:256], start=True, stop=True)
            step("tensor", [s_dve], f_qc, r=("wsp", "qa"), w=("pb1",))

            def f_qrope(e, c=c, vcs=vcs):
                e.tensor_tensor(t_rtmp[:, 0:512], bk(1, 512), vcs[:, 0:512], OP.mult)
                e.drain()
                return e.tensor_tensor(v_q[:, c*256:(c+1)*256], t_rtmp[:, 0:256],
                                       t_rtmp[:, 256:512], OP.add)
            step("vector", [s_pe], f_qrope, r=("pb1", cres), w=("rtmp", "q"))

        ktmax = 4 * i + 4
        for cp in range(4):
            Ksrc = t_ka if cp < 2 else t_kb
            kvlo = cp // 2
            kvhi = kvlo + 2

            def qk(e, kt, cp=cp, Ksrc=Ksrc):
                sb = (kt % 2) * 2
                e.matmul(bk(sb, 256), Ksrc[0:64, kt*128:(kt+1)*128],
                         v_q[0:64, (2*cp)*256:(2*cp+1)*256], start=True, stop=True,
                         tile_position=(0, 0))
                e.matmul(bk(sb, 256, off=256), Ksrc[0:64, kt*128:(kt+1)*128],
                         v_q[0:64, (2*cp+1)*256:(2*cp+2)*256], start=True, stop=True,
                         tile_position=(0, 0))
                e.matmul(bk(sb + 1, 256), Ksrc[64:128, kt*128:(kt+1)*128],
                         v_q[64:128, (2*cp)*256:(2*cp+1)*256], start=True, stop=True,
                         tile_position=(64, 0))
                return e.matmul(bk(sb + 1, 256, off=256), Ksrc[64:128, kt*128:(kt+1)*128],
                                v_q[64:128, (2*cp+1)*256:(2*cp+2)*256], start=True,
                                stop=True, tile_position=(64, 0))

            def sres(kt):
                return ("pb0", "pb1") if kt % 2 == 0 else ("pb2", "pb3")

            def f_tzero(e):
                e.memset(psA[0:128, 4*512:5*512], 0.0)
                return e.memset(psA[0:128, 5*512:6*512], 0.0)
            step("vector", [], f_tzero, w=("pb4", "pb5"))

            step("tensor", [], lambda e, qk=qk: qk(e, 0),
                 r=("q", "ka", "kb"), w=sres(0))
            if ktmax > 1:
                step("tensor", [], lambda e, qk=qk: qk(e, 1),
                     r=("q", "ka", "kb"), w=sres(1))

            for kt in range(ktmax):
                pb = kt % 2
                step("scalar", [], lambda e, pb=pb: e.activation(
                    v_ps[pb][:, 0:1024], psA[0:128, pb*1024:(pb+1)*1024], ACT.Exp,
                    scale=SCALE),
                     r=sres(kt), w=("p%d" % pb,))

                if kt >= 4 * i:
                    j = kt - 4 * i

                    def f_mask(e, j=j, pb=pb, vmk=vmk):
                        mb = vmk[:, j*256:(j+1)*256]
                        bc4 = mb.unsqueeze(1).to_broadcast([128, 4, 256])
                        pv = v_ps[pb][:, 0:1024].rearrange("p (s j) -> p s j", s=4)
                        return e.tensor_tensor(pv, pv, bc4, OP.mult)
                    step("vector", [], f_mask, r=("p%d" % pb, mres), w=("p%d" % pb,))

                def f_av(e, kt=kt, cp=cp, kvlo=kvlo, kvhi=kvhi, qk=qk, pb=pb,
                         last=(kt == ktmax - 1), pre=(kt + 2 < ktmax)):
                    st = (kt == 0)
                    sp_ = last
                    vp = v_ps[pb]
                    e.matmul(psA[0:64, 4*512: 4*512+256],
                             t_v[:, kt*260 + kvlo*65: kt*260 + kvlo*65 + 64],
                             vp[:, 0:256], start=False, stop=sp_, tile_position=(0, 0),
                             skip_group_check=True)
                    e.matmul(psA[64:128, 4*512: 4*512+256],
                             t_v[:, kt*260 + kvlo*65: kt*260 + kvlo*65 + 64],
                             vp[:, 256:512], start=False, stop=sp_, tile_position=(0, 64),
                             skip_group_check=True)
                    e.matmul(psA[0:64, 5*512: 5*512+256],
                             t_v[:, kt*260 + kvhi*65: kt*260 + kvhi*65 + 64],
                             vp[:, 512:768], start=False, stop=sp_, tile_position=(0, 0),
                             skip_group_check=True)
                    e.matmul(psA[64:128, 5*512: 5*512+256],
                             t_v[:, kt*260 + kvhi*65: kt*260 + kvhi*65 + 64],
                             vp[:, 768:1024], start=False, stop=sp_, tile_position=(0, 64),
                             skip_group_check=True)
                    e.matmul(bk(6, 512, 0, 1), t_onescb[:, 0:1], vp[:, 0:512],
                             start=st, stop=sp_)
                    r = e.matmul(bk(7, 512, 0, 1), t_onescb[:, 0:1], vp[:, 512:1024],
                                 start=st, stop=sp_)
                    if pre:
                        r = qk(e, kt + 2)
                    return r
                avw = ("pb4", "pb5", "pb6", "pb7") + (sres(kt) if kt + 2 < ktmax else ())
                step("tensor", [], f_av, r=("p%d" % pb, "v", "consts", "q", "ka", "kb"),
                     w=avw)

            def f_rc(e):
                return e.reciprocal(t_rc[0:1, 0:1024], psA[0:1, 6*512:8*512])
            step("vector", [], f_rc, r=("pb6", "pb7"), w=("stat",))

            def f_rb(e):
                e.matmul(bk(6, 256, 0, 64), t_onesr[0:1, 0:64], t_rc[0:1, 0:256],
                         start=True, stop=True, tile_position=(0, 0))
                e.matmul(bk(6, 256, 64, 128), t_onesr[0:1, 0:64], t_rc[0:1, 256:512],
                         start=True, stop=True, tile_position=(0, 64))
                e.matmul(bk(6, 256, 0, 64, off=256), t_onesr[0:1, 0:64],
                         t_rc[0:1, 512:768], start=True, stop=True, tile_position=(0, 0))
                return e.matmul(bk(6, 256, 64, 128, off=256), t_onesr[0:1, 0:64],
                                t_rc[0:1, 768:1024], start=True, stop=True,
                                tile_position=(0, 64))
            step("tensor", [], f_rb, r=("stat", "consts"), w=("pb6",))

            def f_norm(e, cp=cp):
                e.tensor_copy(v_rb[:, 0:512], bk(6, 512))
                e.drain()
                e.tensor_tensor(v_a[0:64, (2*cp)*256:(2*cp+1)*256],
                                psA[0:64, 4*512: 4*512+256], v_rb[0:64, 0:256], OP.mult)
                e.tensor_tensor(v_a[64:128, (2*cp)*256:(2*cp+1)*256],
                                psA[64:128, 4*512: 4*512+256], v_rb[64:128, 0:256],
                                OP.mult)
                e.tensor_tensor(v_a[0:64, (2*cp+1)*256:(2*cp+2)*256],
                                psA[0:64, 5*512: 5*512+256], v_rb[0:64, 256:512],
                                OP.mult)
                return e.tensor_tensor(v_a[64:128, (2*cp+1)*256:(2*cp+2)*256],
                                       psA[64:128, 5*512: 5*512+256],
                                       v_rb[64:128, 256:512], OP.mult)
            step("vector", [], f_norm,
                 r=("pb6", "pb4", "pb5"), w=("rtmp", "a"))

        for oc in range(8):
            wbk = 1 if oc % 2 == 0 else 0
            wres = "pb1" if oc % 2 == 0 else "pb0"

            def f_wo(e, oc=oc, wbk=wbk):
                r = None
                for ec in range(8):
                    r = e.matmul(bk(wbk, 256),
                                 wout_chunk(ec)[:, oc*128:(oc+1)*128],
                                 v_a[:, ec*256:(ec+1)*256], start=(ec == 0), stop=(ec == 7),
                                 skip_group_check=True)
                return r
            step("tensor", [s_dve], f_wo, r=("a", "wsp"), w=(wres,))

            def f_x2(e, oc=oc, vx1=vx1, wbk=wbk):
                return e.scalar_tensor_tensor(
                    t_x1g[:, oc*256:(oc+1)*256], bk(wbk, 256),
                    t_bout[:, oc:oc+1],
                    vx1[:, oc*256:(oc+1)*256], OP.add, OP.add)
            step("vector", [s_pe], f_x2, r=(wres, xres, xresh, "bout"), w=("x1g",))

        def f_x2sp(e, i=i):
            return e.dma_start(out=d_x2sp[:, :, i*256:(i+1)*256].rearrange("c p j -> p c j"),
                               in_=t_x1g[:].rearrange("p (c j) -> p c j", c=8))
        step("sync", [s_dve], f_x2sp, dma_sem=s_sp, r=("x1g",), w=("x2sp%d" % i,))

    nc.leave_named_scope("phaseB", _sidB, False)
    _sidC = nc.enter_named_scope("phaseC", False)[0]
    for dst_, srcs_ in (("ht", ("q", "p0", "p1", "a")),
                        ("xh", ("ewb2", "cs2"))):
        for s__ in srcs_:
            if s__ in writers:
                readers.setdefault(dst_, []).append(writers[s__])
            readers.setdefault(dst_, []).extend(readers.get(s__, ()))
    def f_wreload(e):
        r = []
        for x_ in range(NEXP):
            r.append(e.dma_start(
                out=t_w1[x_][:, 0:1024].rearrange("p (c j) -> p c j", c=2),
                in_=d_w1[x_][0:2].rearrange("c p j -> p c j")))
        for x_ in range(2):
            r.append(e.dma_start(
                out=t_w2[x_][:, 0:1024].rearrange("p (c j) -> p c j", c=1),
                in_=d_w2[x_][0:1].rearrange("c p j -> p c j")))
        return r
    step("gpsimd", [], f_wreload, dma_sem=s_dmag, ndma=8, w=("wsp",))

    for j in range(NQG if do_c else 0):
        t_xg = t_xga if j % 2 == 0 else t_xgb
        xgres = ("xga", "xgah") if j % 2 == 0 else ("xgb", "xgbh")
        src_xg = mk_src(t_xg)

        if j == 0:
            def f_cload(e, j=j, t_xg=t_xg):
                return [e.dma_start(out=t_xg[:].rearrange("p (c j) -> p c j", c=8),
                                    in_=d_x2sp[:, :, j*GRP:(j+1)*GRP].rearrange("c p j -> p c j"))]
            step("sync", COMPUTE + [s_sp], f_cload, dma_sem=s_dma,
                 r=("x2sp%d" % j,), w=xgres)
        else:
            def f_cload(e, j=j, t_xg=t_xg):
                return [e.dma_start(
                    out=t_xg[:, 0:4*GRP].rearrange("p (c j) -> p c j", c=4),
                    in_=d_x2sp[0:4, :, j*GRP:(j+1)*GRP].rearrange("c p j -> p c j"))]
            step("sync", COMPUTE + [s_sp], f_cload, dma_sem=s_dma,
                 r=("x2sp%d" % j,), w=(xgres[0],))

            def f_cloadh(e, j=j, t_xg=t_xg):
                return e.dma_start(
                    out=t_xg[:, 4*GRP:8*GRP].rearrange("p (c j) -> p c j", c=4),
                    in_=d_x2sp[4:8, :, j*GRP:(j+1)*GRP].rearrange("c p j -> p c j"))
            step("gpsimd", COMPUTE + [s_sp], f_cloadh, dma_sem=s_dmag, ndma=1,
                 r=("x2sp%d" % j,), w=(xgres[1],))

        ln_block(src_xg, t_xh, True, xgres, "xh", src_t=t_xg)
        router_block(src_xg, xgres)
        ffn_block(mk_evac(t_xg), xgres)

        def f_store(e, j=j):
            return e.dma_start(out=d_out[:, :, j*GRP:(j+1)*GRP].rearrange("c p j -> p c j"),
                               in_=t_x1g[:].rearrange("p (c j) -> p c j", c=8))
        step("sync", [s_dve], f_store, dma_sem=s_sp, r=("x1g",), w=("d_out",))

    nc.leave_named_scope("phaseC", _sidC, False)
    flush_plan()
    _sp_total = sum(s.amt for s in all_steps if s.sem is s_sp)

    def f_final(e):
        e.wait_ge(s_sp, _sp_total)
    blk.sync(f_final)

    es.close()
    return nc



def kernel(**inputs):
    in_maps, metas = host_prepare(inputs)
    nc = build_program()
    res = run_bass_kernel_spmd(nc, in_maps, core_ids=list(range(8)))
    out = np.zeros((4, S, E), dtype=np.float32)
    for core in range(8):
        b, own = metas[core]
        o = res.results[core]["out"].reshape(E, OWN)
        for j, g in enumerate(own):
            out[b, g*128:(g+1)*128, :] = o[:, j*128:(j+1)*128].T
    return out
